# version 05
# baseline (speedup 1.0000x reference)
import os
import numpy as np
from contextlib import ExitStack
import concourse.bass as bass
import concourse.mybir as mybir
from concourse.bass_utils import run_bass_kernel_spmd
from concourse.bass_isa import ReduceOp
import concourse.library_config as _libcfg

F32 = mybir.dt.float32
BF16 = mybir.dt.bfloat16
I32 = mybir.dt.int32
AF = mybir.ActivationFunctionType
ALU = mybir.AluOpType
AX = mybir.AxisListType

S = 2048
D = 1024
NBLK = int(os.environ.get("K_TAP", "8"))
KDUMP = os.environ.get("K_DUMP", "")


def host_constants():
    c = {}
    c["ident"] = np.eye(128, dtype=np.float32)
    c["ones"] = np.ones((128, 128), np.float32)
    c["onescol"] = np.ones((128, 1), np.float32)
    ho = np.zeros((8, 128, 16), np.float32)
    for cc in range(8):
        ho[cc, 0:64, 2 * cc] = 1.0
        ho[cc, 64:128, 2 * cc + 1] = 1.0
    c["hones16"] = ho
    dim_head, window, n = 64, 512, S
    half = np.arange(0, dim_head, 2, dtype=np.float64)
    inv_freq = 1.0 / 10000.0 ** (half / dim_head)
    t = np.arange(n, dtype=np.float64)
    fr = np.concatenate([t[:, None] * inv_freq] * 2, -1)
    base = (half + 0.4 * dim_head) / (1.4 * dim_head)
    power = (t - n // 2) / float(window // 2)
    sc = np.concatenate([base[None, :] ** power[:, None]] * 2, -1)
    sgn = np.concatenate([-np.ones(32), np.ones(32)])
    for nm, tab in [("xpos_qc", (np.cos(fr) * sc).T), ("xpos_qs", (np.sin(fr) * sc).T * sgn[:, None]),
                    ("xpos_kc", (np.cos(fr) / sc).T), ("xpos_ks", (np.sin(fr) / sc).T * sgn[:, None])]:
        c[nm] = np.tile(tab, (2, 1)).astype(np.float32)
    p64 = np.zeros((64, 64), np.float32)
    for dd in range(64):
        p64[(dd + 32) % 64, dd] = 1.0
    z6 = np.zeros((64, 64), np.float32)
    c["perm_half"] = np.block([[p64, z6], [z6, p64]]).astype(np.float32)
    pp = np.zeros((128, 128), np.float32)
    for dd in range(128):
        pp[dd ^ 1, dd] = 1.0
    c["perm_pair"] = pp
    kj = np.arange(1024)[:, None]
    qi = np.arange(512)[None, :] + 512
    c["lmask"] = ((qi >= kj) & (qi <= kj + 512)).astype(np.float32)
    top = np.float32(2595.0) * np.log10(np.float32(1.0 + 4000.0 / 200.0))
    lin = (np.linspace(0.0, top, 32, dtype=np.float32) / np.float32(2595.0)).astype(np.float32)
    mel = np.power(np.float32(10.0), lin) - np.float32(1.0)
    c["l3_fb"] = ((np.float32(200.0) * mel / np.float32(1000.0)).astype(np.float32).reshape(32, 1))
    twopi = 2.0 * np.pi
    cw1 = np.float32(np.ldexp(np.floor(np.ldexp(twopi, 9)), -9))
    cw2 = np.float32(np.ldexp(np.floor(np.ldexp(twopi - np.float64(cw1), 21)), -21))
    cw3 = np.float32(twopi - np.float64(cw1) - np.float64(cw2))
    c["cw"] = np.array([[cw1], [cw2], [cw3], [np.float32(1.0 / twopi)]], np.float32)
    yst = np.zeros((8, 128, 16), np.float32)
    for j in range(8):
        for e in range(64):
            yst[j, 2 * e, 2 * j] = 1.0
            yst[j, 2 * e + 1, 2 * j + 1] = 1.0
    c["yred_st"] = yst
    yr2 = np.zeros((32, 128, 128), np.float32)
    for dh in range(32):
        for dp in range(2):
            yr2[dh, 2 * np.arange(64) + dp, 2 * dh + dp] = 1.0
    c["yred2"] = yr2
    c["l3t"] = np.tile(np.arange(2048, dtype=np.float32)[None, :], (32, 1))
    dp = np.zeros((2, 128), np.float32)
    dp[0, 0::2] = 1.0
    dp[1, 1::2] = 1.0
    c["dppat"] = dp
    return c


def host_layout(inp):
    L = {}
    f32 = lambda a: np.ascontiguousarray(a, dtype=np.float32)
    L["l0_wq"], L["l0_wk"], L["l0_wv"], L["l0_wo"] = (f32(inp["L0_attn_Wq"]), f32(inp["L0_attn_Wk"]),
                                                      f32(inp["L0_attn_Wv"]), f32(inp["L0_attn_Wo"]))
    for nm in "qkvo":
        L["l0_b" + nm] = f32(inp["L0_attn_b" + nm]).reshape(8, 128, 1)
    L["l0_bv_row"] = f32(inp["L0_attn_bv"]).reshape(1, D)
    L["l0_reltab"] = f32(inp["L0_attn_rel_table"].T)
    L["l0f_wg"] = f32(inp["L0_ffn_w_gate"].T)
    L["l0f_wu"] = f32(inp["L0_ffn_w_up"].T)
    L["l0f_wd"] = f32(inp["L0_ffn_w_down"].T)
    L["l1_wq"], L["l1_wk"], L["l1_wv"] = f32(inp["L1_attn_Wq"]), f32(inp["L1_attn_Wk"]), f32(inp["L1_attn_Wv"])
    for nm in "qkv":
        L["l1_b" + nm] = f32(inp["L1_attn_b" + nm]).reshape(8, 128, 1)
    L["l1_bv_row"] = f32(inp["L1_attn_bv"]).reshape(1, D)
    L["l1_wg"] = f32(inp["L1_attn_Wg"])
    L["l1_bg"] = f32(inp["L1_attn_bg"]).reshape(1, 1)
    L["l1f_wqkv"] = f32(inp["L1_ffn_wqkv"])
    L["l1f_wo"] = f32(inp["L1_ffn_wo"])
    HID, PAD = 2730, 2816
    w1, b1, w2 = inp["L1_ffn_w1"], inp["L1_ffn_b1"], inp["L1_ffn_w2"]
    w1p = np.zeros((2, D, 2 * PAD), np.float32)
    w1p[:, :, :HID] = w1[:, :, :HID]
    w1p[:, :, PAD:PAD + HID] = w1[:, :, HID:]
    b1p = np.zeros((2, 2 * PAD), np.float32)
    b1p[:, :HID] = b1[:, :HID]
    b1p[:, PAD:PAD + HID] = b1[:, HID:]
    w2p = np.zeros((2, PAD, D), np.float32)
    w2p[:, :HID, :] = w2
    L["l1f_w1"] = w1p
    L["l1f_b1a"] = f32(b1p[:, :PAD].reshape(2, 22, 128).transpose(0, 2, 1))
    L["l1f_b1g"] = f32(b1p[:, PAD:].reshape(2, 22, 128).transpose(0, 2, 1))
    L["l1f_w2"] = w2p
    L["l1f_b2"] = f32(inp["L1_ffn_b2"].reshape(2, 8, 128).transpose(0, 2, 1))
    L["l1f_qs"] = f32(np.tile(inp["L1_ffn_q_scale"][:, None, :], (1, 2, 1)).reshape(2, 128, 1))
    L["l1f_ks"] = f32(np.tile(inp["L1_ffn_k_scale"][:, None, :], (1, 2, 1)).reshape(2, 128, 1))
    L["l1f_qsr"] = f32(inp["L1_ffn_q_scale"].reshape(2, 1, 64))
    L["l1f_ksr"] = f32(inp["L1_ffn_k_scale"].reshape(2, 1, 64))
    L["l1f_lng"] = f32(inp["L1_ffn_ln_g"].reshape(2, 8, 128).transpose(0, 2, 1))
    L["l1f_lnb"] = f32(inp["L1_ffn_ln_b"].reshape(2, 8, 128).transpose(0, 2, 1))
    L["l2_wqkv"] = f32(inp["L2_attn_W_qkv"].T)
    L["l2_wo"] = f32(inp["L2_attn_W_o"].T)
    L["l2_wvgs"] = f32(inp["L2_attn_W_vgs"].T)
    wl = np.asarray(inp["L2_attn_W_LTM"], np.float32)
    L["l2_wlsm"] = np.ascontiguousarray(
        wl.reshape(16, 32, 2, 64).transpose(0, 3, 2, 1).reshape(16, 128, 32), np.float32)
    L["l2_wlT"] = f32(wl.transpose(0, 2, 1))
    for nin, nm in [("V_gs_base", "vgsb"), ("V_T0", "vt0"), ("beta_tau", "btau"),
                    ("beta_gm", "bgm"), ("C_ch", "cch"), ("gamma", "gam")]:
        L["l2_" + nm] = f32(inp["L2_attn_" + nin]).reshape(16, 1)
    L["l2_alpha"] = f32(inp["L2_attn_alpha_ppd"]).reshape(1, 1)
    L["l2_icth"] = f32(inp["L2_attn_IC_threshold"]).reshape(1, 1)
    L["l2f_wr"], L["l2f_wk"], L["l2f_wv"] = (f32(inp["L2_ffn_W_r"].T), f32(inp["L2_ffn_W_k"].T),
                                             f32(inp["L2_ffn_W_v"].T))
    L["l2f_mur"] = f32(inp["L2_ffn_mu_r"].reshape(8, 128).T)
    L["l2f_muk"] = f32(inp["L2_ffn_mu_k"].reshape(8, 128).T)
    L["l3_lnw"] = f32(inp["L3_attn_ln_w"].reshape(8, 128).T)
    L["l3_lnb"] = f32(inp["L3_attn_ln_b"].reshape(8, 128).T)
    L["l3_wq"], L["l3_wkv"], L["l3_wo"] = f32(inp["L3_attn_w_q"]), f32(inp["L3_attn_w_kv"]), f32(inp["L3_attn_w_o"])
    L["l3_theta"] = f32(inp["L3_attn_theta"]).reshape(1, 1)
    L["l3f_gw"] = f32(inp["L3_ffn_gate_w"])
    L["l3f_wu"] = f32(inp["L3_ffn_Wu"].transpose(1, 0, 2).reshape(D, 2048))
    L["l3f_wv"] = f32(inp["L3_ffn_Wv"].transpose(1, 0, 2).reshape(D, 2048))
    L["l3f_wo"] = f32(inp["L3_ffn_Wo"])
    return L


WSHAPES = {
    "l0_wq": (D, D), "l0_wk": (D, D), "l0_wv": (D, D), "l0_wo": (D, D),
    "l0_bq": (8, 128, 1), "l0_bk": (8, 128, 1), "l0_bv": (8, 128, 1), "l0_bo": (8, 128, 1),
    "l0_bv_row": (1, D), "l0_reltab": (16, 1023),
    "l0f_wg": (D, 4096), "l0f_wu": (D, 4096), "l0f_wd": (4096, D),
    "l1_wq": (D, D), "l1_wk": (D, D), "l1_wv": (D, D),
    "l1_bq": (8, 128, 1), "l1_bk": (8, 128, 1), "l1_bv": (8, 128, 1),
    "l1_bv_row": (1, D), "l1_wg": (D, 1), "l1_bg": (1, 1),
    "l1f_wqkv": (2, D, 1536), "l1f_wo": (2, 512, D),
    "l1f_w1": (2, D, 5632), "l1f_b1a": (2, 128, 22), "l1f_b1g": (2, 128, 22),
    "l1f_w2": (2, 2816, D), "l1f_b2": (2, 128, 8),
    "l1f_qs": (2, 128, 1), "l1f_ks": (2, 128, 1), "l1f_qsr": (2, 1, 64), "l1f_ksr": (2, 1, 64),
    "l1f_lng": (2, 128, 8), "l1f_lnb": (2, 128, 8),
    "l2_wqkv": (D, 3072), "l2_wo": (D, D), "l2_wvgs": (D, 16),
    "l2_wlsm": (16, 128, 32), "l2_wlT": (16, 64, 64),
    "l2_vgsb": (16, 1), "l2_vt0": (16, 1), "l2_btau": (16, 1), "l2_bgm": (16, 1),
    "l2_cch": (16, 1), "l2_gam": (16, 1), "l2_alpha": (1, 1), "l2_icth": (1, 1),
    "l2f_wr": (D, D), "l2f_wk": (D, D), "l2f_wv": (D, D), "l2f_mur": (128, 8), "l2f_muk": (128, 8),
    "l3_lnw": (128, 8), "l3_lnb": (128, 8),
    "l3_wq": (D, D), "l3_wkv": (D, 2 * D), "l3_wo": (D, D), "l3_theta": (1, 1),
    "l3f_gw": (D, 8), "l3f_wu": (D, 2048), "l3f_wv": (D, 2048), "l3f_wo": (8, 256, D),
}


class Prog:
    ENGS = ("sync", "vector", "scalar", "tensor", "gpsimd")

    def __init__(self, nc, ctx):
        self.nc = nc
        self.plan = {e: [] for e in self.ENGS}
        self.s = {}
        for name in ("tv", "vv", "sv", "gv"):
            self.s[name] = [ctx.enter_context(nc.semaphore(name)), 0]
        self.NPOOL = {"sync": 30, "gpsimd": 16}
        self.cur = {"sync": 0, "gpsimd": 0}
        for q, n in self.NPOOL.items():
            for i in range(n):
                nm = f"d{q[0]}{i}"
                self.s[nm] = [ctx.enter_context(nc.semaphore(nm)), 0]

    def val(self, name):
        return self.s[name][1]

    def wait(self, eng, name, v=None):
        if isinstance(name, tuple):
            name, v = name
        if v is None or v <= 0:
            return
        h = self.s[name][0]
        self.plan[eng].append(lambda e: e.wait_ge(h, v))

    def op(self, eng, fn, inc=None, k=1):
        if inc is None:
            self.plan[eng].append(fn)
            return None
        sem = self.s[inc]
        sem[1] += k
        h, v = sem[0], sem[1]
        self.plan[eng].append(lambda e: fn(e).then_inc(h, k))
        return v

    def dma(self, queue, dst, src, cont=False):
        if cont:
            i = (self.cur[queue] - 1) % self.NPOOL[queue]
            nm = f"d{queue[0]}{i}"
        else:
            i = self.cur[queue]
            self.cur[queue] = (i + 1) % self.NPOOL[queue]
            nm = f"d{queue[0]}{i}"
            sem = self.s[nm]
            if sem[1] > 0:
                self.wait(queue, nm, sem[1])
        v = self.op(queue, lambda e, d=dst, s=src: e.dma_start(out=d, in_=s), inc=nm, k=16)
        return (nm, v)

    def dma_barrier(self, eng, queue):
        for i in range(self.NPOOL[queue]):
            nm = f"d{queue[0]}{i}"
            self.wait(eng, nm, self.s[nm][1])

    def drain(self, eng):
        self.plan[eng].append(lambda e: e.drain())

    def uselib(self, name):
        if getattr(self, "_gplib", None) == name:
            return
        self._gplib = name
        lib = getattr(_libcfg, name)
        self.plan["gpsimd"].append(lambda e: e.load_library(lib))

    def join(self):
        snap = {n: self.s[n][1] for n in self.s}
        owner = {"tv": "tensor", "vv": "vector", "sv": "scalar", "gv": "gpsimd"}
        for eng in self.ENGS:
            for n, v in snap.items():
                if owner.get(n) != eng:
                    self.wait(eng, n, v)
        for eng in ("vector", "scalar", "gpsimd"):
            self.drain(eng)

    def scope(self, name):
        for eng in self.ENGS:
            self.plan[eng].append(("scope", name))

    def run(self):
        nc = self.nc
        with self.nc.Block() as blk:
            for eng in self.ENGS:
                def mk(eng_name):
                    def body(e):
                        cur = [None]
                        for fn in self.plan[eng_name]:
                            if isinstance(fn, tuple) and fn[0] == "scope":
                                if cur[0] is not None:
                                    nc.leave_named_scope(cur[0][0], cur[0][1], False)
                                sid, _ = nc.enter_named_scope(fn[1], False)
                                cur[0] = (fn[1], sid)
                            else:
                                fn(e)
                        if cur[0] is not None:
                            nc.leave_named_scope(cur[0][0], cur[0][1], False)
                    return body
                getattr(blk, eng)(mk(eng))


def build(consts):
    nc = bass.Bass("TRN2", target_bir_lowering=False)
    ctx = ExitStack()
    I = {}
    I["x_in"] = nc.dram_tensor("x_in", [S, D], F32, kind="ExternalInput").ap()
    out_x = nc.dram_tensor("out", [S, D], F32, kind="ExternalOutput").ap()
    dbg = nc.dram_tensor("dbg", [1024, S], F32, kind="ExternalOutput").ap()
    def dump_bf(srcfn, nchunks=8):
        p.join()
        for dc in range(nchunks):
            p.op("vector", lambda e, dc=dc: e.tensor_copy(out=AF32[:, 0:2048], in_=srcfn(dc)), inc="vv")
            vv = p.val("vv")
            p.wait("sync", "vv", vv)
            sv = p.dma("sync", dbg[dc * 128:(dc + 1) * 128, :], AF32[:, 0:2048])
            p.wait("vector", sv)
        p.join()
    for nm, sh in WSHAPES.items():
        I[nm] = nc.dram_tensor(nm, list(sh), F32, kind="ExternalInput").ap()
    for nm, arr in consts.items():
        I["c_" + nm] = nc.dram_tensor("c_" + nm, list(arr.shape), F32, kind="ExternalInput").ap()

    SCR = {}
    def scr(name, shape, dt=F32):
        SCR[name] = nc.dram_tensor(name, list(shape), dt, kind="Internal").ap()
        return SCR[name]
    scr("rows8", (8, S))
    scr("band", (16, 4096))
    scr("band2", (16, 128 * 4225 + 16))
    scr("vtok", (16, 128, 1024), BF16)
    scr("penf", (16, 128, S), BF16)
    scr("h0", (32, 128, S))
    scr("wqg", (8, 128, 4096), BF16)
    scr("wqu", (8, 128, 4096), BF16)
    scr("wqd", (32, 128, 1024), BF16)
    scr("qscr", (16, 64, S))
    scr("kscr", (16, 64, S))
    scr("vscr", (16, 64, S))
    scr("ascr", (16, S))
    scr("yscr", (16, 16, 16, 512))
    scr("oT", (8, 128, S), BF16)
    scr("kraw", (8, 128, S))
    scr("vrc", (16, 16, 8192))
    scr("y2", (16, 64, S))
    scr("c16", (16, 3))
    scr("c16b", (16, 1))
    scr("rows16", (16, S))
    scr("l3cs", (32, S))
    scr("l3sn", (2, 32, S))
    scr("w1b", (2, 1024, 5632), BF16)
    scr("w2b", (2, 2816, 1024), BF16)

    def sbuf(name, shape, dt):
        return ctx.enter_context(nc.sbuf_tensor(name, list(shape), dt))

    x_res = sbuf("x_res", (128, 8 * S), F32)
    BFA = sbuf("BFA", (128, 32768), BF16)
    BFB = sbuf("BFB", (128, 16384), BF16)
    A = sbuf("A", (128, 2048), F32)
    W0 = sbuf("W0", (128, 8192), BF16)
    W1 = sbuf("W1", (128, 2048), BF16)
    Db = sbuf("Db", (128, 4096), BF16)
    rows = sbuf("rows", (128, 2048), F32)
    colf = sbuf("colf", (128, 64), F32)
    id32 = sbuf("id32", (128, 128), F32)
    onesb = sbuf("onesb", (128, 128), BF16)
    ones32 = sbuf("ones32", (128, 4), F32)
    honesb = sbuf("honesb", (128, 128), BF16)
    sstat = sbuf("sstat", (128, 256), F32)
    hones32 = sbuf("hones32", (128, 64), F32)
    psum = [ctx.enter_context(nc.psum_tensor(f"ps{i}", [128, 512], F32)) for i in range(4)]
    psZ = ctx.enter_context(nc.psum_tensor("psZ", [128, 2048], F32))

    p = Prog(nc, ctx)
    AF32 = A[:]
    A2 = lambda: rows[:, 0:2048]
    BG = {}
    def bank_wait(key):
        p.wait("tensor", "vv", BG.get(key, 0))
    def bank_set(key, val):
        BG[key] = val
    Dbf = Db[:].bitcast(F32)
    Aint = A[:].bitcast(I32)
    BFAf = BFA[:].bitcast(F32)
    BFBf = BFB[:].bitcast(F32)
    BFBi = BFB[:].bitcast(I32)

    xr = lambda dc: x_res[:, dc * S:(dc + 1) * S]

    p.op("gpsimd", lambda e: e.memset(colf[:, 62:63], 1e-5), inc="gv")
    p.dma("sync", id32[:], I["c_ident"])
    p.dma("gpsimd", onesb[:], I["c_ones"])
    p.dma("sync", ones32[:, 0:1], I["c_onescol"])
    for cc in range(8):
        p.dma("gpsimd", honesb[:, cc * 16:cc * 16 + 16], I["c_hones16"][cc])
    p.join()

    for tc in range(16):
        sl = tc % 4
        v = p.dma("sync", BFAf[:, sl * 1024:sl * 1024 + 1024], I["x_in"][tc * 128:(tc + 1) * 128, :])
        p.wait("tensor", v)
        for dc in range(8):
            bk = dc % 4
            bank_wait(bk)
            tv = p.op("tensor", lambda e, sl=sl, dc=dc, bk=bk: e.transpose(
                psum[bk][:, 0:128], BFAf[:, sl * 1024 + dc * 128: sl * 1024 + dc * 128 + 128],
                id32[:]), inc="tv")
            p.wait("vector", "tv", tv)
            bank_set(bk, p.op("vector", lambda e, tc=tc, dc=dc, bk=bk: e.tensor_copy(
                out=x_res[:, dc * S + tc * 128: dc * S + tc * 128 + 128],
                in_=psum[bk][:, 0:128]), inc="vv"))
        if tc >= 12:
            pass
        p.wait("sync", "tv", p.val("tv"))
    p.join()

    def cast_xb():
        for dc in range(8):
            p.op("vector", lambda e, dc=dc: e.tensor_copy(out=BFA[:, dc * S:(dc + 1) * S],
                                                          in_=xr(dc)), inc="vv")
        p.join()

    xb = lambda dc: BFA[:, dc * S:(dc + 1) * S]
    qTb = lambda oc: BFA[:, 16384 + oc * S: 16384 + oc * S + S]
    kTb = lambda oc: BFB[:, oc * S: oc * S + S]

    def load_w_bf16(dst, name, ncols, nchunks, roff=0):
        v = None
        for ic in range(nchunks):
            v = p.dma("gpsimd", dst[:, ic * ncols:(ic + 1) * ncols],
                      I[name][roff + ic * 128: roff + (ic + 1) * 128, :], cont=(ic > 0))
        return v

    def load_bias_cols(name, col0, n=8):
        t = None
        for dc in range(n):
            t = p.dma("sync", colf[:, col0 + dc:col0 + dc + 1], I[name][dc], cont=(dc > 0))
        p.wait("vector", t)
        p.wait("scalar", t)

    def proj_feat(wbuf, bias_col0, dst, n_oc=8, n_ic=8, wcols=D, movsrc=None, psb=0,
                  out_bf16=True, bias=True):
        movsrc = movsrc or xb
        for oc in range(n_oc):
            for tc in range(4):
                bk = psb + ((oc * 4 + tc) % 2)
                bank_wait(bk)
                for ic in range(n_ic):
                    p.op("tensor", lambda e, oc=oc, tc=tc, ic=ic, bk=bk: e.matmul(
                        psum[bk][:, 0:512],
                        wbuf[:, ic * wcols + oc * 128: ic * wcols + oc * 128 + 128],
                        movsrc(ic)[:, tc * 512: tc * 512 + 512],
                        start=(ic == 0), stop=(ic == n_ic - 1)), inc="tv")
                tvv = p.val("tv")
                p.wait("vector", "tv", tvv)
                if bias:
                    bank_set(bk, p.op("vector", lambda e, oc=oc, tc=tc, bk=bk: e.tensor_scalar(
                        out=dst(oc)[:, tc * 512: tc * 512 + 512], in0=psum[bk][:, 0:512],
                        scalar1=colf[:, bias_col0 + oc: bias_col0 + oc + 1], scalar2=None,
                        op0=ALU.add), inc="vv"))
                else:
                    bank_set(bk, p.op("vector", lambda e, oc=oc, tc=tc, bk=bk: e.tensor_copy(
                        out=dst(oc)[:, tc * 512: tc * 512 + 512], in_=psum[bk][:, 0:512]), inc="vv"))

    def ln_epilogue():
        p.join()
        for tc in range(4):
            for dc in range(8):
                sl = dc % 2
                sq = p.op("scalar", lambda e, dc=dc, tc=tc, sl=sl: e.activation(
                    out=AF32[:, sl * 512: sl * 512 + 512],
                    in_=xr(dc)[:, tc * 512: tc * 512 + 512], func=AF.Square), inc="sv")
                p.wait("tensor", "sv", sq)
                p.op("tensor", lambda e, dc=dc, tc=tc: e.matmul(
                    psum[0][0:1, 0:512], ones32[:, 0:1],
                    xr(dc)[:, tc * 512: tc * 512 + 512],
                    start=(dc == 0), stop=(dc == 7)), inc="tv")
                tvq = p.op("tensor", lambda e, dc=dc, tc=tc, sl=sl: e.matmul(
                    psum[1][0:1, 0:512], ones32[:, 0:1],
                    AF32[:, sl * 512: sl * 512 + 512],
                    start=(dc == 0), stop=(dc == 7)), inc="tv")
                p.wait("scalar", "tv", tvq)
            tvv = p.val("tv")
            p.wait("vector", "tv", tvv)
            p.op("vector", lambda e, tc=tc: e.tensor_scalar(
                out=BFBf[0:1, tc * 512:tc * 512 + 512], in0=psum[0][0:1, 0:512],
                scalar1=1.0 / 1024.0, scalar2=None, op0=ALU.mult), inc="vv")
            p.op("vector", lambda e, tc=tc: e.tensor_scalar(
                out=BFBf[0:1, 2048 + tc * 512: 2048 + tc * 512 + 512], in0=psum[1][0:1, 0:512],
                scalar1=1.0 / 1024.0, scalar2=None, op0=ALU.mult), inc="vv")
            vv = p.val("vv")
            p.wait("tensor", "vv", vv)
        p.drain("vector")
        MEAN = BFBf[0:1, 0:S]; M2 = BFBf[0:1, 2048:2048 + S]
        AV_ = BFBf[0:1, 4096:4096 + S]; BV_ = BFBf[0:1, 6144:6144 + S]
        p.op("vector", lambda e: e.scalar_tensor_tensor(
            out=AV_, in0=MEAN, scalar=-1.0, in1=MEAN, op0=ALU.mult, op1=ALU.mult))
        p.drain("vector")
        p.op("vector", lambda e: e.tensor_tensor(out=AV_, in0=AV_, in1=M2, op=ALU.add), inc="vv")
        p.drain("vector")
        vv = p.val("vv")
        p.wait("scalar", "vv", vv)
        p.op("scalar", lambda e: e.activation(out=AV_, in_=AV_, func=AF.Sqrt,
                                              bias=colf[0:1, 62:63]), inc="sv")
        p.wait("vector", "sv", p.val("sv"))
        p.op("vector", lambda e: e.reciprocal(out=AV_, in_=AV_))
        p.drain("vector")
        p.op("vector", lambda e: e.scalar_tensor_tensor(
            out=BV_, in0=MEAN, scalar=-1.0, in1=AV_, op0=ALU.mult, op1=ALU.mult), inc="vv")
        vv = p.val("vv")
        p.wait("sync", "vv", vv)
        wv = p.dma("sync", SCR["rows8"][0:1, :], AV_)
        wv = p.dma("sync", SCR["rows8"][1:2, :], BV_, cont=True)
        p.wait("sync", wv)
        p.dma("sync", BFBf[:, 0:2048], SCR["rows8"][0:1, :].broadcast_to([128, S]))
        vb = p.dma("sync", BFBf[:, 2048:4096], SCR["rows8"][1:2, :].broadcast_to([128, S]),
                   cont=True)
        p.wait("vector", vb)
        for dc in range(8):
            p.op("vector", lambda e, dc=dc: e.tensor_tensor(
                out=xr(dc), in0=xr(dc), in1=BFBf[:, 0:2048], op=ALU.mult))
            p.drain("vector")
            p.op("vector", lambda e, dc=dc: e.tensor_tensor(
                out=xr(dc), in0=xr(dc), in1=BFBf[:, 2048:4096], op=ALU.add), inc="vv")
            p.drain("vector")
        p.join()

    def band_src(h, k0, q0):
        st = SCR["band2"][h]
        return bass.AP(st.tensor, st.offset + (2047 - (k0 - q0)), [[4224, 128], [1, 512]])

    def headnorms(srcfn, dst_r0):
        for tc in range(4):
            for dc in range(8):
                sl = dc % 2
                p.wait("scalar", "tv", p.val("tv"))
                sq = p.op("scalar", lambda e, dc=dc, tc=tc, sl=sl: e.activation(
                    out=Db[:, 2048 + sl * 512: 2048 + sl * 512 + 512],
                    in_=srcfn(dc)[:, tc * 512: tc * 512 + 512], func=AF.Square), inc="sv")
                p.wait("tensor", "sv", sq)
                p.op("tensor", lambda e, dc=dc, tc=tc, sl=sl: e.matmul(
                    psum[3][0:16, 0:512], honesb[:, dc * 16: dc * 16 + 16],
                    Db[:, 2048 + sl * 512: 2048 + sl * 512 + 512],
                    start=(dc == 0), stop=(dc == 7)), inc="tv")
            tvv = p.val("tv")
            p.wait("scalar", "tv", tvv)
            p.op("scalar", lambda e, tc=tc, dst_r0=dst_r0: e.activation(
                out=rows[dst_r0:dst_r0 + 16, tc * 512: tc * 512 + 512],
                in_=psum[3][0:16, 0:512], func=AF.Sqrt), inc="sv")
            p.wait("tensor", "sv", p.val("sv"))

    def l0_attn():
        cast_xb()
        load_bias_cols("l0_bq", 0)
        load_bias_cols("l0_bk", 8)
        load_bias_cols("l0_bo", 24)
        bv1 = p.dma("sync", rows[0:16, 0:1023], I["l0_reltab"])
        p.wait("vector", bv1)
        p.op("vector", lambda e: e.tensor_scalar(
            out=rows[0:16, 0:1023], in0=rows[0:16, 0:1023],
            scalar1=rows[0:16, 1022:1023], scalar2=None, op0=ALU.subtract), inc="vv")
        p.drain("vector")
        p.wait("gpsimd", "vv", p.val("vv"))
        p.op("gpsimd", lambda e: e.memset(BFBf[0:16, 0:4096], 1.0), inc="gv")
        p.wait("scalar", "gv", p.val("gv"))
        p.wait("scalar", "vv", p.val("vv"))
        p.op("scalar", lambda e: e.activation(
            out=BFBf[0:16, 0:1023], in_=rows[0:16, 0:1023], func=AF.Exp), inc="sv")
        p.wait("sync", "sv", p.val("sv"))
        bw = p.dma("sync", SCR["band"], BFBf[0:16, 0:4096])
        p.wait("sync", bw)
        for h in range(16):
            st = SCR["band2"][h]
            p.dma("sync", bass.AP(st.tensor, st.offset, [[4225, 128], [1, 4096]]),
                  SCR["band"][h:h + 1, 0:4096].broadcast_to([128, 4096]))
        p.join()
        vw = load_w_bf16(W0, "l0_wv", D, 8)
        p.wait("tensor", vw)
        p.join()
        wv = p.dma("sync", SCR["rows8"][2:3, 0:D], I["l0_bv_row"])
        p.wait("sync", wv)
        vb = p.dma("sync", AF32[:, 1024:2048], SCR["rows8"][2:3, 0:D].broadcast_to([128, D]))
        p.wait("vector", vb)
        for tc in range(16):
            for og in range(2):
                bk = og
                bank_wait(bk)
                for ic in range(8):
                    p.op("tensor", lambda e, tc=tc, og=og, ic=ic, bk=bk: e.matmul(
                        psum[bk][:, 0:512],
                        xb(ic)[:, tc * 128: tc * 128 + 128],
                        W0[:, ic * D + og * 512: ic * D + og * 512 + 512],
                        start=(ic == 0), stop=(ic == 7)), inc="tv")
                tvv = p.val("tv")
                p.wait("vector", "tv", tvv)
                bank_set(bk, p.op("vector", lambda e, tc=tc, og=og, bk=bk: e.tensor_tensor(
                    out=BFB[:, tc * 1024 + og * 512: tc * 1024 + og * 512 + 512],
                    in0=psum[bk][:, 0:512], in1=AF32[:, 1024 + og * 512: 1024 + og * 512 + 512],
                    op=ALU.add), inc="vv"))
        vvV = p.val("vv")
        p.wait("sync", "vv", vvV)
        for tc in range(16):
            p.dma("sync", SCR["vtok"][tc], BFB[:, tc * 1024:(tc + 1) * 1024])
        p.join()
        qw = load_w_bf16(W0, "l0_wq", D, 8)
        p.wait("tensor", qw)
        proj_feat(W0, 0, qTb)
        p.join()
        kw = load_w_bf16(W0, "l0_wk", D, 8)
        p.wait("tensor", kw)
        p.join()
        proj_feat(W0, 8, kTb, psb=2)
        p.join()
        if KDUMP == "q":
            dump_bf(qTb)
        if KDUMP == "k":
            dump_bf(kTb)
        headnorms(kTb, 0)
        p.join()
        p.op("vector", lambda e: e.tensor_reduce(out=colf[0:16, 33:34], in_=rows[0:16, 0:S],
                                                 axis=AX.X, op=ALU.max), inc="vv")
        p.join()
        headnorms(qTb, 0)
        p.join()
        p.op("vector", lambda e: e.tensor_scalar(
            out=rows[0:16, 0:S], in0=rows[0:16, 0:S],
            scalar1=colf[0:16, 33:34], scalar2=-1.0, op0=ALU.mult, op1=ALU.mult), inc="vv")
        p.join()

        W0f = W0[:].bitcast(F32)
        p.op("gpsimd", lambda e: e.memset(Db[64:65, 2048:4096], 1.0), inc="gv")
        p.wait("sync", "gv", p.val("gv"))
        probsT = lambda kc: W1[:, kc * 512:(kc + 1) * 512]
        PB = {}
        for h in range(16):
            cc, hh = h // 2, h % 2
            p.wait("sync", "tv", p.val("tv"))
            p.wait("gpsimd", "tv", p.val("tv"))
            c1 = p.dma("sync", Db[0:64, 0:S], qTb(cc)[hh * 64: hh * 64 + 64, :])
            c3 = p.dma("sync", Db[0:64, 2048:2048 + S], kTb(cc)[hh * 64: hh * 64 + 64, :],
                       cont=True)
            c2 = p.dma("gpsimd", Db[64:65, 0:S], rows[h:h + 1, 0:S])
            vl = None
            for kc in range(16):
                vl = p.dma("gpsimd", W0[:, 6144 + kc * 64: 6144 + kc * 64 + 64],
                           SCR["vtok"][kc][:, h * 64:(h + 1) * 64], cont=(kc > 0))
            p.wait("tensor", c3)
            p.wait("tensor", c2)
            p.wait("tensor", vl)
            btoks = {}
            def band_load(qc, kc):
                sl = kc % 4
                p.wait("gpsimd", "vv", PB.get(sl, 0))
                btoks[(qc, kc)] = p.dma("gpsimd", W0[:, sl * 512: sl * 512 + 512],
                                        band_src(h, kc * 128, qc * 512))
            for qc in range(4):
                elig = [kc for kc in range(16) if kc * 128 + 127 - qc * 512 >= 1025]
                for kc in elig[:4]:
                    band_load(qc, kc)
                for half in range(4):
                  for k8 in range(4):
                    kc = half * 4 + k8
                    bk = kc % 2
                    tvv = p.op("tensor", lambda e, qc=qc, kc=kc, bk=bk: e.matmul(
                        psum[bk][:, 0:512],
                        Db[0:65, 2048 + kc * 128: 2048 + kc * 128 + 128],
                        Db[0:65, qc * 512: qc * 512 + 512], start=True, stop=True), inc="tv")
                    p.wait("scalar", "tv", tvv)
                    sv = p.op("scalar", lambda e, k8=k8, bk=bk: e.activation(
                        out=probsT(k8), in_=psum[bk][:, 0:512], func=AF.Exp, scale=0.125),
                        inc="sv")
                    p.wait("tensor", "sv", sv)
                    k0, q0 = kc * 128, qc * 512
                    if k0 + 127 - q0 >= 1025:
                        sl = kc % 4
                        p.wait("vector", btoks.pop((qc, kc)))
                        p.wait("vector", "sv", sv)
                        vvb = p.op("vector", lambda e, k8=k8, sl=sl: e.tensor_tensor(
                            out=probsT(k8), in0=probsT(k8),
                            in1=W0[:, sl * 512: sl * 512 + 512],
                            op=ALU.mult), inc="vv")
                        PB[sl] = vvb
                        ei = elig.index(kc)
                        if ei + 4 < len(elig):
                            band_load(qc, elig[ei + 4])
                  p.wait("tensor", "sv", p.val("sv"))
                  p.wait("tensor", "vv", p.val("vv"))
                  for k8 in range(4):
                    kc = half * 4 + k8
                    p.op("tensor", lambda e, k8=k8, kc=kc: e.matmul(
                        psum[2][0:64, 0:512], W0[:, 6144 + kc * 64: 6144 + kc * 64 + 64],
                        probsT(k8), start=(kc == 0), stop=(kc == 15)), inc="tv")
                    p.op("tensor", lambda e, k8=k8, kc=kc: e.matmul(
                        psum[3][0:1, 0:512], onesb[:, 0:1],
                        probsT(k8), start=(kc == 0), stop=(kc == 15)), inc="tv")
                  p.wait("tensor", "tv", p.val("tv"))
                tvv = p.val("tv")
                p.wait("vector", "tv", tvv)
                p.op("vector", lambda e, hh=hh, qc=qc: e.tensor_copy(
                    out=AF32[hh * 64: hh * 64 + 64, qc * 512: qc * 512 + 512],
                    in_=psum[2][0:64, 0:512]), inc="vv")
                vv = p.op("vector", lambda e, hh=hh, qc=qc: e.reciprocal(
                    out=rows[32 + hh * 32: 33 + hh * 32, qc * 512: qc * 512 + 512],
                    in_=psum[3][0:1, 0:512]), inc="vv")
                p.wait("tensor", "vv", vv)
            if hh == 1:
                p.drain("vector")
                vv = p.val("vv")
                p.wait("sync", "vv", vv)
                p.dma("sync", SCR["rows8"][3:4, :], rows[32:33, 0:S])
                wv2 = p.dma("sync", SCR["rows8"][4:5, :], rows[64:65, 0:S], cont=True)
                p.wait("sync", wv2)
                p.wait("sync", "vv", p.val("vv"))
                v1 = p.dma("sync", W0f[0:64, 1024:3072],
                           SCR["rows8"][3:4, :].broadcast_to([64, S]))
                v2 = p.dma("sync", W0f[64:128, 1024:3072],
                           SCR["rows8"][4:5, :].broadcast_to([64, S]), cont=True)
                p.wait("vector", v2)
                p.op("vector", lambda e, cc=cc: e.tensor_tensor(
                    out=BFA[:, cc * S:(cc + 1) * S], in0=AF32[:, 0:2048],
                    in1=W0f[:, 1024:3072], op=ALU.mult), inc="vv")
                p.drain("vector")
        p.join()
        if KDUMP == "av":
            dump_bf(lambda dc: BFA[:, dc * S:(dc + 1) * S])
        ow = load_w_bf16(W0, "l0_wo", D, 8)
        p.wait("tensor", ow)
        avb = lambda ic: BFA[:, ic * S:(ic + 1) * S]
        for oc in range(8):
            for tc in range(4):
                bk = (oc * 4 + tc) % 2
                for ic in range(8):
                    p.op("tensor", lambda e, oc=oc, tc=tc, ic=ic, bk=bk: e.matmul(
                        psum[bk][:, 0:512],
                        W0[:, ic * D + oc * 128: ic * D + oc * 128 + 128],
                        avb(ic)[:, tc * 512: tc * 512 + 512],
                        start=(ic == 0), stop=(ic == 7)), inc="tv")
                tvv = p.val("tv")
                p.wait("vector", "tv", tvv)
                vv = p.op("vector", lambda e, oc=oc, tc=tc, bk=bk: e.scalar_tensor_tensor(
                    out=xr(oc)[:, tc * 512: tc * 512 + 512], in0=psum[bk][:, 0:512],
                    scalar=colf[:, 24 + oc: 24 + oc + 1],
                    in1=xr(oc)[:, tc * 512: tc * 512 + 512], op0=ALU.add, op1=ALU.add), inc="vv")
                p.wait("tensor", "vv", vv)
        ln_epilogue()

    def par_max_to_rep(accap, dstap, scr_row):
        p.join()
        for tc in range(16):
            tv = p.op("tensor", lambda e, tc=tc: e.transpose(
                psum[3][:, 0:128], accap[:, tc * 128: tc * 128 + 128], id32[:]), inc="tv")
            p.wait("vector", "tv", tv)
            vv = p.op("vector", lambda e, tc=tc: e.tensor_reduce(
                out=rows[:, 1024 + tc: 1024 + tc + 1], in_=psum[3][:, 0:128],
                axis=AX.X, op=ALU.max), inc="vv")
            p.wait("tensor", "vv", vv)
        p.join()
        for tc in range(16):
            p.dma("sync", SCR["rows8"][scr_row:scr_row + 1, tc * 128:(tc + 1) * 128],
                  rows[:, 1024 + tc: 1024 + tc + 1])
        p.dma_barrier("sync", "sync")
        bv = p.dma("sync", dstap, SCR["rows8"][scr_row:scr_row + 1, :].broadcast_to([128, S]))
        p.wait("vector", bv)
        p.join()


    def quantize_weight(name, nchunks, ncols, scr_name, mcol):
        p.join()
        nel = float(nchunks * 128 * ncols)
        for ic in range(nchunks):
            v = p.dma("sync", BFAf[:, 0:ncols] if ic % 2 == 0 else BFAf[:, ncols:2 * ncols],
                      I[name][ic * 128:(ic + 1) * 128, :])
            p.wait("vector", v)
            p.op("vector", lambda e, ic=ic: e.tensor_reduce(
                out=colf[:, 40:41], in_=BFAf[:, 0:ncols] if ic % 2 == 0 else BFAf[:, ncols:2 * ncols],
                axis=AX.X, op=ALU.add, apply_absolute_value=True), inc="vv")
            p.drain("vector")
            if ic == 0:
                p.op("vector", lambda e: e.tensor_copy(out=colf[:, 41:42], in_=colf[:, 40:41]), inc="vv")
            else:
                p.op("vector", lambda e: e.tensor_tensor(out=colf[:, 41:42], in0=colf[:, 41:42],
                                                         in1=colf[:, 40:41], op=ALU.add), inc="vv")
            p.drain("vector")
            p.wait("sync", "vv", p.val("vv"))
        p.join()
        tv = p.op("tensor", lambda e: e.matmul(psum[3][0:1, 0:1], colf[:, 41:42],
                                               ones32[:, 0:1], start=True, stop=True), inc="tv")
        p.wait("vector", "tv", tv)
        vv = p.op("vector", lambda e: e.tensor_copy(out=rows[0:1, 0:1], in_=psum[3][0:1, 0:1]),
                  inc="vv")
        p.wait("sync", "vv", vv)
        wv = p.dma("sync", SCR["rows8"][7:8, 0:1], rows[0:1, 0:1])
        p.wait("sync", wv)
        bv = p.dma("sync", colf[:, 42:43], SCR["rows8"][7:8, 0:1].broadcast_to([128, 1]))
        p.wait("vector", bv)
        p.op("vector", lambda e: e.tensor_scalar(out=colf[:, mcol:mcol + 1], in0=colf[:, 42:43],
                                                 scalar1=1.0 / nel, scalar2=1e-5,
                                                 op0=ALU.mult, op1=ALU.max), inc="vv")
        p.drain("vector")
        p.op("vector", lambda e: e.reciprocal(out=colf[:, 43:44], in_=colf[:, mcol:mcol + 1]), inc="vv")
        p.drain("vector")
        Bi32 = BFB[:].bitcast(I32)
        for ic in range(nchunks):
            p.wait("sync", "vv", p.val("vv"))
            v = p.dma("sync", BFAf[:, 0:ncols], I[name][ic * 128:(ic + 1) * 128, :])
            p.wait("vector", v)
            p.op("vector", lambda e: e.tensor_scalar(out=BFBf[:, 0:ncols], in0=BFAf[:, 0:ncols],
                                                     scalar1=colf[:, 43:44], scalar2=None,
                                                     op0=ALU.mult), inc="vv")
            p.drain("vector")
            p.op("vector", lambda e: e.tensor_copy(out=Bi32[:, 4096:4096 + ncols],
                                                   in_=BFBf[:, 0:ncols]), inc="vv")
            p.drain("vector")
            p.op("vector", lambda e: e.tensor_copy(out=BFBf[:, 0:ncols],
                                                   in_=Bi32[:, 4096:4096 + ncols]), inc="vv")
            p.drain("vector")
            p.op("vector", lambda e: e.tensor_scalar(out=Db[:, 0:ncols], in0=BFBf[:, 0:ncols],
                                                     scalar1=-1.0, scalar2=1.0,
                                                     op0=ALU.max, op1=ALU.min), inc="vv")
            p.drain("vector")
            vv = p.val("vv")
            p.wait("sync", "vv", vv)
            sv = p.dma("sync", SCR[scr_name][ic], Db[:, 0:ncols])
            p.wait("vector", sv)
        p.join()

    def l0_ffn():
        quantize_weight("l0f_wg", 8, 4096, "wqg", 48)
        quantize_weight("l0f_wu", 8, 4096, "wqu", 49)
        quantize_weight("l0f_wd", 32, 1024, "wqd", 50)
        p.op("vector", lambda e: e.scalar_tensor_tensor(
            out=AF32[:, 0:2048], in0=xr(0), scalar=-1.0, in1=xr(0),
            op0=ALU.mult, op1=ALU.max), inc="vv")
        p.drain("vector")
        for dc in range(1, 8):
            p.op("vector", lambda e, dc=dc: e.scalar_tensor_tensor(
                out=BFBf[:, 6144:8192], in0=xr(dc), scalar=-1.0, in1=xr(dc),
                op0=ALU.mult, op1=ALU.max), inc="vv")
            p.drain("vector")
            p.op("vector", lambda e: e.tensor_tensor(out=AF32[:, 0:2048], in0=AF32[:, 0:2048],
                                                     in1=BFBf[:, 6144:8192], op=ALU.max), inc="vv")
            p.drain("vector")
        par_max_to_rep(AF32[:, 0:2048], AF32[:, 0:2048], 6)
        p.op("vector", lambda e: e.tensor_scalar(out=AF32[:, 0:2048], in0=AF32[:, 0:2048],
                                                 scalar1=1e-5, scalar2=None, op0=ALU.max), inc="vv")
        p.drain("vector")
        p.op("vector", lambda e: e.reciprocal(out=BFBf[:, 0:2048], in_=AF32[:, 0:2048]), inc="vv")
        p.drain("vector")
        Bi32 = BFB[:].bitcast(I32)
        for dc in range(8):
            p.op("vector", lambda e, dc=dc: e.scalar_tensor_tensor(
                out=BFBf[:, 2048:4096], in0=xr(dc), scalar=127.0, in1=BFBf[:, 0:2048],
                op0=ALU.mult, op1=ALU.mult), inc="vv")
            p.drain("vector")
            p.op("vector", lambda e: e.tensor_copy(out=Bi32[:, 4096:6144], in_=BFBf[:, 2048:4096]), inc="vv")
            p.drain("vector")
            p.op("vector", lambda e: e.tensor_copy(out=BFBf[:, 2048:4096], in_=Bi32[:, 4096:6144]), inc="vv")
            p.drain("vector")
            p.op("vector", lambda e, dc=dc: e.tensor_scalar(
                out=BFA[:, dc * S:(dc + 1) * S], in0=BFBf[:, 2048:4096],
                scalar1=-128.0, scalar2=127.0, op0=ALU.max, op1=ALU.min), inc="vv")
            p.drain("vector")
        if KDUMP == "xhat":
            dump_bf(lambda dc: BFA[:, dc * S:(dc + 1) * S])
        p.op("vector", lambda e: e.tensor_scalar(out=BFBf[:, 0:2048], in0=AF32[:, 0:2048],
                                                 scalar1=colf[:, 48:49], scalar2=1.0 / 127.0,
                                                 op0=ALU.mult, op1=ALU.mult), inc="vv")
        p.op("vector", lambda e: e.tensor_scalar(out=BFBf[:, 2048:4096], in0=AF32[:, 0:2048],
                                                 scalar1=colf[:, 49:50], scalar2=1.0 / 127.0,
                                                 op0=ALU.mult, op1=ALU.mult), inc="vv")
        p.wait("gpsimd", "vv", p.val("vv"))
        p.op("gpsimd", lambda e: e.memset(BFBf[:, 4096:6144], 0.0), inc="gv")
        p.join()
        xhat = lambda ic: BFA[:, ic * S:(ic + 1) * S]
        for oc in range(32):
            p.wait("sync", "tv", p.val("tv"))
            for ic in range(8):
                p.dma("sync", Db[:, ic * 128: ic * 128 + 128],
                      SCR["wqg"][ic][:, oc * 128:(oc + 1) * 128], cont=(ic > 0))
                wv = p.dma("sync", Db[:, 1024 + ic * 128: 1024 + ic * 128 + 128],
                           SCR["wqu"][ic][:, oc * 128:(oc + 1) * 128], cont=True)
            p.wait("tensor", wv)
            for tc in range(4):
                bkg, bku = tc % 2, 2 + tc % 2
                bank_wait(bkg)
                bank_wait(bku)
                for ic in range(8):
                    p.op("tensor", lambda e, ic=ic, tc=tc, bkg=bkg: e.matmul(
                        psum[bkg][:, 0:512], Db[:, ic * 128: ic * 128 + 128],
                        xhat(ic)[:, tc * 512: tc * 512 + 512],
                        start=(ic == 0), stop=(ic == 7)), inc="tv")
                    p.op("tensor", lambda e, ic=ic, tc=tc, bku=bku: e.matmul(
                        psum[bku][:, 0:512], Db[:, 1024 + ic * 128: 1024 + ic * 128 + 128],
                        xhat(ic)[:, tc * 512: tc * 512 + 512],
                        start=(ic == 0), stop=(ic == 7)), inc="tv")
                tvv = p.val("tv")
                p.wait("vector", "tv", tvv)
                p.wait("vector", "sv", p.val("sv"))
                p.op("vector", lambda e, tc=tc, bkg=bkg: e.tensor_tensor(
                    out=AF32[:, 0:512], in0=psum[bkg][:, 0:512],
                    in1=BFBf[:, tc * 512: tc * 512 + 512], op=ALU.mult), inc="vv")
                vvg = p.val("vv")
                p.wait("scalar", "vv", vvg)
                p.op("scalar", lambda e: e.activation(out=AF32[:, 512:1024], in_=AF32[:, 0:512],
                                                      func=AF.Sigmoid), inc="sv")
                p.wait("vector", "sv", p.val("sv"))
                p.op("vector", lambda e: e.tensor_tensor(
                    out=AF32[:, 512:1024], in0=AF32[:, 512:1024], in1=AF32[:, 0:512],
                    op=ALU.mult), inc="vv")
                p.drain("vector")
                p.op("vector", lambda e, tc=tc, bku=bku: e.tensor_tensor(
                    out=AF32[:, 1024:1536], in0=psum[bku][:, 0:512],
                    in1=BFBf[:, 2048 + tc * 512: 2048 + tc * 512 + 512], op=ALU.mult), inc="vv")
                p.wait("vector", "sv", p.val("sv"))
                p.drain("vector")
                bank_set(bkg, p.val("vv"))
                bank_set(bku, p.val("vv"))
                hvv = p.op("vector", lambda e: e.tensor_tensor(out=AF32[:, 1536:2048],
                                                               in0=AF32[:, 512:1024],
                                                               in1=AF32[:, 1024:1536],
                                                               op=ALU.mult), inc="vv")
                p.drain("vector")
                p.op("vector", lambda e: e.scalar_tensor_tensor(
                    out=AF32[:, 0:512], in0=AF32[:, 1536:2048], scalar=-1.0,
                    in1=AF32[:, 1536:2048], op0=ALU.mult, op1=ALU.max), inc="vv")
                p.drain("vector")
                p.op("vector", lambda e, tc=tc: e.tensor_tensor(
                    out=BFBf[:, 4096 + tc * 512: 4096 + tc * 512 + 512],
                    in0=BFBf[:, 4096 + tc * 512: 4096 + tc * 512 + 512],
                    in1=AF32[:, 0:512], op=ALU.max), inc="vv")
                vv = p.val("vv")
                p.wait("sync", "vv", vv)
                hv = p.dma("sync", SCR["h0"][oc][:, tc * 512:(tc + 1) * 512], AF32[:, 1536:2048])
                p.wait("vector", hv)
        p.join()
        if KDUMP == "hmax":
            dump_bf(lambda dc: BFBf[:, 4096:6144] if dc == 0 else BFBf[:, 0:2048], 2)
        par_max_to_rep(BFBf[:, 4096:6144], BFBf[:, 4096:6144], 6)
        p.op("vector", lambda e: e.tensor_scalar(out=BFBf[:, 4096:6144], in0=BFBf[:, 4096:6144],
                                                 scalar1=1e-5, scalar2=None, op0=ALU.max), inc="vv")
        p.drain("vector")
        p.op("vector", lambda e: e.tensor_scalar(out=BFBf[:, 6144:8192], in0=BFBf[:, 4096:6144],
                                                 scalar1=colf[:, 50:51], scalar2=1.0 / 127.0,
                                                 op0=ALU.mult, op1=ALU.mult), inc="vv")
        p.drain("vector")
        p.op("vector", lambda e: e.reciprocal(out=BFBf[:, 4096:6144], in_=BFBf[:, 4096:6144]), inc="vv")
        p.join()
        if KDUMP == "dq2":
            dump_bf(lambda dc: BFBf[:, 6144:8192] if dc == 0 else BFBf[:, 4096:6144], 2)
        Bi32 = BFB[:].bitcast(I32)
        xh2 = lambda ic: BFA[:, 16384 + ic * 512: 16384 + ic * 512 + 512]
        for tc in range(4):
            for ic in range(32):
                sl = ic % 2
                p.wait("sync", "vv", p.val("vv"))
                hv = p.dma("sync", BFAf[:, 4096 + sl * 512: 4096 + sl * 512 + 512],
                           SCR["h0"][ic][:, tc * 512:(tc + 1) * 512])
                p.wait("vector", hv)
                p.op("vector", lambda e, sl=sl, tc=tc: e.scalar_tensor_tensor(
                    out=BFAf[:, 5120:5632], in0=BFAf[:, 4096 + sl * 512: 4096 + sl * 512 + 512],
                    scalar=127.0, in1=BFBf[:, 4096 + tc * 512: 4096 + tc * 512 + 512],
                    op0=ALU.mult, op1=ALU.mult), inc="vv")
                p.drain("vector")
                p.op("vector", lambda e: e.tensor_copy(out=Bi32[:, 0:512],
                                                       in_=BFAf[:, 5120:5632]), inc="vv")
                p.drain("vector")
                p.op("vector", lambda e: e.tensor_copy(out=BFAf[:, 5120:5632],
                                                       in_=Bi32[:, 0:512]), inc="vv")
                p.drain("vector")
                p.op("vector", lambda e, ic=ic: e.tensor_scalar(
                    out=xh2(ic), in0=BFAf[:, 5120:5632],
                    scalar1=-128.0, scalar2=127.0, op0=ALU.max, op1=ALU.min), inc="vv")
                p.drain("vector")
            vvq = p.val("vv")
            p.wait("tensor", "vv", vvq)
            p.wait("sync", "vv", vvq)
            for og in range(8):
                bk = og % 2
                for ic in range(32):
                    sl2 = ic % 2
                    p.wait("sync", "tv", p.val("tv"))
                    wv = p.dma("sync", Db[:, sl2 * 128: sl2 * 128 + 128],
                               SCR["wqd"][ic][:, og * 128:(og + 1) * 128])
                    p.wait("tensor", wv)
                    p.op("tensor", lambda e, og=og, ic=ic, bk=bk, sl2=sl2: e.matmul(
                        psum[bk][:, 0:512], Db[:, sl2 * 128: sl2 * 128 + 128],
                        xh2(ic), start=(ic == 0), stop=(ic == 31)), inc="tv")
                tvv = p.val("tv")
                p.wait("vector", "tv", tvv)
                p.op("vector", lambda e, og=og, tc=tc, bk=bk: e.tensor_tensor(
                    out=AF32[:, 0:512], in0=psum[bk][:, 0:512],
                    in1=BFBf[:, 6144 + tc * 512: 6144 + tc * 512 + 512], op=ALU.mult), inc="vv")
                p.drain("vector")
                if KDUMP == "dout" and tc == 0:
                    vvd = p.val("vv")
                    p.wait("sync", "vv", vvd)
                    dv = p.dma("sync", dbg[og * 128:(og + 1) * 128, 0:512], AF32[:, 0:512])
                    p.wait("vector", dv)
                vre = p.op("vector", lambda e, og=og, tc=tc: e.tensor_tensor(
                    out=xr(og)[:, tc * 512: tc * 512 + 512], in0=AF32[:, 0:512],
                    in1=xr(og)[:, tc * 512: tc * 512 + 512], op=ALU.add), inc="vv")
                p.drain("vector")
                p.wait("tensor", "vv", vre)
        if KDUMP == "y0":
            dump_bf(xr)
        if KDUMP == "wg":
            p.join()
            for ic in range(4):
                v = p.dma("sync", Db[:, 0:4096], SCR["wqg"][ic])
                p.wait("vector", v)
                p.op("vector", lambda e: e.tensor_copy(out=AF32[:, 0:2048], in_=Db[:, 0:2048]), inc="vv")
                vv = p.val("vv")
                p.wait("sync", "vv", vv)
                sv = p.dma("sync", dbg[ic * 128:(ic + 1) * 128, :], AF32[:, 0:2048])
                p.wait("vector", sv)
                p.wait("sync", sv)
        ln_epilogue()

    def l1_attn():
        cast_xb()
        load_bias_cols("l1_bq", 0)
        load_bias_cols("l1_bk", 8)
        gv1 = None
        for ic in range(8):
            gv1 = p.dma("gpsimd", Db[:, ic:ic + 1], I["l1_wg"][ic * 128:(ic + 1) * 128, :],
                        cont=(ic > 0))
        bgv = p.dma("sync", colf[0:1, 35:36], I["l1_bg"])
        p.op("gpsimd", lambda e: e.memset(colf[0:1, 36:37], float(np.log(2046.0))), inc="gv")
        p.dma_barrier("tensor", "gpsimd")
        p.wait("tensor", "vv", p.val("vv"))
        for tc in range(4):
            for ic in range(8):
                p.op("tensor", lambda e, tc=tc, ic=ic: e.matmul(
                    psum[0][0:1, 0:512], Db[:, ic:ic + 1],
                    xb(ic)[:, tc * 512: tc * 512 + 512],
                    start=(ic == 0), stop=(ic == 7)), inc="tv")
            tvv = p.val("tv")
            p.wait("scalar", "tv", tvv)
            p.wait("scalar", bgv)
            p.wait("scalar", "gv", p.val("gv"))
            sv = p.op("scalar", lambda e, tc=tc: e.activation(
                out=BFBf[0:1, tc * 512: tc * 512 + 512], in_=psum[0][0:1, 0:512],
                func=AF.Sigmoid, bias=colf[0:1, 35:36]), inc="sv")
            p.wait("tensor", "sv", sv)
        p.drain("scalar")
        p.op("scalar", lambda e: e.activation(
            out=BFBf[0:1, 2048:2048 + S], in_=BFBf[0:1, 0:S], func=AF.Exp,
            scale=-float(np.log(2046.0)), bias=colf[0:1, 36:37]), inc="sv")
        sv = p.val("sv")
        p.wait("sync", "sv", sv)
        wv = p.dma("sync", SCR["rows8"][0:1, :], BFBf[0:1, 0:S])
        wv = p.dma("sync", SCR["rows8"][1:2, :], BFBf[0:1, 2048:2048 + S], cont=True)
        p.wait("sync", wv)
        if KDUMP == "g":
            p.dma("sync", dbg[0:2, :], SCR["rows8"][0:2, :])
        p.dma("sync", BFBf[:, 2048:4096], SCR["rows8"][0:1, :].broadcast_to([128, S]))
        bv = p.dma("sync", BFBf[:, 4096:6144], SCR["rows8"][1:2, :].broadcast_to([128, S]),
                   cont=True)
        p.wait("gpsimd", wv)
        p.op("gpsimd", lambda e: e.iota(out=BFBf[:, 0:2048], pattern=[[1, 2048]], base=0,
                                        channel_multiplier=0,
                                        allow_small_or_imprecise_dtypes=True), inc="gv")
        p.op("gpsimd", lambda e: e.iota(out=colf[:, 37:38], pattern=[[1, 1]], base=0,
                                        channel_multiplier=1,
                                        allow_small_or_imprecise_dtypes=True), inc="gv")
        gvv = p.val("gv")
        p.wait("vector", "gv", gvv)
        p.wait("vector", bv)
        for kc in range(16):
            c1 = p.dma("sync", colf[:, 38:39],
                       SCR["rows8"][0:1, kc * 128:(kc + 1) * 128].transpose([1, 0]))
            p.wait("vector", c1)
            p.op("vector", lambda e, kc=kc: e.tensor_scalar(
                out=colf[:, 39:40], in0=colf[:, 37:38], scalar1=float(kc * 128),
                scalar2=None, op0=ALU.add), inc="vv")
            p.drain("vector")
            p.op("vector", lambda e: e.tensor_scalar(
                out=BFBf[:, 6144:8192], in0=BFBf[:, 0:2048], scalar1=-1.0,
                scalar2=colf[:, 39:40], op0=ALU.mult, op1=ALU.add), inc="vv")
            p.drain("vector")
            p.op("vector", lambda e: e.scalar_tensor_tensor(
                out=BFBf[:, 6144:8192], in0=BFBf[:, 6144:8192], scalar=-1.0,
                in1=BFBf[:, 6144:8192], op0=ALU.mult, op1=ALU.max), inc="vv")
            p.drain("vector")
            p.op("vector", lambda e: e.tensor_scalar(
                out=AF32[:, 0:2048], in0=BFBf[:, 6144:8192], scalar1=-1.0, scalar2=2.0,
                op0=ALU.mult, op1=ALU.add), inc="vv")
            p.drain("vector")
            p.op("vector", lambda e: e.tensor_tensor(
                out=AF32[:, 0:2048], in0=AF32[:, 0:2048], in1=BFBf[:, 4096:6144],
                op=ALU.add), inc="vv")
            p.drain("vector")
            p.op("vector", lambda e: e.tensor_scalar(
                out=AF32[:, 0:2048], in0=AF32[:, 0:2048], scalar1=1e-32, scalar2=1.0,
                op0=ALU.max, op1=ALU.min), inc="vv")
            p.drain("vector")
            p.op("vector", lambda e: e.tensor_scalar(
                out=colf[:, 40:41], in0=colf[:, 38:39], scalar1=-1.0, scalar2=1.0,
                op0=ALU.mult, op1=ALU.add), inc="vv")
            p.op("vector", lambda e: e.tensor_scalar(
                out=colf[:, 41:42], in0=colf[:, 38:39], scalar1=1.0, scalar2=None,
                op0=ALU.add), inc="vv")
            p.drain("vector")
            p.op("vector", lambda e: e.tensor_scalar(
                out=BFBf[:, 6144:8192], in0=BFBf[:, 2048:4096], scalar1=-1.0,
                scalar2=colf[:, 40:41], op0=ALU.mult, op1=ALU.add), inc="vv")
            p.drain("vector")
            p.op("vector", lambda e: e.tensor_scalar(
                out=BFBf[:, 6144:8192], in0=BFBf[:, 6144:8192], scalar1=0.0, scalar2=None,
                op0=ALU.max), inc="vv")
            p.drain("vector")
            p.op("vector", lambda e: e.tensor_scalar(
                out=A2(), in0=BFBf[:, 2048:4096], scalar1=-1.0, scalar2=1.0,
                op0=ALU.mult, op1=ALU.add), inc="vv")
            p.drain("vector")
            p.op("vector", lambda e: e.tensor_tensor(
                out=BFBf[:, 6144:8192], in0=BFBf[:, 6144:8192], in1=A2(), op=ALU.mult), inc="vv")
            p.drain("vector")
            p.op("vector", lambda e: e.tensor_scalar(
                out=A2(), in0=BFBf[:, 2048:4096], scalar1=-1.0,
                scalar2=colf[:, 41:42], op0=ALU.mult, op1=ALU.add), inc="vv")
            p.drain("vector")
            p.op("vector", lambda e: e.tensor_scalar(
                out=A2(), in0=A2(), scalar1=1.0, scalar2=None, op0=ALU.min), inc="vv")
            p.drain("vector")
            p.op("vector", lambda e: e.tensor_tensor(
                out=A2(), in0=A2(), in1=BFBf[:, 2048:4096], op=ALU.mult), inc="vv")
            p.drain("vector")
            p.op("vector", lambda e: e.tensor_tensor(
                out=BFBf[:, 6144:8192], in0=BFBf[:, 6144:8192], in1=A2(), op=ALU.add), inc="vv")
            p.drain("vector")
            p.op("vector", lambda e: e.tensor_tensor(
                out=BFBf[:, 6144:8192], in0=BFBf[:, 6144:8192], in1=AF32[:, 0:2048],
                op=ALU.mult), inc="vv")
            p.drain("vector")
            p.op("vector", lambda e: e.tensor_scalar(
                out=Db[:, 0:2048], in0=BFBf[:, 6144:8192], scalar1=1e-32, scalar2=None,
                op0=ALU.max), inc="vv")
            vvp = p.val("vv")
            p.wait("sync", "vv", vvp)
            sv2 = p.dma("sync", SCR["penf"][kc], Db[:, 0:2048])
            p.wait("vector", sv2)
        p.join()
        if KDUMP == "pen":
            for kc in range(8):
                v = p.dma("sync", Db[:, 0:2048], SCR["penf"][kc])
                p.wait("vector", v)
                p.op("vector", lambda e: e.tensor_copy(out=AF32[:, 0:2048], in_=Db[:, 0:2048]), inc="vv")
                vv = p.val("vv")
                p.wait("sync", "vv", vv)
                sv = p.dma("sync", dbg[kc * 128:(kc + 1) * 128, :], AF32[:, 0:2048])
                p.wait("vector", sv)
                p.wait("sync", sv)
        vw = load_w_bf16(W0, "l1_wv", D, 8)
        p.wait("tensor", vw)
        p.join()
        wv = p.dma("sync", SCR["rows8"][2:3, 0:D], I["l1_bv_row"])
        p.wait("sync", wv)
        vb = p.dma("sync", AF32[:, 1024:2048], SCR["rows8"][2:3, 0:D].broadcast_to([128, D]))
        p.wait("vector", vb)
        for tc in range(16):
            for og in range(2):
                bk = og
                bank_wait(bk)
                for ic in range(8):
                    p.op("tensor", lambda e, tc=tc, og=og, ic=ic, bk=bk: e.matmul(
                        psum[bk][:, 0:512],
                        xb(ic)[:, tc * 128: tc * 128 + 128],
                        W0[:, ic * D + og * 512: ic * D + og * 512 + 512],
                        start=(ic == 0), stop=(ic == 7)), inc="tv")
                tvv = p.val("tv")
                p.wait("vector", "tv", tvv)
                bank_set(bk, p.op("vector", lambda e, tc=tc, og=og, bk=bk: e.tensor_tensor(
                    out=BFB[:, tc * 1024 + og * 512: tc * 1024 + og * 512 + 512],
                    in0=psum[bk][:, 0:512], in1=AF32[:, 1024 + og * 512: 1024 + og * 512 + 512],
                    op=ALU.add), inc="vv"))
        vvV = p.val("vv")
        p.wait("sync", "vv", vvV)
        for tc in range(16):
            p.dma("sync", SCR["vtok"][tc], BFB[:, tc * 1024:(tc + 1) * 1024])
        p.join()
        qw = load_w_bf16(W0, "l1_wq", D, 8)
        p.wait("tensor", qw)
        proj_feat(W0, 0, qTb)
        p.join()
        kw = load_w_bf16(W0, "l1_wk", D, 8)
        p.wait("tensor", kw)
        proj_feat(W0, 8, kTb, psb=2)
        p.join()
        headnorms(kTb, 0)
        p.join()
        p.op("vector", lambda e: e.tensor_reduce(out=colf[0:16, 33:34], in_=rows[0:16, 0:S],
                                                 axis=AX.X, op=ALU.max), inc="vv")
        p.join()
        headnorms(qTb, 0)
        p.join()
        p.op("vector", lambda e: e.tensor_scalar(
            out=rows[0:16, 0:S], in0=rows[0:16, 0:S],
            scalar1=colf[0:16, 33:34], scalar2=-1.0, op0=ALU.mult, op1=ALU.mult), inc="vv")
        p.join()
        W0f = W0[:].bitcast(F32)
        p.op("gpsimd", lambda e: e.memset(Db[64:65, 2048:4096], 1.0), inc="gv")
        p.wait("sync", "gv", p.val("gv"))
        probsT = lambda kc: W1[:, kc * 512:(kc + 1) * 512]
        PG = {}
        for h in range(16):
            cc, hh = h // 2, h % 2
            p.wait("sync", "tv", p.val("tv"))
            p.wait("gpsimd", "tv", p.val("tv"))
            c1 = p.dma("sync", Db[0:64, 0:S], qTb(cc)[hh * 64: hh * 64 + 64, :])
            c3 = p.dma("sync", Db[0:64, 2048:2048 + S], kTb(cc)[hh * 64: hh * 64 + 64, :],
                       cont=True)
            c2 = p.dma("gpsimd", Db[64:65, 0:S], rows[h:h + 1, 0:S])
            vl = None
            for kc in range(16):
                vl = p.dma("gpsimd", W0[:, 6144 + kc * 64: 6144 + kc * 64 + 64],
                           SCR["vtok"][kc][:, h * 64:(h + 1) * 64], cont=(kc > 0))
            p.wait("tensor", c3)
            p.wait("tensor", c2)
            p.wait("tensor", vl)
            ptok = {}
            def pen_load(qc, kc):
                sl = kc % 4
                q = "sync" if kc % 2 else "gpsimd"
                p.wait(q, "vv", PG.get(sl, 0))
                ptok[kc] = p.dma(q, W0[:, sl * 512: sl * 512 + 512],
                                 SCR["penf"][kc][:, qc * 512:(qc + 1) * 512])
            for qc in range(4):
                for kc in range(4):
                    pen_load(qc, kc)
                for half in range(4):
                  for k8 in range(4):
                    kc = half * 4 + k8
                    bk = kc % 2
                    sl = kc % 4
                    tvv = p.op("tensor", lambda e, qc=qc, kc=kc, bk=bk: e.matmul(
                        psum[bk][:, 0:512],
                        Db[0:65, 2048 + kc * 128: 2048 + kc * 128 + 128],
                        Db[0:65, qc * 512: qc * 512 + 512], start=True, stop=True), inc="tv")
                    p.wait("scalar", "tv", tvv)
                    sv = p.op("scalar", lambda e, k8=k8, bk=bk: e.activation(
                        out=probsT(k8), in_=psum[bk][:, 0:512], func=AF.Exp, scale=0.125),
                        inc="sv")
                    p.wait("tensor", "sv", sv)
                    p.wait("vector", ptok.pop(kc))
                    p.wait("vector", "sv", sv)
                    vvp = p.op("vector", lambda e, k8=k8, sl=sl: e.tensor_tensor(
                        out=probsT(k8), in0=probsT(k8),
                        in1=W0[:, sl * 512: sl * 512 + 512], op=ALU.mult), inc="vv")
                    PG[sl] = vvp
                    if kc + 4 < 16:
                        pen_load(qc, kc + 4)
                  p.wait("tensor", "sv", p.val("sv"))
                  p.wait("tensor", "vv", p.val("vv"))
                  for k8 in range(4):
                    kc = half * 4 + k8
                    p.op("tensor", lambda e, k8=k8, kc=kc: e.matmul(
                        psum[2][0:64, 0:512], W0[:, 6144 + kc * 64: 6144 + kc * 64 + 64],
                        probsT(k8), start=(kc == 0), stop=(kc == 15)), inc="tv")
                    p.op("tensor", lambda e, k8=k8, kc=kc: e.matmul(
                        psum[3][0:1, 0:512], onesb[:, 0:1],
                        probsT(k8), start=(kc == 0), stop=(kc == 15)), inc="tv")
                  p.wait("tensor", "tv", p.val("tv"))
                tvv = p.val("tv")
                p.wait("vector", "tv", tvv)
                p.op("vector", lambda e, hh=hh, qc=qc: e.tensor_copy(
                    out=AF32[hh * 64: hh * 64 + 64, qc * 512: qc * 512 + 512],
                    in_=psum[2][0:64, 0:512]), inc="vv")
                vv = p.op("vector", lambda e, hh=hh, qc=qc: e.reciprocal(
                    out=rows[32 + hh * 32: 33 + hh * 32, qc * 512: qc * 512 + 512],
                    in_=psum[3][0:1, 0:512]), inc="vv")
                p.wait("tensor", "vv", vv)
            if hh == 1:
                p.drain("vector")
                vv = p.val("vv")
                p.wait("sync", "vv", vv)
                p.dma("sync", SCR["rows8"][3:4, :], rows[32:33, 0:S])
                wv2 = p.dma("sync", SCR["rows8"][4:5, :], rows[64:65, 0:S], cont=True)
                p.wait("sync", wv2)
                p.wait("sync", "vv", p.val("vv"))
                v1 = p.dma("sync", W0f[0:64, 1024:3072],
                           SCR["rows8"][3:4, :].broadcast_to([64, S]))
                v2 = p.dma("sync", W0f[64:128, 1024:3072],
                           SCR["rows8"][4:5, :].broadcast_to([64, S]), cont=True)
                p.wait("vector", v2)
                p.op("vector", lambda e: e.tensor_tensor(
                    out=psZ[:, 0:2048], in0=AF32[:, 0:2048],
                    in1=W0f[:, 1024:3072], op=ALU.mult), inc="vv")
                p.drain("vector")
                p.op("vector", lambda e, cc=cc: e.tensor_tensor(
                    out=xr(cc), in0=xr(cc), in1=psZ[:, 0:2048], op=ALU.add), inc="vv")
                p.drain("vector")
        ln_epilogue()

    def l1_ffn_layer(l):
        pcast_tok = None
        for ic in range(8):
            p.dma("gpsimd", SCR["w1b"][l][ic * 128:(ic + 1) * 128, :],
                  I["l1f_w1"][l][ic * 128:(ic + 1) * 128, :], cont=(ic > 0))
        for jc in range(22):
            pcast_tok = p.dma("gpsimd", SCR["w2b"][l][jc * 128:(jc + 1) * 128, :],
                              I["l1f_w2"][l][jc * 128:(jc + 1) * 128, :], cont=True)
        p.join()
        for dc in range(4):
            p.op("vector", lambda e, dc=dc: e.tensor_copy(out=BFA[:, dc * S:(dc + 1) * S],
                                                          in_=xr(dc)), inc="vv")
        for dc in range(4, 8):
            p.op("gpsimd", lambda e, dc=dc: e.memset(BFA[:, dc * S: dc * S + 1], 0.0), inc="gv")
            p.op("vector", lambda e, dc=dc: e.tensor_copy(
                out=BFA[:, dc * S + 1:(dc + 1) * S], in_=xr(dc)[:, 0:S - 1]), inc="vv")
        p.join()
        hb = lambda ic: BFA[:, ic * S:(ic + 1) * S]
        vw = None
        for ic in range(8):
            vw = p.dma("gpsimd", W0[:, ic * 512:(ic + 1) * 512],
                       I["l1f_wqkv"][l][ic * 128:(ic + 1) * 128, 1024:1536], cont=(ic > 0))
        p.wait("tensor", vw)
        for tc in range(16):
            bank_wait(0)
            for ic in range(8):
                p.op("tensor", lambda e, tc=tc, ic=ic: e.matmul(
                    psum[0][:, 0:512],
                    hb(ic)[:, tc * 128: tc * 128 + 128],
                    W0[:, ic * 512:(ic + 1) * 512],
                    start=(ic == 0), stop=(ic == 7)), inc="tv")
            tvv = p.val("tv")
            p.wait("vector", "tv", tvv)
            bank_set(0, p.op("vector", lambda e, tc=tc: e.tensor_copy(
                out=BFB[:, tc * 512:(tc + 1) * 512], in_=psum[0][:, 0:512]), inc="vv"))
        vvV = p.val("vv")
        p.wait("sync", "vv", vvV)
        for tc in range(16):
            p.dma("sync", SCR["vtok"][tc][:, 0:512], BFB[:, tc * 512:(tc + 1) * 512])
        p.join()
        qw = None
        for ic in range(8):
            qw = p.dma("gpsimd", W0[:, ic * 1024:(ic + 1) * 1024],
                       I["l1f_wqkv"][l][ic * 128:(ic + 1) * 128, 0:1024], cont=(ic > 0))
        p.wait("tensor", qw)
        qf = lambda c: BFAf[:, 8192 + c * S: 8192 + (c + 1) * S]
        kf = lambda c: BFBf[:, c * S:(c + 1) * S]
        for half, dstf in ((0, qf), (1, kf)):
            for oc in range(4):
                for tc in range(4):
                    bk = (oc * 4 + tc) % 2
                    bank_wait(bk)
                    for ic in range(8):
                        p.op("tensor", lambda e, half=half, oc=oc, tc=tc, ic=ic, bk=bk: e.matmul(
                            psum[bk][:, 0:512],
                            W0[:, ic * 1024 + half * 512 + oc * 128:
                                  ic * 1024 + half * 512 + oc * 128 + 128],
                            hb(ic)[:, tc * 512: tc * 512 + 512],
                            start=(ic == 0), stop=(ic == 7)), inc="tv")
                    tvv = p.val("tv")
                    p.wait("vector", "tv", tvv)
                    bank_set(bk, p.op("vector", lambda e, dstf=dstf, oc=oc, tc=tc, bk=bk:
                        e.tensor_copy(out=dstf(oc)[:, tc * 512: tc * 512 + 512],
                                      in_=psum[bk][:, 0:512]), inc="vv"))
        p.join()
        def l2norm_scale(srcf, scol_name, row8):
            for tc in range(4):
                for c in range(4):
                    sl = c % 2
                    p.wait("scalar", "tv", p.val("tv"))
                    sq = p.op("scalar", lambda e, c=c, tc=tc, sl=sl: e.activation(
                        out=AF32[:, sl * 512: sl * 512 + 512],
                        in_=srcf(c)[:, tc * 512: tc * 512 + 512], func=AF.Square), inc="sv")
                    p.wait("tensor", "sv", sq)
                    p.op("tensor", lambda e, c=c, tc=tc, sl=sl: e.matmul(
                        psum[3][0:8, 0:512], hones32[:, c * 16: c * 16 + 8],
                        AF32[:, sl * 512: sl * 512 + 512],
                        start=(c == 0), stop=(c == 3)), inc="tv")
                tvv = p.val("tv")
                p.wait("scalar", "tv", tvv)
                sv = p.op("scalar", lambda e, tc=tc: e.activation(
                    out=rows[0:8, tc * 512: tc * 512 + 512],
                    in_=psum[3][0:8, 0:512], func=AF.Sqrt), inc="sv")
                p.wait("tensor", "sv", sv)
                p.wait("vector", "sv", sv)
            p.op("vector", lambda e: e.tensor_scalar(out=rows[0:8, 0:S], in0=rows[0:8, 0:S],
                                                     scalar1=1e-12, scalar2=None,
                                                     op0=ALU.max), inc="vv")
            p.drain("vector")
            p.op("vector", lambda e: e.reciprocal(out=rows[0:8, 0:S], in_=rows[0:8, 0:S]), inc="vv")
            vv = p.val("vv")
            p.wait("sync", "vv", vv)
            wv = p.dma("sync", SCR["rows8"][row8:row8 + 8, :], rows[0:8, 0:S])
            p.wait("sync", wv)
            p.wait("scalar", wv)
            p.wait("vector", wv)
        l2norm_scale(qf, "qs", 0)
        def apply_scale(srcf, row8, scol):
            for c in range(4):
                p.wait("sync", "vv", p.val("vv"))
                rep = p.dma("sync", Dbf[:, 0:2048],
                            SCR["rows8"][row8 + 2 * c: row8 + 2 * c + 2, :]
                            .unsqueeze(1).broadcast_to([2, 64, S]))
                p.wait("vector", rep)
                p.op("vector", lambda e, srcf=srcf, c=c, scol=scol: e.scalar_tensor_tensor(
                    out=srcf(c), in0=srcf(c), scalar=scol, in1=Dbf[:, 0:2048],
                    op0=ALU.mult, op1=ALU.mult), inc="vv")
                p.drain("vector")
                vv = p.val("vv")
                p.wait("sync", "vv", vv)
        qscol = colf[:, 44:45]
        kscol = colf[:, 45:46]
        p.dma("sync", qscol, I["l1f_qs"][l])
        kv1 = p.dma("sync", kscol, I["l1f_ks"][l], cont=True)
        p.wait("vector", kv1)
        apply_scale(qf, 0, qscol)
        l2norm_scale(kf, "ks", 0)
        apply_scale(kf, 0, kscol)
        p.join()
        qb = lambda c: BFA[:, c * S:(c + 1) * S]
        kb = lambda c: BFA[:, 8192 + c * S: 8192 + (c + 1) * S]
        pw = p.dma("sync", sstat[:, 0:128], I["c_perm_half"])
        for (srcf, dstf, tcos, tsin) in ((qf, qb, "c_xpos_qc", "c_xpos_qs"),
                                         (kf, kb, "c_xpos_kc", "c_xpos_ks")):
            p.wait("sync", "vv", p.val("vv"))
            t1 = p.dma("sync", AF32[:, 0:2048], I[tcos])
            t2 = p.dma("sync", rows[:, 0:2048], I[tsin], cont=True)
            p.wait("tensor", t2)
            p.wait("tensor", pw)
            p.wait("vector", t2)
            for c in range(4):
                for tc in range(4):
                    bk = 2 + (tc % 2)
                    bank_wait(bk)
                    p.op("tensor", lambda e, srcf=srcf, c=c, tc=tc, bk=bk: e.matmul(
                        psum[bk][:, 0:512], sstat[:, 0:128],
                        srcf(c)[:, tc * 512: tc * 512 + 512],
                        start=True, stop=True), inc="tv")
                    tvv = p.val("tv")
                    p.wait("vector", "tv", tvv)
                    p.op("vector", lambda e, srcf=srcf, c=c, tc=tc: e.tensor_tensor(
                        out=Dbf[:, 0:512],
                        in0=srcf(c)[:, tc * 512: tc * 512 + 512],
                        in1=AF32[:, tc * 512: tc * 512 + 512], op=ALU.mult), inc="vv")
                    p.drain("vector")
                    p.op("vector", lambda e, c=c, tc=tc, bk=bk: e.tensor_tensor(
                        out=psum[bk][:, 0:512], in0=psum[bk][:, 0:512],
                        in1=rows[:, tc * 512: tc * 512 + 512], op=ALU.mult), inc="vv")
                    p.drain("vector")
                    vv = p.op("vector", lambda e, dstf=dstf, c=c, tc=tc, bk=bk: e.tensor_tensor(
                        out=dstf(c)[:, tc * 512: tc * 512 + 512],
                        in0=Dbf[:, 0:512],
                        in1=psum[bk][:, 0:512], op=ALU.add), inc="vv")
                    p.drain("vector")
                    bank_set(bk, vv)
        p.join()
        return hb, qb, kb, pcast_tok

    def l1_ffn():
        p.join()
        for dc in range(8):
            p.dma("sync", SCR["h0"][dc], xr(dc), cont=(dc > 0))
        for c in range(4):
            p.dma("sync", hones32[:, c * 16:(c + 1) * 16], I["c_hones16"][c])
        p.join()
        for l in range(2):
            hbf, qb, kb, pcast_tok = l1_ffn_layer(l)
            if KDUMP == "l1fq" and l == 0:
                dump_bf(qb, 4)
            if KDUMP == "l1fk" and l == 0:
                dump_bf(kb, 4)
            q1 = p.dma("sync", rows[0:1, 0:64], I["l1f_qsr"][l])
            k1 = p.dma("sync", rows[0:1, 64:128], I["l1f_ksr"][l], cont=True)
            p.wait("vector", k1)
            p.op("vector", lambda e: e.tensor_reduce(out=rows[0:1, 128:129], in_=rows[0:1, 0:64],
                                                     axis=AX.X, op=ALU.max,
                                                     apply_absolute_value=True), inc="vv")
            p.op("vector", lambda e: e.tensor_reduce(out=rows[0:1, 129:130], in_=rows[0:1, 64:128],
                                                     axis=AX.X, op=ALU.max,
                                                     apply_absolute_value=True), inc="vv")
            p.drain("vector")
            p.op("vector", lambda e: e.tensor_tensor(out=rows[0:1, 130:131], in0=rows[0:1, 128:129],
                                                     in1=rows[0:1, 129:130], op=ALU.mult), inc="vv")
            p.drain("vector")
            p.op("vector", lambda e: e.tensor_scalar(out=rows[0:1, 130:131], in0=rows[0:1, 130:131],
                                                     scalar1=-8.0, scalar2=None, op0=ALU.mult),
                 inc="vv")
            vv = p.val("vv")
            p.wait("sync", "vv", vv)
            mv = p.dma("sync", SCR["rows8"][7:8, 0:1], rows[0:1, 130:131])
            p.wait("sync", mv)
            mv2 = p.dma("sync", colf[:, 46:47], SCR["rows8"][7:8, 0:1].broadcast_to([128, 1]))
            p.wait("scalar", mv2)
            oT = lambda c: BFBf[:, c * S:(c + 1) * S]
            lmaskb = W0
            lv = None
            for kc in range(8):
                lv = p.dma("gpsimd", lmaskb[:, kc * 512:(kc + 1) * 512],
                           I["c_lmask"][kc * 128:(kc + 1) * 128, :], cont=(kc > 0))
            p.wait("vector", lv)
            p.wait("tensor", lv)
            for h in range(8):
                cc, hh = h // 2, h % 2
                for w in range(4):
                    kc_lo = 4 if w == 0 else 0
                    for half in range(2):
                        kcs = [kc for kc in range(kc_lo, 8) if kc % 2 == half] if False                             else list(range(kc_lo + half * ((8 - kc_lo) // 2), 8))
                        pass
                    kcl = list(range(kc_lo, 8))
                    for gi in range(0, len(kcl), 4):
                        grp = kcl[gi:gi + 4]
                        for j, kc in enumerate(grp):
                            bk = kc % 2
                            kcol = (w - 1) * 512 + kc * 128
                            tvv = p.op("tensor", lambda e, cc=cc, hh=hh, w=w, bk=bk,
                                       kcol=kcol: e.matmul(
                                psum[bk][:, 0:512],
                                kb(cc)[hh * 64: hh * 64 + 64, kcol: kcol + 128],
                                qb(cc)[hh * 64: hh * 64 + 64, w * 512:(w + 1) * 512],
                                start=True, stop=True), inc="tv")
                            p.wait("scalar", "tv", tvv)
                            sv = p.op("scalar", lambda e, j=j, bk=bk: e.activation(
                                out=W1[:, j * 512:(j + 1) * 512], in_=psum[bk][:, 0:512],
                                func=AF.Exp, scale=8.0, bias=colf[:, 46:47]), inc="sv")
                            p.wait("vector", "sv", sv)
                            p.wait("tensor", "sv", sv)
                            p.op("vector", lambda e, j=j, kc=kc: e.tensor_tensor(
                                out=W1[:, j * 512:(j + 1) * 512],
                                in0=W1[:, j * 512:(j + 1) * 512],
                                in1=lmaskb[:, kc * 512:(kc + 1) * 512], op=ALU.mult), inc="vv")
                        p.wait("tensor", "vv", p.val("vv"))
                        for j, kc in enumerate(grp):
                            kcol = (w - 1) * 512 + kc * 128
                            tci = (w - 1) * 4 + kc
                            sl = kc % 2
                            p.wait("sync", "tv", p.val("tv"))
                            vl = p.dma("sync", W0[:, 4096 + sl * 128: 4096 + sl * 128 + 64],
                                       SCR["vtok"][tci][:, h * 64:(h + 1) * 64])
                            p.wait("tensor", vl)
                            first = (kc == kc_lo)
                            last = (kc == 7)
                            p.op("tensor", lambda e, j=j, sl=sl, first=first, last=last: e.matmul(
                                psum[2][0:64, 0:512], W0[:, 4096 + sl * 128: 4096 + sl * 128 + 64],
                                W1[:, j * 512:(j + 1) * 512], start=first, stop=last), inc="tv")
                            p.op("tensor", lambda e, j=j, first=first, last=last: e.matmul(
                                psum[3][0:1, 0:512], onesb[:, 0:1],
                                W1[:, j * 512:(j + 1) * 512], start=first, stop=last), inc="tv")
                        p.wait("tensor", "tv", p.val("tv"))
                    tvv = p.val("tv")
                    p.wait("vector", "tv", tvv)
                    p.op("vector", lambda e, hh=hh, h=h, w=w, cc=cc: e.tensor_copy(
                        out=oT(cc)[hh * 64: hh * 64 + 64, w * 512:(w + 1) * 512],
                        in_=psum[2][0:64, 0:512]), inc="vv")
                    vvr = p.op("vector", lambda e, w=w: e.reciprocal(
                        out=rows[64:65, w * 512: w * 512 + 512],
                        in_=psum[3][0:1, 0:512]), inc="vv")
                    p.wait("tensor", "vv", vvr)
                p.drain("vector")
                vv = p.val("vv")
                p.wait("sync", "vv", vv)
                dn = p.dma("sync", SCR["rows8"][h:h + 1, :], rows[64:65, 0:S])
                p.wait("sync", dn)
            for c in range(4):
                p.wait("sync", "vv", p.val("vv"))
                rep = p.dma("sync", Dbf[:, 0:2048],
                            SCR["rows8"][2 * c: 2 * c + 2, :].unsqueeze(1)
                            .broadcast_to([2, 64, S]))
                p.wait("vector", rep)
                p.op("vector", lambda e, c=c: e.tensor_tensor(
                    out=BFA[:, c * S:(c + 1) * S], in0=oT(c), in1=Dbf[:, 0:2048],
                    op=ALU.mult), inc="vv")
                p.drain("vector")
            if KDUMP == "l1fo" and l == 0:
                dump_bf(lambda c: BFA[:, c * S:(c + 1) * S], 4)
            p.join()
            ow = None
            for ic in range(4):
                ow = p.dma("gpsimd", W0[:, ic * 1024:(ic + 1) * 1024],
                           I["l1f_wo"][l][ic * 128:(ic + 1) * 128, :], cont=(ic > 0))
            p.wait("tensor", ow)
            for oc in range(8):
                for tc in range(4):
                    bk = (oc * 4 + tc) % 2
                    for ic in range(4):
                        p.op("tensor", lambda e, oc=oc, tc=tc, ic=ic, bk=bk: e.matmul(
                            psum[bk][:, 0:512],
                            W0[:, ic * 1024 + oc * 128: ic * 1024 + oc * 128 + 128],
                            BFA[:, ic * S + tc * 512: ic * S + tc * 512 + 512],
                            start=(ic == 0), stop=(ic == 3)), inc="tv")
                    tvv = p.val("tv")
                    p.wait("vector", "tv", tvv)
                    vv = p.op("vector", lambda e, oc=oc, tc=tc, bk=bk: e.tensor_tensor(
                        out=xr(oc)[:, tc * 512: tc * 512 + 512], in0=psum[bk][:, 0:512],
                        in1=xr(oc)[:, tc * 512: tc * 512 + 512], op=ALU.add), inc="vv")
                    p.wait("tensor", "vv", vv)
            if KDUMP == "l1fx" and l == 0:
                dump_bf(xr)
            p.join()
            lgv = p.dma("sync", rows[:, 1024:1032], I["l1f_lng"][l])
            lbv = p.dma("sync", rows[:, 1032:1040], I["l1f_lnb"][l], cont=True)
            b1av = p.dma("sync", rows[:, 1040:1062], I["l1f_b1a"][l], cont=True)
            b1gv = p.dma("sync", rows[:, 1062:1084], I["l1f_b1g"][l], cont=True)
            b2v = p.dma("sync", rows[:, 1084:1092], I["l1f_b2"][l], cont=True)
            p.wait("vector", b2v)
            p.wait("scalar", b2v)
            def build_hsh(dc):
                p.wait("vector", "sv", p.val("sv"))
                p.wait("vector", "tv", p.val("tv"))
                if dc < 4:
                    p.op("vector", lambda e, dc=dc: e.tensor_copy(out=AF32[:, 0:2048],
                                                                  in_=xr(dc)), inc="vv")
                else:
                    p.drain("gpsimd")
                    p.wait("gpsimd", "vv", p.val("vv"))
                    p.wait("gpsimd", "sv", p.val("sv"))
                    p.wait("gpsimd", "tv", p.val("tv"))
                    p.op("gpsimd", lambda e: e.memset(AF32[:, 0:1], 0.0), inc="gv")
                    p.wait("vector", "gv", p.val("gv"))
                    p.op("vector", lambda e, dc=dc: e.tensor_copy(
                        out=AF32[:, 1:2048], in_=xr(dc)[:, 0:S - 1]), inc="vv")
                p.drain("vector")
            for tc in range(4):
                for dc in range(8):
                    build_hsh(dc)
                    vvh = p.val("vv")
                    p.wait("scalar", "vv", vvh)
                    sq = p.op("scalar", lambda e, tc=tc: e.activation(
                        out=BFAf[:, 0:512],
                        in_=AF32[:, tc * 512: tc * 512 + 512], func=AF.Square), inc="sv")
                    p.wait("tensor", "sv", sq)
                    p.wait("tensor", "vv", vvh)
                    p.op("tensor", lambda e, dc=dc, tc=tc: e.matmul(
                        psum[0][0:1, 0:512], ones32[:, 0:1],
                        AF32[:, tc * 512: tc * 512 + 512],
                        start=(dc == 0), stop=(dc == 7)), inc="tv")
                    tvq = p.op("tensor", lambda e, dc=dc, tc=tc: e.matmul(
                        psum[1][0:1, 0:512], ones32[:, 0:1],
                        BFAf[:, 0:512],
                        start=(dc == 0), stop=(dc == 7)), inc="tv")
                    p.wait("scalar", "tv", tvq)
                    p.wait("vector", "tv", tvq)
                tvv = p.val("tv")
                p.wait("vector", "tv", tvv)
                p.op("vector", lambda e, tc=tc: e.tensor_scalar(
                    out=BFBf[0:1, tc * 512:tc * 512 + 512], in0=psum[0][0:1, 0:512],
                    scalar1=1.0 / 1024.0, scalar2=None, op0=ALU.mult), inc="vv")
                p.op("vector", lambda e, tc=tc: e.tensor_scalar(
                    out=BFBf[0:1, 2048 + tc * 512: 2048 + tc * 512 + 512], in0=psum[1][0:1, 0:512],
                    scalar1=1.0 / 1024.0, scalar2=None, op0=ALU.mult), inc="vv")
                vv = p.val("vv")
                p.wait("tensor", "vv", vv)
            p.drain("vector")
            MEAN = BFBf[0:1, 0:S]; M2 = BFBf[0:1, 2048:2048 + S]
            AV_ = BFBf[0:1, 4096:4096 + S]; BV_ = BFBf[0:1, 6144:6144 + S]
            p.op("vector", lambda e: e.scalar_tensor_tensor(
                out=AV_, in0=MEAN, scalar=-1.0, in1=MEAN, op0=ALU.mult, op1=ALU.mult))
            p.drain("vector")
            p.op("vector", lambda e: e.tensor_tensor(out=AV_, in0=AV_, in1=M2, op=ALU.add), inc="vv")
            p.drain("vector")
            p.wait("scalar", "vv", p.val("vv"))
            p.op("scalar", lambda e: e.activation(out=AV_, in_=AV_, func=AF.Sqrt,
                                                  bias=colf[0:1, 62:63]), inc="sv")
            p.wait("vector", "sv", p.val("sv"))
            p.op("vector", lambda e: e.reciprocal(out=AV_, in_=AV_))
            p.drain("vector")
            p.op("vector", lambda e: e.scalar_tensor_tensor(
                out=BV_, in0=MEAN, scalar=-1.0, in1=AV_, op0=ALU.mult, op1=ALU.mult), inc="vv")
            vv = p.val("vv")
            p.wait("sync", "vv", vv)
            p.dma("sync", SCR["rows8"][0:1, :], AV_)
            wv = p.dma("sync", SCR["rows8"][1:2, :], BV_, cont=True)
            p.wait("sync", wv)
            p.dma("sync", BFBf[:, 0:2048], SCR["rows8"][0:1, :].broadcast_to([128, S]))
            vb = p.dma("sync", BFBf[:, 2048:4096], SCR["rows8"][1:2, :].broadcast_to([128, S]),
                       cont=True)
            p.wait("vector", vb)
            hb2 = lambda dc: BFA[:, 16384 + dc * S: 16384 + (dc + 1) * S]
            for dc in range(8):
                build_hsh(dc)
                p.op("vector", lambda e, dc=dc: e.scalar_tensor_tensor(
                    out=AF32[:, 0:2048], in0=AF32[:, 0:2048], scalar=rows[:, 1024 + dc: 1024 + dc + 1],
                    in1=BFBf[:, 0:2048], op0=ALU.mult, op1=ALU.mult), inc="vv")
                p.drain("vector")
                p.op("vector", lambda e, dc=dc: e.scalar_tensor_tensor(
                    out=BFBf[:, 4096:6144], in0=BFBf[:, 2048:4096],
                    scalar=rows[:, 1024 + dc: 1024 + dc + 1],
                    in1=AF32[:, 0:2048], op0=ALU.mult, op1=ALU.add), inc="vv")
                p.drain("vector")
                p.op("vector", lambda e, dc=dc: e.tensor_scalar(
                    out=hb2(dc), in0=BFBf[:, 4096:6144],
                    scalar1=rows[:, 1032 + dc: 1032 + dc + 1], scalar2=None, op0=ALU.add), inc="vv")
                p.drain("vector")
            if KDUMP == "l1fh" and l == 0:
                dump_bf(hb2)
            p.join()
            prod = lambda j: BFA[:, j * 512:(j + 1) * 512]
            w1base = SCR["w1b"][l]
            w2base = SCR["w2b"][l]
            WG = {}
            wtok = {}
            W2G = {}
            w2tok = {}
            p.wait("sync", "tv", p.val("tv"))
            p.wait("gpsimd", "tv", p.val("tv"))
            p.wait("sync", pcast_tok)
            for tc in range(4):
                def w1_load(j, tc=None):
                    sl = j % 4
                    q = "sync" if j % 2 else "gpsimd"
                    p.wait(q, "tv", WG.get(sl, 0))
                    t1 = p.dma(q, W0[:, sl * 2048: sl * 2048 + 1024],
                               bass.AP(w1base.tensor, w1base.offset + j * 128,
                                       [[5632, 128], [128 * 5632, 8], [1, 128]]))
                    t2 = p.dma(q, W0[:, sl * 2048 + 1024: sl * 2048 + 2048],
                               bass.AP(w1base.tensor, w1base.offset + 2816 + j * 128,
                                       [[5632, 128], [128 * 5632, 8], [1, 128]]), cont=True)
                    wtok[j] = (t1, t2)
                for j in range(4):
                    w1_load(j)
                for j in range(22):
                    sl = j % 4
                    t1, t2 = wtok.pop(j)
                    p.wait("tensor", t1)
                    p.wait("tensor", t2)
                    for ic in range(8):
                        p.op("tensor", lambda e, ic=ic, tc=tc, sl=sl: e.matmul(
                            psum[0][:, 0:512], W0[:, sl * 2048 + ic * 128: sl * 2048 + ic * 128 + 128],
                            hb2(ic)[:, tc * 512: tc * 512 + 512],
                            start=(ic == 0), stop=(ic == 7)), inc="tv")
                        p.op("tensor", lambda e, ic=ic, tc=tc, sl=sl: e.matmul(
                            psum[1][:, 0:512], W0[:, sl * 2048 + 1024 + ic * 128:
                                                    sl * 2048 + 1024 + ic * 128 + 128],
                            hb2(ic)[:, tc * 512: tc * 512 + 512],
                            start=(ic == 0), stop=(ic == 7)), inc="tv")
                    tvv = p.val("tv")
                    WG[sl] = tvv
                    if j + 4 < 22:
                        w1_load(j + 4)
                    p.wait("scalar", "tv", tvv)
                    sv = p.op("scalar", lambda e, j=j: e.activation(
                        out=AF32[:, 0:512], in_=psum[1][:, 0:512], func=AF.Gelu,
                        bias=rows[:, 1062 + j: 1062 + j + 1]), inc="sv")
                    p.wait("vector", "sv", sv)
                    vv = p.op("vector", lambda e, j=j: e.scalar_tensor_tensor(
                        out=prod(j), in0=psum[0][:, 0:512], scalar=rows[:, 1040 + j: 1040 + j + 1],
                        in1=AF32[:, 0:512], op0=ALU.add, op1=ALU.mult), inc="vv")
                    p.drain("vector")
                    p.wait("tensor", "vv", vv)
                    p.wait("scalar", "vv", vv)
                def w2_load(op2, j):
                    sl = j % 8
                    q = "sync" if j % 2 else "gpsimd"
                    p.wait(q, "tv", W2G.get(sl, 0))
                    w2tok[j] = p.dma(q, W1[:, sl * 256:(sl + 1) * 256],
                                     w2base[j * 128:(j + 1) * 128, op2 * 256:(op2 + 1) * 256])
                for op2 in range(4):
                    for j in range(8):
                        w2_load(op2, j)
                    for j in range(22):
                        sl = j % 8
                        p.wait("tensor", w2tok.pop(j))
                        p.op("tensor", lambda e, j=j, sl=sl: e.matmul(
                            psum[2][:, 0:512], W1[:, sl * 256: sl * 256 + 128],
                            prod(j), start=(j == 0), stop=(j == 21)), inc="tv")
                        p.op("tensor", lambda e, j=j, sl=sl: e.matmul(
                            psum[3][:, 0:512], W1[:, sl * 256 + 128: sl * 256 + 256],
                            prod(j), start=(j == 0), stop=(j == 21)), inc="tv")
                        W2G[sl] = p.val("tv")
                        if j + 8 < 22:
                            w2_load(op2, j + 8)
                    tvv = p.val("tv")
                    p.wait("vector", "tv", tvv)
                    for pi in range(2):
                        oc = op2 * 2 + pi
                        vv = p.op("vector", lambda e, oc=oc, tc=tc, pi=pi: e.scalar_tensor_tensor(
                            out=xr(oc)[:, tc * 512: tc * 512 + 512], in0=psum[2 + pi][:, 0:512],
                            scalar=rows[:, 1084 + oc: 1084 + oc + 1],
                            in1=xr(oc)[:, tc * 512: tc * 512 + 512], op0=ALU.add, op1=ALU.add),
                            inc="vv")
                    p.drain("vector")
                    p.wait("tensor", "vv", p.val("vv"))
            p.join()
            if KDUMP == "l1fz" and l == 0:
                dump_bf(xr)
            if KDUMP == "l1fz1" and l == 1:
                dump_bf(xr)
        for dc in range(8):
            lv = p.dma("sync", AF32[:, 0:2048], SCR["h0"][dc])
            p.wait("vector", lv)
            vv = p.op("vector", lambda e, dc=dc: e.tensor_tensor(
                out=xr(dc), in0=xr(dc), in1=AF32[:, 0:2048], op=ALU.add), inc="vv")
            p.drain("vector")
            p.wait("sync", "vv", vv)
        ln_epilogue()

    def l2_attn():
        p.join()
        cast_xb()
        Ab = A[:].bitcast(BF16)
        W1f = W1[:].bitcast(F32)
        def sb3(apobj, off, dims):
            return bass.AP(apobj.tensor, apobj.offset + off,
                           [[apobj.ap[0][0], apobj.ap[0][1]]] + dims)
        p.dma("sync", colf[0:16, 48:49], I["l2_vgsb"])
        p.dma("sync", colf[0:16, 49:50], I["l2_vt0"], cont=True)
        p.dma("sync", colf[0:16, 51:52], I["l2_btau"], cont=True)
        p.dma("sync", colf[0:16, 52:53], I["l2_cch"], cont=True)
        p.dma("sync", colf[0:16, 53:54], I["l2_bgm"], cont=True)
        p.dma("sync", colf[0:16, 54:55], I["l2_gam"], cont=True)
        p.dma("sync", colf[0:16, 55:56], I["l2_alpha"].broadcast_to([16, 1]), cont=True)
        cv = p.dma("sync", colf[0:16, 56:57], I["l2_icth"].broadcast_to([16, 1]), cont=True)
        p.wait("vector", cv)
        def colop(fn):
            p.op("vector", fn, inc="vv")
            p.drain("vector")
        colop(lambda e: e.tensor_tensor(out=colf[0:16, 50:51], in0=colf[0:16, 48:49],
                                        in1=colf[0:16, 49:50], op=ALU.subtract))
        colop(lambda e: e.reciprocal(out=colf[0:16, 57:58], in_=colf[0:16, 52:53]))
        colop(lambda e: e.tensor_tensor(out=colf[0:16, 57:58], in0=colf[0:16, 57:58],
                                        in1=colf[0:16, 51:52], op=ALU.mult))
        colop(lambda e: e.tensor_tensor(out=colf[0:16, 58:59], in0=colf[0:16, 51:52],
                                        in1=colf[0:16, 55:56], op=ALU.mult))
        colop(lambda e: e.tensor_scalar(out=colf[0:16, 58:59], in0=colf[0:16, 58:59],
                                        scalar1=-1.0, scalar2=None, op0=ALU.mult))
        colop(lambda e: e.tensor_tensor(out=colf[0:16, 59:60], in0=colf[0:16, 55:56],
                                        in1=colf[0:16, 56:57], op=ALU.mult))
        colop(lambda e: e.tensor_scalar(out=colf[0:16, 59:60], in0=colf[0:16, 59:60],
                                        scalar1=-1.0, scalar2=None, op0=ALU.mult))
        colop(lambda e: e.tensor_tensor(out=colf[0:16, 60:61], in0=colf[0:16, 53:54],
                                        in1=colf[0:16, 54:55], op=ALU.mult))
        vvc = p.val("vv")
        p.wait("sync", "vv", vvc)
        p.dma("sync", SCR["c16"], colf[0:16, 57:60])
        g2 = p.dma("sync", SCR["c16b"], colf[0:16, 60:61], cont=True)
        p.wait("sync", g2)
        wv = None
        for ic in range(8):
            wv = p.dma("gpsimd", W1[:, ic * 16:(ic + 1) * 16],
                       I["l2_wvgs"][ic * 128:(ic + 1) * 128, :], cont=(ic > 0))
        p.wait("tensor", wv)
        for tc in range(4):
            for ic in range(8):
                p.op("tensor", lambda e, tc=tc, ic=ic: e.matmul(
                    psum[2][0:16, 0:512], W1[:, ic * 16:(ic + 1) * 16],
                    xb(ic)[:, tc * 512: tc * 512 + 512],
                    start=(ic == 0), stop=(ic == 7)), inc="tv")
            tvv = p.val("tv")
            p.wait("vector", "tv", tvv)
            vv = p.op("vector", lambda e, tc=tc: e.tensor_scalar(
                out=rows[0:16, tc * 512: tc * 512 + 512], in0=psum[2][0:16, 0:512],
                scalar1=colf[0:16, 50:51], scalar2=None, op0=ALU.add), inc="vv")
            p.wait("tensor", "vv", vv)
        vv = p.val("vv")
        p.wait("sync", "vv", vv)
        at = p.dma("sync", SCR["ascr"], rows[0:16, 0:2048])
        p.wait("vector", at)
        for oc24 in range(24):
            sl = oc24 % 2
            p.wait("gpsimd", "tv", p.val("tv"))
            wv = None
            for ic in range(8):
                wv = p.dma("gpsimd", W0[:, sl * 1024 + ic * 128: sl * 1024 + ic * 128 + 128],
                           I["l2_wqkv"][ic * 128:(ic + 1) * 128, oc24 * 128:(oc24 + 1) * 128],
                           cont=(ic > 0))
            p.wait("tensor", wv)
            for tc in range(4):
                bk = tc % 2
                bank_wait(bk)
                for ic in range(8):
                    p.op("tensor", lambda e, sl=sl, tc=tc, ic=ic, bk=bk: e.matmul(
                        psum[bk][:, 0:512], W0[:, sl * 1024 + ic * 128: sl * 1024 + ic * 128 + 128],
                        xb(ic)[:, tc * 512: tc * 512 + 512],
                        start=(ic == 0), stop=(ic == 7)), inc="tv")
                tvv = p.val("tv")
                p.wait("vector", "tv", tvv)
                bank_set(bk, p.op("vector", lambda e, tc=tc, bk=bk: e.tensor_copy(
                    out=AF32[:, tc * 512: tc * 512 + 512], in_=psum[bk][:, 0:512]), inc="vv"))
            vv = p.val("vv")
            p.wait("sync", "vv", vv)
            if oc24 < 8:
                p.dma("sync", SCR["qscr"][2 * oc24], AF32[0:64, 0:2048])
                t = p.dma("sync", SCR["qscr"][2 * oc24 + 1], AF32[64:128, 0:2048], cont=True)
            elif oc24 < 16:
                t = p.dma("sync", SCR["kraw"][oc24 - 8], AF32[:, 0:2048])
            else:
                c0 = oc24 - 16
                t = None
                for hp in range(2):
                    h = 2 * c0 + hp
                    for cc in range(16):
                        dst = bass.AP(SCR["vrc"].tensor, SCR["vrc"].offset + (h * 16 + cc) * 8192,
                                      [[128, 32], [4096, 2], [1, 128]])
                        t = p.dma("sync", dst,
                                  AF32[hp * 64:(hp + 1) * 64, cc * 128:(cc + 1) * 128],
                                  cont=not (hp == 0 and cc == 0))
            p.wait("vector", t)
        p.join()
        for c in range(8):
            hs = p.dma("sync", sstat[:, 0:16], I["c_hones16"][c])
            kv = p.dma("sync", AF32[:, 0:2048], SCR["kraw"][c], cont=True)
            p.wait("tensor", kv)
            p.wait("scalar", kv)
            p.wait("vector", kv)
            for tc in range(4):
                sq = p.op("scalar", lambda e, tc=tc: e.activation(
                    out=BFBf[:, 0:512], in_=AF32[:, tc * 512: tc * 512 + 512],
                    func=AF.Square), inc="sv")
                p.wait("tensor", "sv", sq)
                p.op("tensor", lambda e, tc=tc: e.matmul(
                    psum[2][0:16, 0:512], sstat[:, 0:16], AF32[:, tc * 512: tc * 512 + 512],
                    start=True, stop=True), inc="tv")
                tvq = p.op("tensor", lambda e: e.matmul(
                    psum[3][0:16, 0:512], sstat[:, 0:16], BFBf[:, 0:512],
                    start=True, stop=True), inc="tv")
                p.wait("vector", "tv", tvq)
                p.wait("scalar", "tv", tvq)
                p.op("vector", lambda e, tc=tc: e.tensor_scalar(
                    out=rows[0:16, tc * 512: tc * 512 + 512], in0=psum[2][0:16, 0:512],
                    scalar1=1.0 / 64.0, scalar2=None, op0=ALU.mult), inc="vv")
                vv = p.op("vector", lambda e, tc=tc: e.tensor_scalar(
                    out=BFBf[0:16, 2048 + tc * 512: 2048 + tc * 512 + 512],
                    in0=psum[3][0:16, 0:512],
                    scalar1=1.0 / 64.0, scalar2=None, op0=ALU.mult), inc="vv")
                p.wait("tensor", "vv", vv)
            p.drain("vector")
            MU = rows[0:16, 0:2048]
            MS = BFBf[0:16, 2048:4096]
            NG = BFBf[0:16, 4096:6144]
            p.op("vector", lambda e: e.scalar_tensor_tensor(
                out=NG, in0=MU, scalar=-1.0,
                in1=MU, op0=ALU.mult, op1=ALU.mult), inc="vv")
            p.drain("vector")
            p.op("vector", lambda e: e.tensor_tensor(
                out=MS, in0=MS, in1=NG, op=ALU.add), inc="vv")
            vv = p.val("vv")
            p.wait("scalar", "vv", vv)
            p.op("scalar", lambda e: e.activation(out=MS, in_=MS, func=AF.Sqrt,
                                                  bias=colf[0:16, 62:63]), inc="sv")
            p.wait("vector", "sv", p.val("sv"))
            p.op("vector", lambda e: e.reciprocal(out=MS, in_=MS))
            p.drain("vector")
            p.op("vector", lambda e: e.tensor_scalar(
                out=MS, in0=MS,
                scalar1=colf[0:16, 60:61], scalar2=None, op0=ALU.mult), inc="vv")
            p.drain("vector")
            p.op("vector", lambda e: e.scalar_tensor_tensor(
                out=NG, in0=MU, scalar=-1.0,
                in1=MS, op0=ALU.mult, op1=ALU.mult), inc="vv")
            vv = p.val("vv")
            p.wait("sync", "vv", vv)
            p.dma("sync", SCR["rows8"][0:2, :], BFBf[2 * c: 2 * c + 2, 2048:4096])
            wb = p.dma("sync", SCR["rows8"][2:4, :], BFBf[2 * c: 2 * c + 2, 4096:6144],
                       cont=True)
            p.wait("sync", wb)
            p.dma("sync", Dbf[:, 0:2048],
                  SCR["rows8"][0:2, :].unsqueeze(1).broadcast_to([2, 64, S]))
            rb = p.dma("sync", BFBf[:, 2048:4096],
                       SCR["rows8"][2:4, :].unsqueeze(1).broadcast_to([2, 64, S]), cont=True)
            p.wait("vector", rb)
            p.op("vector", lambda e: e.tensor_tensor(out=AF32[:, 0:2048], in0=AF32[:, 0:2048],
                                                     in1=Dbf[:, 0:2048], op=ALU.mult), inc="vv")
            p.drain("vector")
            p.op("vector", lambda e: e.tensor_tensor(out=AF32[:, 0:2048], in0=AF32[:, 0:2048],
                                                     in1=BFBf[:, 2048:4096], op=ALU.add), inc="vv")
            vv = p.val("vv")
            p.wait("sync", "vv", vv)
            p.dma("sync", SCR["kscr"][2 * c], AF32[0:64, 0:2048])
            kw = p.dma("sync", SCR["kscr"][2 * c + 1], AF32[64:128, 0:2048], cont=True)
            p.wait("vector", kw)
        p.join()
        if KDUMP == "l2k":
            dv = p.dma("sync", dbg, bass.AP(SCR["kscr"].tensor, SCR["kscr"].offset,
                                            [[2048, 1024], [1, 2048]]))
            p.wait("sync", dv)
        if KDUMP == "l2a":
            dv = p.dma("sync", dbg[0:16, :], SCR["ascr"])
            p.wait("sync", dv)
        yv = None
        for dh in range(32):
            yv = p.dma("gpsimd", W0[:, dh * 128:(dh + 1) * 128], I["c_yred2"][dh], cont=(dh > 0))
        p.wait("tensor", yv)
        dpv = p.dma("sync", rows[0:2, 0:128], I["c_dppat"])
        p.wait("tensor", dpv)
        PZ_G = 0
        A1o, A2o, A3o, A4o = 0, 4096, 8192, 12288
        A1 = BFAf[:, 0:4096]; A2 = BFAf[:, 4096:8192]
        A3 = BFAf[:, 8192:12288]; A4 = BFAf[:, 12288:16384]
        d3 = [[128, 32], [1, 128]]
        A1_3 = sb3(BFAf, A1o, d3); A2_3 = sb3(BFAf, A2o, d3)
        Db_3 = sb3(Db[:], 0, d3)
        Wsm3 = None
        koff = lambda sl: 64 + sl * 384
        qoff = lambda sl: 192 + sl * 384
        aoff = lambda sl: 320 + sl * 384
        vt = lambda sl: BFBf[:, sl * 4096:(sl + 1) * 4096]
        VT_G = [0, 0]
        YF_G = 0
        y2tok = None
        gq_tok = None
        for h in range(16):
            p.wait("sync", "sv", p.val("sv"))
            ccols = p.dma("sync", colf[:, 57:60], SCR["c16"][h:h + 1, 0:3].broadcast_to([128, 3]))
            p.wait("scalar", ccols)
            p.wait("sync", "vv", p.val("vv"))
            wsm = p.dma("sync", W1f[:, 0:32], I["l2_wlsm"][h])
            p.wait("vector", wsm)
            p.drain("gpsimd")
            p.wait("gpsimd", "vv", p.val("vv"))
            p.wait("gpsimd", "tv", p.val("tv"))
            p.op("gpsimd", lambda e: e.memset(W1f[:, 32:64], 0.0), inc="gv")
            gq = p.dma("gpsimd", Ab[64:128, 0:2048], SCR["qscr"][h])
            gq_tok = p.dma("gpsimd", Ab[64:128, 2048:2112], I["l2_wlT"][h], cont=True)
            p.wait("vector", "gv", p.val("gv"))
            if y2tok is not None:
                p.wait("vector", y2tok)
            Wsm3 = W1f[:, 0:32].unsqueeze(2).broadcast_to([128, 32, 128])
            for ch in range(16):
                sl = ch % 2
                p.wait("sync", "vv", VT_G[sl])
                a_t = p.dma("sync", W1f[:, aoff(sl):aoff(sl) + 128],
                            SCR["ascr"][h:h + 1, ch * 128:(ch + 1) * 128].broadcast_to([128, 128]))
                p.dma("sync", W1f[:, koff(sl):koff(sl) + 128],
                      bass.AP(SCR["kscr"].tensor, SCR["kscr"].offset + h * 131072 + ch * 128,
                              [[2048, 64], [0, 2], [1, 128]]), cont=True)
                p.dma("sync", W1f[:, qoff(sl):qoff(sl) + 128],
                      bass.AP(SCR["qscr"].tensor, SCR["qscr"].offset + h * 131072 + ch * 128,
                              [[2048, 64], [0, 2], [1, 128]]), cont=True)
                v_t = p.dma("sync", BFBf[0:2, sl * 4096:(sl + 1) * 4096],
                            bass.AP(SCR["vrc"].tensor, SCR["vrc"].offset + (h * 16 + ch) * 8192,
                                    [[4096, 2], [1, 4096]]), cont=True)
                p.wait("vector", v_t)
                p.wait("tensor", v_t)
                d3h = [[128, 16], [1, 128]]
                zvs = []
                for ph in range(2):
                    po = ph * 2048
                    wsm3h = W1f[:, 16 * ph: 16 * ph + 16].unsqueeze(2).broadcast_to([128, 16, 128])
                    a3 = W1f[:, aoff(sl):aoff(sl) + 128].unsqueeze(1).broadcast_to([128, 16, 128])
                    zv = p.op("vector", lambda e, wsm3h=wsm3h, a3=a3, po=po: e.tensor_tensor(
                        out=sb3(BFAf, A1o + po, d3h), in0=wsm3h, in1=a3, op=ALU.add), inc="vv")
                    zvs.append(zv)
                p.drain("vector")
                sths = []
                for ph in range(2):
                    po = ph * 2048
                    A1h = BFAf[:, A1o + po: A1o + po + 2048]
                    A2h = BFAf[:, A2o + po: A2o + po + 2048]
                    A3h = BFAf[:, A3o + po: A3o + po + 2048]
                    A4h = BFAf[:, A4o + po: A4o + po + 2048]
                    p.wait("scalar", "vv", zvs[ph])
                    p.op("scalar", lambda e, A1h=A1h, A2h=A2h: e.activation(
                        out=A2h, in_=A1h, func=AF.Sigmoid, scale=-1.0), inc="sv")
                    p.drain("scalar")
                    p.op("scalar", lambda e, A1h=A1h, A2h=A2h: e.activation(
                        out=A1h, in_=A2h, func=AF.Ln), inc="sv")
                    p.drain("scalar")
                    p.op("scalar", lambda e, A1h=A1h, A3h=A3h: e.activation(
                        out=A3h, in_=A1h, func=AF.Exp, scale=colf[:, 57:58]), inc="sv")
                    p.drain("scalar")
                    sth = p.op("scalar", lambda e, A1h=A1h, A4h=A4h: e.activation(
                        out=A4h, in_=A1h, func=AF.Tanh, scale=colf[:, 58:59],
                        bias=colf[:, 59:60]), inc="sv")
                    p.drain("scalar")
                    sths.append(sth)
                ybvs = []
                for ph in range(2):
                    po = ph * 2048
                    A1h = BFAf[:, A1o + po: A1o + po + 2048]
                    A2h = BFAf[:, A2o + po: A2o + po + 2048]
                    A3h = BFAf[:, A3o + po: A3o + po + 2048]
                    A4h = BFAf[:, A4o + po: A4o + po + 2048]
                    k3 = W1f[:, koff(sl):koff(sl) + 128].unsqueeze(1).broadcast_to([128, 16, 128])
                    q3 = W1f[:, qoff(sl):qoff(sl) + 128].unsqueeze(1).broadcast_to([128, 16, 128])
                    p.wait("vector", "sv", sths[ph])
                    p.op("vector", lambda e, A1h=A1h, A2h=A2h: e.scalar_tensor_tensor(
                        out=A2h, in0=A2h, scalar=1.0, in1=A1h,
                        op0=ALU.subtract, op1=ALU.mult), inc="vv")
                    p.drain("vector")
                    p.op("vector", lambda e, A2h=A2h, A4h=A4h: e.tensor_tensor(
                        out=A2h, in0=A2h, in1=A4h, op=ALU.mult), inc="vv")
                    p.drain("vector")
                    p.op("vector", lambda e, k3=k3, po=po: e.tensor_tensor(
                        out=sb3(BFAf, A2o + po, d3h), in0=sb3(BFAf, A2o + po, d3h),
                        in1=k3, op=ALU.mult), inc="vv")
                    p.drain("vector")
                    p.wait("tensor", "vv", PZ_G)
                    for s in range(4):
                        p.op("tensor", lambda e, s=s, sl=sl, po=po: e.matmul(
                            psZ[:, s * 512:(s + 1) * 512], rows[0:2, 0:128],
                            BFBf[0:2, sl * 4096 + po + s * 512: sl * 4096 + po + s * 512 + 512],
                            start=True, stop=True), inc="tv")
                    tvz = p.val("tv")
                    p.wait("vector", "tv", tvz)
                    bv = p.op("vector", lambda e, A2h=A2h: e.tensor_tensor(
                        out=A2h, in0=A2h, in1=psZ[:, 0:2048], op=ALU.mult), inc="vv")
                    p.drain("vector")
                    PZ_G = bv
                    for jj in range(16):
                        j = ph * 16 + jj
                        p.op("vector", lambda e, j=j, po=po, jj=jj: e.tensor_tensor_scan(
                            out=A1[:, j * 128:(j + 1) * 128], data0=A3[:, j * 128:(j + 1) * 128],
                            data1=A2[:, j * 128:(j + 1) * 128], initial=W1f[:, 32 + j: 33 + j],
                            op0=ALU.mult, op1=ALU.add), inc="vv")
                    p.drain("vector")
                    p.op("vector", lambda e, ph=ph, po=po: e.tensor_copy(
                        out=W1f[:, 32 + 16 * ph: 48 + 16 * ph],
                        in_=sb3(BFAf, A1o + po + 127, [[128, 16]])), inc="vv")
                    p.drain("vector")
                    p.wait("vector", "tv", YF_G)
                    ybv = p.op("vector", lambda e, q3=q3, po=po: e.tensor_tensor(
                        out=sb3(Db[:], po, d3h), in0=sb3(BFAf, A1o + po, d3h),
                        in1=q3, op=ALU.mult), inc="vv")
                    p.drain("vector")
                    ybvs.append(ybv)
                    p.wait("tensor", "vv", ybv)
                    for jj in range(16):
                        dh = ph * 16 + jj
                        p.op("tensor", lambda e, dh=dh: e.matmul(
                            psum[2][:, 0:128], W0[:, dh * 128:(dh + 1) * 128],
                            Db[:, dh * 128:(dh + 1) * 128],
                            start=(dh == 0), stop=(dh == 31)), inc="tv")
                VT_G[sl] = ybvs[1]
                YF_G = p.val("tv")
                p.wait("vector", "tv", YF_G)
                p.op("vector", lambda e, ch=ch: e.tensor_copy(
                    out=AF32[0:64, ch * 128:(ch + 1) * 128], in_=psum[2][0:64, 0:128]), inc="vv")
            p.wait("tensor", gq_tok)
            for tc in range(4):
                tvv = p.op("tensor", lambda e, tc=tc: e.matmul(
                    psum[3][0:64, 0:512], Ab[64:128, 2048:2112],
                    Ab[64:128, tc * 512:(tc + 1) * 512], start=True, stop=True), inc="tv")
                p.wait("vector", "tv", tvv)
                vv = p.op("vector", lambda e, tc=tc: e.tensor_tensor(
                    out=rows[64:128, tc * 512:(tc + 1) * 512], in0=psum[3][0:64, 0:512],
                    in1=AF32[0:64, tc * 512:(tc + 1) * 512], op=ALU.add), inc="vv")
                p.wait("tensor", "vv", vv)
            vv = p.val("vv")
            p.wait("sync", "vv", vv)
            y2tok = p.dma("sync", SCR["y2"][h], rows[64:128, 0:2048])
        p.join()
        if KDUMP == "l2y":
            for c in range(8):
                yv = p.dma("gpsimd", BFA[:, c * 2048:(c + 1) * 2048], SCR["y2"][2 * c:2 * c + 2],
                           cont=(c > 0))
            p.wait("vector", yv)
            dump_bf(lambda dc: BFA[:, dc * 2048:(dc + 1) * 2048])
        yv = None
        for c in range(8):
            yv = p.dma("gpsimd", BFA[:, c * 2048:(c + 1) * 2048], SCR["y2"][2 * c:2 * c + 2],
                       cont=(c > 0))
        wv = load_w_bf16(W0, "l2_wo", 1024, 8)
        p.wait("tensor", yv)
        p.wait("tensor", wv)
        for oc in range(8):
            for tc in range(4):
                bk = (oc * 4 + tc) % 2
                bank_wait(bk)
                for ic in range(8):
                    p.op("tensor", lambda e, oc=oc, tc=tc, ic=ic, bk=bk: e.matmul(
                        psum[bk][:, 0:512],
                        W0[:, ic * 1024 + oc * 128: ic * 1024 + oc * 128 + 128],
                        BFA[:, ic * 2048 + tc * 512: ic * 2048 + tc * 512 + 512],
                        start=(ic == 0), stop=(ic == 7)), inc="tv")
                tvv = p.val("tv")
                p.wait("vector", "tv", tvv)
                bank_set(bk, p.op("vector", lambda e, oc=oc, tc=tc, bk=bk: e.tensor_tensor(
                    out=xr(oc)[:, tc * 512: tc * 512 + 512], in0=psum[bk][:, 0:512],
                    in1=xr(oc)[:, tc * 512: tc * 512 + 512], op=ALU.add), inc="vv"))
        ln_epilogue()

    def l2_ffn():
        p.join()
        mv = p.dma("sync", colf[:, 44:52], I["l2f_mur"])
        mv = p.dma("sync", colf[:, 52:60], I["l2f_muk"], cont=True)
        p.wait("vector", mv)
        xrb = lambda ic: BFA[:, ic * S:(ic + 1) * S]
        xkb = lambda ic: BFA[:, 16384 + ic * S: 16384 + (ic + 1) * S]
        for dc in range(8):
            p.drain("gpsimd")
            p.wait("gpsimd", "vv", p.val("vv"))
            p.op("gpsimd", lambda e: e.memset(AF32[:, 0:1], 0.0), inc="gv")
            p.wait("vector", "gv", p.val("gv"))
            p.op("vector", lambda e, dc=dc: e.tensor_copy(out=AF32[:, 1:2048],
                                                          in_=xr(dc)[:, 0:S - 1]), inc="vv")
            p.drain("vector")
            p.op("vector", lambda e, dc=dc: e.tensor_tensor(
                out=Dbf[:, 0:2048], in0=xr(dc), in1=AF32[:, 0:2048], op=ALU.subtract), inc="vv")
            p.drain("vector")
            p.op("vector", lambda e, dc=dc: e.scalar_tensor_tensor(
                out=xrb(dc), in0=Dbf[:, 0:2048], scalar=colf[:, 44 + dc: 44 + dc + 1],
                in1=AF32[:, 0:2048], op0=ALU.mult, op1=ALU.add), inc="vv")
            p.op("vector", lambda e, dc=dc: e.scalar_tensor_tensor(
                out=xkb(dc), in0=Dbf[:, 0:2048], scalar=colf[:, 52 + dc: 52 + dc + 1],
                in1=AF32[:, 0:2048], op0=ALU.mult, op1=ALU.add), inc="vv")
            p.drain("vector")
        p.join()
        SG = {}
        wv = load_w_bf16(W0, "l2f_wk", 1024, 8)
        p.wait("tensor", wv)
        for oc in range(8):
            for tc in range(4):
                bk = (oc * 4 + tc) % 2
                bank_wait(bk)
                for ic in range(8):
                    p.op("tensor", lambda e, oc=oc, tc=tc, ic=ic, bk=bk: e.matmul(
                        psum[bk][:, 0:512],
                        W0[:, ic * 1024 + oc * 128: ic * 1024 + oc * 128 + 128],
                        xkb(ic)[:, tc * 512: tc * 512 + 512],
                        start=(ic == 0), stop=(ic == 7)), inc="tv")
                tvv = p.val("tv")
                p.wait("scalar", "tv", tvv)
                sl2 = (oc * 4 + tc) % 4
                p.wait("scalar", "vv", SG.get(sl2, 0))
                sv = p.op("scalar", lambda e, bk=bk, sl2=sl2: e.activation(
                    out=AF32[:, sl2 * 512: sl2 * 512 + 512], in_=psum[bk][:, 0:512],
                    func=AF.Relu), inc="sv")
                p.wait("vector", "sv", sv)
                vv = p.op("vector", lambda e, oc=oc, tc=tc, sl2=sl2: e.tensor_tensor(
                    out=BFB[:, oc * S + tc * 512: oc * S + tc * 512 + 512],
                    in0=AF32[:, sl2 * 512: sl2 * 512 + 512],
                    in1=AF32[:, sl2 * 512: sl2 * 512 + 512], op=ALU.mult), inc="vv")
                SG[sl2] = vv
                p.wait("tensor", "sv", p.val("sv"))
        p.join()
        RG = {}
        wv = load_w_bf16(W0, "l2f_wr", 1024, 8)
        p.wait("tensor", wv)
        for oc in range(8):
            slw = oc % 2
            p.wait("gpsimd", "tv", p.val("tv"))
            vv2 = None
            for ic in range(8):
                vv2 = p.dma("gpsimd", W1[:, slw * 1024 + ic * 128: slw * 1024 + ic * 128 + 128],
                            I["l2f_wv"][ic * 128:(ic + 1) * 128, oc * 128:(oc + 1) * 128],
                            cont=(ic > 0))
            for tc in range(4):
                bk = tc % 2
                bank_wait(bk)
                p.wait("tensor", "sv", RG.get(bk, 0))
                for ic in range(8):
                    p.op("tensor", lambda e, oc=oc, tc=tc, ic=ic, bk=bk: e.matmul(
                        psum[bk][:, 0:512],
                        W0[:, ic * 1024 + oc * 128: ic * 1024 + oc * 128 + 128],
                        xrb(ic)[:, tc * 512: tc * 512 + 512],
                        start=(ic == 0), stop=(ic == 7)), inc="tv")
                tvv = p.val("tv")
                p.wait("scalar", "tv", tvv)
                sv = p.op("scalar", lambda e, tc=tc, bk=bk: e.activation(
                    out=AF32[:, tc * 512: tc * 512 + 512], in_=psum[bk][:, 0:512],
                    func=AF.Sigmoid), inc="sv")
                p.wait("vector", "sv", sv)
                RG[bk] = sv
            p.wait("tensor", "sv", p.val("sv"))
            p.wait("tensor", vv2)
            for tc in range(4):
                bk = 2 + tc % 2
                bank_wait(bk)
                for ic in range(8):
                    p.op("tensor", lambda e, oc=oc, tc=tc, ic=ic, bk=bk, slw=slw: e.matmul(
                        psum[bk][:, 0:512],
                        W1[:, slw * 1024 + ic * 128: slw * 1024 + ic * 128 + 128],
                        BFB[:, ic * S + tc * 512: ic * S + tc * 512 + 512],
                        start=(ic == 0), stop=(ic == 7)), inc="tv")
                tvv = p.val("tv")
                p.wait("vector", "tv", tvv)
                p.op("vector", lambda e, tc=tc, bk=bk: e.tensor_tensor(
                    out=Dbf[:, 0:512], in0=psum[bk][:, 0:512],
                    in1=AF32[:, tc * 512: tc * 512 + 512], op=ALU.mult), inc="vv")
                p.drain("vector")
                vv = p.op("vector", lambda e, oc=oc, tc=tc, bk=bk: e.tensor_tensor(
                    out=xr(oc)[:, tc * 512: tc * 512 + 512], in0=Dbf[:, 0:512],
                    in1=xr(oc)[:, tc * 512: tc * 512 + 512], op=ALU.add), inc="vv")
                p.drain("vector")
                bank_set(bk, vv)
        ln_epilogue()

    def l3_attn():
        p.join()
        PI = float(np.float32(np.pi))
        cwv = p.dma("sync", colf[0:32, 40:41], I["l3_theta"].broadcast_to([32, 1]))
        for j in range(4):
            cwv = p.dma("sync", colf[0:32, 41 + j: 42 + j],
                        I["c_cw"][j:j + 1, :].broadcast_to([32, 1]), cont=True)
        fbv = p.dma("sync", colf[0:32, 45:46], I["c_l3_fb"], cont=True)
        tv_ = p.dma("sync", rows[0:32, 0:2048], I["c_l3t"], cont=True)
        p.wait("vector", tv_)
        p.op("vector", lambda e: e.tensor_scalar(out=colf[0:32, 46:47], in0=colf[0:32, 40:41],
                                                 scalar1=1.0 / 220.0, scalar2=None,
                                                 op0=ALU.mult), inc="vv")
        p.drain("vector")
        p.op("vector", lambda e: e.tensor_tensor(out=colf[0:32, 46:47], in0=colf[0:32, 46:47],
                                                 in1=colf[0:32, 45:46], op=ALU.mult), inc="vv")
        p.drain("vector")
        for j in range(3):
            p.op("vector", lambda e, j=j: e.tensor_scalar(
                out=colf[0:32, 47 + j: 48 + j], in0=colf[0:32, 41 + j: 42 + j],
                scalar1=-1.0, scalar2=None, op0=ALU.mult), inc="vv")
        p.drain("vector")
        p.op("vector", lambda e: e.tensor_scalar(out=rows[0:32, 0:2048], in0=rows[0:32, 0:2048],
                                                 scalar1=colf[0:32, 46:47], scalar2=None,
                                                 op0=ALU.mult), inc="vv")
        p.drain("vector")
        p.op("vector", lambda e: e.tensor_scalar(out=BFBf[0:32, 0:2048], in0=rows[0:32, 0:2048],
                                                 scalar1=colf[0:32, 44:45], scalar2=None,
                                                 op0=ALU.mult), inc="vv")
        p.drain("vector")
        p.op("vector", lambda e: e.tensor_copy(out=BFBi[0:32, 2048:4096],
                                               in_=BFBf[0:32, 0:2048]), inc="vv")
        p.drain("vector")
        p.op("vector", lambda e: e.tensor_copy(out=BFBf[0:32, 0:2048],
                                               in_=BFBi[0:32, 2048:4096]), inc="vv")
        p.drain("vector")
        for j in range(3):
            p.op("vector", lambda e, j=j: e.scalar_tensor_tensor(
                out=rows[0:32, 0:2048], in0=BFBf[0:32, 0:2048],
                scalar=colf[0:32, 47 + j: 48 + j], in1=rows[0:32, 0:2048],
                op0=ALU.mult, op1=ALU.add), inc="vv")
            p.drain("vector")
        p.op("vector", lambda e: e.tensor_scalar(out=rows[0:32, 0:2048], in0=rows[0:32, 0:2048],
                                                 scalar1=-PI, scalar2=PI,
                                                 op0=ALU.max, op1=ALU.min), inc="vv")
        vv = p.val("vv")
        p.wait("scalar", "vv", vv)
        sv = p.op("scalar", lambda e: e.activation(out=BFBf[0:32, 0:2048],
                                                   in_=rows[0:32, 0:2048],
                                                   func=AF.Sin), inc="sv")
        p.wait("vector", "sv", sv)
        p.op("vector", lambda e: e.scalar_tensor_tensor(
            out=rows[0:32, 0:2048], in0=rows[0:32, 0:2048], scalar=-1.0,
            in1=rows[0:32, 0:2048], op0=ALU.mult, op1=ALU.max), inc="vv")
        p.drain("vector")
        p.op("vector", lambda e: e.tensor_scalar(out=rows[0:32, 0:2048], in0=rows[0:32, 0:2048],
                                                 scalar1=-1.0, scalar2=float(np.float32(np.pi / 2)),
                                                 op0=ALU.mult, op1=ALU.add), inc="vv")
        p.op("vector", lambda e: e.tensor_scalar(out=BFBf[0:32, 2048:4096],
                                                 in0=BFBf[0:32, 0:2048],
                                                 scalar1=-1.0, scalar2=None,
                                                 op0=ALU.mult), inc="vv")
        vv = p.val("vv")
        p.wait("scalar", "vv", vv)
        sv = p.op("scalar", lambda e: e.activation(out=rows[0:32, 0:2048],
                                                   in_=rows[0:32, 0:2048],
                                                   func=AF.Sin), inc="sv")
        p.wait("sync", "sv", sv)
        p.wait("sync", "vv", vv)
        p.dma("sync", SCR["l3cs"], rows[0:32, 0:2048])
        p.dma("sync", SCR["l3sn"][0], BFBf[0:32, 2048:4096], cont=True)
        tb = p.dma("sync", SCR["l3sn"][1], BFBf[0:32, 0:2048], cont=True)
        p.wait("vector", tb)
        p.wait("scalar", tb)
        lw = p.dma("sync", colf[:, 0:8], I["l3_lnw"])
        lb = p.dma("sync", colf[:, 8:16], I["l3_lnb"], cont=True)
        p.wait("vector", lb)
        p.wait("scalar", lb)
        for tc in range(4):
            for dc in range(8):
                sl = dc % 2
                sq = p.op("scalar", lambda e, dc=dc, tc=tc, sl=sl: e.activation(
                    out=AF32[:, sl * 512: sl * 512 + 512],
                    in_=xr(dc)[:, tc * 512: tc * 512 + 512], func=AF.Square), inc="sv")
                p.wait("tensor", "sv", sq)
                p.op("tensor", lambda e, dc=dc, tc=tc: e.matmul(
                    psum[0][0:1, 0:512], ones32[:, 0:1],
                    xr(dc)[:, tc * 512: tc * 512 + 512],
                    start=(dc == 0), stop=(dc == 7)), inc="tv")
                tvq = p.op("tensor", lambda e, dc=dc, tc=tc, sl=sl: e.matmul(
                    psum[1][0:1, 0:512], ones32[:, 0:1],
                    AF32[:, sl * 512: sl * 512 + 512],
                    start=(dc == 0), stop=(dc == 7)), inc="tv")
                p.wait("scalar", "tv", tvq)
            tvv = p.val("tv")
            p.wait("vector", "tv", tvv)
            p.op("vector", lambda e, tc=tc: e.tensor_scalar(
                out=BFBf[0:1, tc * 512:tc * 512 + 512], in0=psum[0][0:1, 0:512],
                scalar1=1.0 / 1024.0, scalar2=None, op0=ALU.mult), inc="vv")
            p.op("vector", lambda e, tc=tc: e.tensor_scalar(
                out=BFBf[0:1, 2048 + tc * 512: 2048 + tc * 512 + 512], in0=psum[1][0:1, 0:512],
                scalar1=1.0 / 1024.0, scalar2=None, op0=ALU.mult), inc="vv")
            vv = p.val("vv")
            p.wait("tensor", "vv", vv)
        p.drain("vector")
        MEAN = BFBf[0:1, 0:S]; M2 = BFBf[0:1, 2048:2048 + S]
        AV_ = BFBf[0:1, 4096:4096 + S]; BV_ = BFBf[0:1, 6144:6144 + S]
        p.op("vector", lambda e: e.scalar_tensor_tensor(
            out=AV_, in0=MEAN, scalar=-1.0, in1=MEAN, op0=ALU.mult, op1=ALU.mult))
        p.drain("vector")
        p.op("vector", lambda e: e.tensor_tensor(out=AV_, in0=AV_, in1=M2, op=ALU.add), inc="vv")
        p.drain("vector")
        p.wait("scalar", "vv", p.val("vv"))
        p.op("scalar", lambda e: e.activation(out=AV_, in_=AV_, func=AF.Sqrt,
                                              bias=colf[0:1, 62:63]), inc="sv")
        p.wait("vector", "sv", p.val("sv"))
        p.op("vector", lambda e: e.reciprocal(out=AV_, in_=AV_))
        p.drain("vector")
        p.op("vector", lambda e: e.scalar_tensor_tensor(
            out=BV_, in0=MEAN, scalar=-1.0, in1=AV_, op0=ALU.mult, op1=ALU.mult), inc="vv")
        vv = p.val("vv")
        p.wait("sync", "vv", vv)
        p.dma("sync", SCR["rows8"][0:1, :], AV_)
        wv = p.dma("sync", SCR["rows8"][1:2, :], BV_, cont=True)
        p.wait("sync", wv)
        p.dma("sync", BFBf[:, 0:2048], SCR["rows8"][0:1, :].broadcast_to([128, S]))
        vb = p.dma("sync", BFBf[:, 2048:4096], SCR["rows8"][1:2, :].broadcast_to([128, S]),
                   cont=True)
        p.wait("vector", vb)
        for dc in range(8):
            p.op("vector", lambda e, dc=dc: e.scalar_tensor_tensor(
                out=AF32[:, 0:2048], in0=xr(dc), scalar=colf[:, dc: dc + 1],
                in1=BFBf[:, 0:2048], op0=ALU.mult, op1=ALU.mult), inc="vv")
            p.drain("vector")
            p.op("vector", lambda e, dc=dc: e.scalar_tensor_tensor(
                out=AF32[:, 0:2048], in0=BFBf[:, 2048:4096], scalar=colf[:, dc: dc + 1],
                in1=AF32[:, 0:2048], op0=ALU.mult, op1=ALU.add), inc="vv")
            p.drain("vector")
            p.op("vector", lambda e, dc=dc: e.tensor_scalar(
                out=BFA[:, dc * S:(dc + 1) * S], in0=AF32[:, 0:2048],
                scalar1=colf[:, 8 + dc: 8 + dc + 1], scalar2=None, op0=ALU.add), inc="vv")
            p.drain("vector")
        p.join()
        hb = lambda ic: BFA[:, ic * S:(ic + 1) * S]
        pv = p.dma("sync", sstat[:, 0:128], I["c_perm_pair"])
        p.wait("tensor", pv)
        def proj_rope(wname, coloff, dstf):
            p.wait("gpsimd", "tv", p.val("tv"))
            wv = None
            for ic in range(8):
                wv = p.dma("gpsimd", W0[:, ic * 1024:(ic + 1) * 1024],
                           I[wname][ic * 128:(ic + 1) * 128, coloff:coloff + 1024],
                           cont=(ic > 0))
            p.wait("tensor", wv)
            for oc in range(8):
                for tc in range(4):
                    bk = tc % 2
                    bank_wait(bk)
                    for ic in range(8):
                        p.op("tensor", lambda e, oc=oc, tc=tc, ic=ic, bk=bk: e.matmul(
                            psum[bk][:, 0:512],
                            W0[:, ic * 1024 + oc * 128: ic * 1024 + oc * 128 + 128],
                            hb(ic)[:, tc * 512: tc * 512 + 512],
                            start=(ic == 0), stop=(ic == 7)), inc="tv")
                    tvv = p.val("tv")
                    p.wait("vector", "tv", tvv)
                    bank_set(bk, p.op("vector", lambda e, tc=tc, bk=bk: e.tensor_copy(
                        out=AF32[:, tc * 512: tc * 512 + 512], in_=psum[bk][:, 0:512]), inc="vv"))
                vvq = p.val("vv")
                for tc in range(4):
                    p.wait("sync", "vv", p.val("vv"))
                    cs1 = p.dma("sync", Dbf[0:64, 1024:1536],
                                bass.AP(SCR["l3cs"].tensor, SCR["l3cs"].offset + tc * 512,
                                        [[2048, 32], [0, 2], [1, 512]]))
                    cs2 = p.dma("sync", Dbf[64:128, 1024:1536],
                                bass.AP(SCR["l3cs"].tensor, SCR["l3cs"].offset + tc * 512,
                                        [[2048, 32], [0, 2], [1, 512]]), cont=True)
                    sn1 = p.dma("sync", Dbf[0:64, 1536:2048],
                                bass.AP(SCR["l3sn"].tensor, SCR["l3sn"].offset + tc * 512,
                                        [[2048, 32], [65536, 2], [1, 512]]), cont=True)
                    sn2 = p.dma("sync", Dbf[64:128, 1536:2048],
                                bass.AP(SCR["l3sn"].tensor, SCR["l3sn"].offset + tc * 512,
                                        [[2048, 32], [65536, 2], [1, 512]]), cont=True)
                    p.wait("tensor", "vv", vvq)
                    p.wait("tensor", sn2)
                    tvp = p.op("tensor", lambda e, tc=tc: e.matmul(
                        psum[2][:, 0:512], sstat[:, 0:128],
                        AF32[:, tc * 512: tc * 512 + 512], start=True, stop=True), inc="tv")
                    p.wait("vector", sn2)
                    p.wait("vector", "tv", tvp)
                    p.op("vector", lambda e, tc=tc: e.tensor_tensor(
                        out=Dbf[:, 0:512], in0=AF32[:, tc * 512: tc * 512 + 512],
                        in1=Dbf[:, 1024:1536], op=ALU.mult), inc="vv")
                    p.op("vector", lambda e: e.tensor_tensor(
                        out=Dbf[:, 512:1024], in0=psum[2][:, 0:512],
                        in1=Dbf[:, 1536:2048], op=ALU.mult), inc="vv")
                    p.drain("vector")
                    vvr = p.op("vector", lambda e, oc=oc, tc=tc: e.tensor_tensor(
                        out=dstf(oc)[:, tc * 512: tc * 512 + 512], in0=Dbf[:, 0:512],
                        in1=Dbf[:, 512:1024], op=ALU.add), inc="vv")
                    p.drain("vector")
                    p.wait("tensor", "vv", vvr)
        qTb3 = lambda oc: BFA[:, 16384 + oc * S: 16384 + (oc + 1) * S]
        kTb3 = lambda oc: BFB[:, oc * S:(oc + 1) * S]
        if KDUMP == "l3h":
            dump_bf(hb)
        if KDUMP == "l3tab":
            p.join()
            dv = p.dma("sync", dbg[0:32, :], SCR["l3cs"])
            dv = p.dma("sync", dbg[32:64, :], SCR["l3sn"][1], cont=True)
            dv = p.dma("sync", dbg[64:96, :], SCR["l3sn"][0], cont=True)
            p.wait("sync", dv)
        proj_rope("l3_wq", 0, qTb3)
        if KDUMP == "l3q":
            dump_bf(qTb3)
        proj_rope("l3_wkv", 0, kTb3)
        if KDUMP == "l3k":
            dump_bf(kTb3)
        for dblk in range(2):
            p.wait("gpsimd", "tv", p.val("tv"))
            wv = None
            for ic in range(8):
                wv = p.dma("gpsimd", W0[:, ic * 512:(ic + 1) * 512],
                           I["l3_wkv"][ic * 128:(ic + 1) * 128,
                                       1024 + dblk * 512: 1024 + (dblk + 1) * 512],
                           cont=(ic > 0))
            p.wait("tensor", wv)
            for tt in range(16):
                bk = tt % 2
                bank_wait(bk)
                for ic in range(8):
                    p.op("tensor", lambda e, ic=ic, tt=tt, bk=bk: e.matmul(
                        psum[bk][:, 0:512], hb(ic)[:, tt * 128:(tt + 1) * 128],
                        W0[:, ic * 512:(ic + 1) * 512],
                        start=(ic == 0), stop=(ic == 7)), inc="tv")
                tvv = p.val("tv")
                p.wait("vector", "tv", tvv)
                vv = p.op("vector", lambda e, tt=tt, bk=bk: e.tensor_copy(
                    out=Db[:, bk * 512: bk * 512 + 512], in_=psum[bk][:, 0:512]), inc="vv")
                p.wait("sync", "vv", vv)
                dv = p.dma("sync", SCR["vtok"][tt][:, dblk * 512:(dblk + 1) * 512],
                           Db[:, bk * 512: bk * 512 + 512])
                p.wait("vector", dv)
                bank_set(bk, vv)
        p.join()
        return hb
    def l3_attn2(hb):
        qTb3 = lambda oc: BFA[:, 16384 + oc * S: 16384 + (oc + 1) * S]
        kTb3 = lambda oc: BFB[:, oc * S:(oc + 1) * S]
        p.op("gpsimd", lambda e: e.memset(colf[:, 56:57], float(np.float32(1.0 / np.sqrt(2.0)))),
             inc="gv")
        p.wait("scalar", "gv", p.val("gv"))
        for dblk in range(2):
            for tt in range(16):
                vl = p.dma("sync", Db[:, 0:512] if tt % 2 == 0 else Db[:, 512:1024],
                           SCR["vtok"][tt][:, dblk * 512:(dblk + 1) * 512])
                p.wait("tensor", vl)
                p.op("tensor", lambda e, tt=tt: e.matmul(
                    psum[3][0:1, 0:512], onesb[:, 0:1],
                    Db[:, 0:512] if tt % 2 == 0 else Db[:, 512:1024],
                    start=(tt == 0), stop=(tt == 15)), inc="tv")
                p.wait("sync", "tv", p.val("tv"))
            tvv = p.val("tv")
            p.wait("vector", "tv", tvv)
            vv = p.op("vector", lambda e, dblk=dblk: e.tensor_scalar(
                out=rows[0:1, dblk * 512: dblk * 512 + 512], in0=psum[3][0:1, 0:512],
                scalar1=0.5, scalar2=None, op0=ALU.mult), inc="vv")
            p.wait("tensor", "vv", vv)
        vv = p.val("vv")
        p.wait("sync", "vv", vv)
        vs = p.dma("sync", SCR["rows8"][4:5, 0:1024], rows[0:1, 0:1024])
        p.wait("sync", vs)
        vh = None
        for c in range(8):
            vh = p.dma("sync", colf[:, 48 + c: 48 + c + 1],
                       SCR["rows8"][4:5, c * 128:(c + 1) * 128], cont=(c > 0))
        p.wait("vector", vh)
        p.wait("scalar", vh)
        p.wait("gpsimd", "vv", p.val("vv"))
        VG = {}
        for h in range(16):
            cc, hh = h // 2, h % 2
            slv = h % 2
            p.wait("gpsimd", "tv", VG.get(slv, 0))
            vl = None
            for tt in range(16):
                vl = p.dma("gpsimd", W1[:, slv * 1024 + tt * 64: slv * 1024 + (tt + 1) * 64],
                           SCR["vtok"][tt][:, h * 64:(h + 1) * 64], cont=(tt > 0))
            for qc in range(4):
                p.wait("scalar", "tv", p.val("tv"))
                for jc in range(16):
                    bk = jc % 2
                    tvv = p.op("tensor", lambda e, cc=cc, hh=hh, qc=qc, jc=jc, bk=bk: e.matmul(
                        psum[bk][:, 0:512],
                        kTb3(cc)[hh * 64: hh * 64 + 64, jc * 128:(jc + 1) * 128],
                        qTb3(cc)[hh * 64: hh * 64 + 64, qc * 512:(qc + 1) * 512],
                        start=True, stop=True), inc="tv")
                    p.wait("scalar", "tv", tvv)
                    sv = p.op("scalar", lambda e, jc=jc, bk=bk: e.activation(
                        out=W0[:, jc * 512:(jc + 1) * 512], in_=psum[bk][:, 0:512],
                        func=AF.Square, scale=float(np.float32(0.125 / np.sqrt(2.0))),
                        bias=colf[:, 56:57]), inc="sv")
                    p.wait("tensor", "sv", sv)
                p.wait("tensor", vl)
                for jc in range(16):
                    p.op("tensor", lambda e, jc=jc: e.matmul(
                        psum[3][0:1, 0:512], onesb[:, 0:1], W0[:, jc * 512:(jc + 1) * 512],
                        start=(jc == 0), stop=(jc == 15)), inc="tv")
                    p.op("tensor", lambda e, h=h, jc=jc, hh=hh, slv=slv: e.matmul(
                        psum[2][hh * 64: hh * 64 + 64, 0:512],
                        W1[:, slv * 1024 + jc * 64: slv * 1024 + (jc + 1) * 64],
                        W0[:, jc * 512:(jc + 1) * 512],
                        start=(jc == 0), stop=(jc == 15)), inc="tv")
                tvv = p.val("tv")
                VG[slv] = tvv
                if KDUMP == "l3tp" and h == 0 and qc == 0:
                    p.join()
                    for cpart in range(4):
                        p.op("vector", lambda e, cpart=cpart: e.tensor_copy(
                            out=AF32[:, 0:2048], in_=W0[:, cpart * 2048:(cpart + 1) * 2048]),
                            inc="vv")
                        vvd = p.val("vv")
                        p.wait("sync", "vv", vvd)
                        dd = p.dma("sync", dbg[cpart * 128:(cpart + 1) * 128, :], AF32[:, 0:2048])
                        p.wait("vector", dd)
                    p.join()
                p.wait("vector", "tv", tvv)
                p.op("vector", lambda e: e.tensor_scalar(
                    out=rows[0:1, 0:512], in0=psum[3][0:1, 0:512],
                    scalar1=1024.0, scalar2=None, op0=ALU.add), inc="vv")
                p.drain("vector")
                p.op("vector", lambda e: e.reciprocal(out=rows[0:1, 0:512],
                                                      in_=rows[0:1, 0:512]), inc="vv")
                p.drain("vector")
                vvr = p.op("vector", lambda e, h=h, hh=hh, cc=cc, qc=qc: e.tensor_scalar(
                    out=qTb3(cc)[hh * 64: hh * 64 + 64, qc * 512:(qc + 1) * 512],
                    in0=psum[2][hh * 64: hh * 64 + 64, 0:512],
                    scalar1=colf[hh * 64: hh * 64 + 64, 48 + cc: 48 + cc + 1], scalar2=None,
                    op0=ALU.add), inc="vv")
                p.wait("sync", "vv", vvr)
                dn = p.dma("sync", SCR["rows16"][h:h + 1, qc * 512:(qc + 1) * 512],
                           rows[0:1, 0:512])
                p.wait("vector", dn)
                p.wait("tensor", "vv", vvr)
        p.join()
        if KDUMP == "l3d":
            dv = p.dma("sync", dbg[0:16, :], SCR["rows16"])
            p.wait("sync", dv)
        for c in range(8):
            p.wait("sync", "vv", p.val("vv"))
            rb = p.dma("sync", Dbf[:, 0:2048],
                       SCR["rows16"][2 * c: 2 * c + 2, :].unsqueeze(1).broadcast_to([2, 64, S]))
            p.wait("vector", rb)
            p.op("vector", lambda e, c=c: e.tensor_tensor(
                out=BFA[:, c * S:(c + 1) * S], in0=BFA[:, 16384 + c * S: 16384 + (c + 1) * S],
                in1=Dbf[:, 0:2048], op=ALU.mult), inc="vv")
            p.drain("vector")
        p.join()
        if KDUMP == "l3o":
            dump_bf(lambda c: BFA[:, c * S:(c + 1) * S])
        wv = load_w_bf16(W0, "l3_wo", 1024, 8)
        p.wait("tensor", wv)
        for oc in range(8):
            for tc in range(4):
                bk = (oc * 4 + tc) % 2
                bank_wait(bk)
                for ic in range(8):
                    p.op("tensor", lambda e, oc=oc, tc=tc, ic=ic, bk=bk: e.matmul(
                        psum[bk][:, 0:512],
                        W0[:, ic * 1024 + oc * 128: ic * 1024 + oc * 128 + 128],
                        BFA[:, ic * S + tc * 512: ic * S + tc * 512 + 512],
                        start=(ic == 0), stop=(ic == 7)), inc="tv")
                tvv = p.val("tv")
                p.wait("vector", "tv", tvv)
                bank_set(bk, p.op("vector", lambda e, oc=oc, tc=tc, bk=bk: e.tensor_tensor(
                    out=xr(oc)[:, tc * 512: tc * 512 + 512], in0=psum[bk][:, 0:512],
                    in1=xr(oc)[:, tc * 512: tc * 512 + 512], op=ALU.add), inc="vv"))
        ln_epilogue()

    def l3_attn_all():
        hb = l3_attn()
        l3_attn2(hb)

    def l3_ffn():
        p.join()
        cast_xb()
        gw = None
        for ic in range(8):
            gw = p.dma("sync", sstat[:, ic * 8:(ic + 1) * 8],
                       I["l3f_gw"][ic * 128:(ic + 1) * 128, :], cont=(ic > 0))
        p.wait("tensor", gw)
        for tc in range(4):
            for ic in range(8):
                p.op("tensor", lambda e, tc=tc, ic=ic: e.matmul(
                    psum[2][0:8, 0:512], sstat[:, ic * 8:(ic + 1) * 8],
                    xr(ic)[:, tc * 512: tc * 512 + 512],
                    start=(ic == 0), stop=(ic == 7)), inc="tv")
            tvv = p.val("tv")
            p.wait("vector", "tv", tvv)
            vv = p.op("vector", lambda e, tc=tc: e.tensor_copy(
                out=rows[0:8, tc * 512: tc * 512 + 512], in_=psum[2][0:8, 0:512]), inc="vv")
            p.wait("tensor", "vv", vv)
        for tt in range(16):
            bk = tt % 2
            p.wait("tensor", "vv", BG.get(("l3f", bk), 0))
            tvv = p.op("tensor", lambda e, tt=tt, bk=bk: e.matmul(
                psum[bk][:, 0:8], rows[0:8, tt * 128:(tt + 1) * 128],
                id32[0:8, 0:8], start=True, stop=True), inc="tv")
            p.wait("vector", "tv", tvv)
            vv = p.op("vector", lambda e, tt=tt, bk=bk: e.tensor_copy(
                out=AF32[:, tt * 8:(tt + 1) * 8], in_=psum[bk][:, 0:8]), inc="vv")
            BG[("l3f", bk)] = vv
        FMAX = 3.0e38
        for tt in range(16):
            lt = AF32[:, tt * 8:(tt + 1) * 8]
            cf = AF32[:, 128 + tt * 8: 128 + (tt + 1) * 8]
            p.op("vector", lambda e, lt=lt: e.tensor_reduce(
                out=colf[:, 40:41], in_=lt, axis=AX.X, op=ALU.max), inc="vv")
            p.drain("vector")
            p.op("vector", lambda e, lt=lt: e.tensor_scalar(
                out=AF32[:, 256:264], in0=lt, scalar1=colf[:, 40:41], scalar2=None,
                op0=ALU.is_equal), inc="vv")
            p.drain("vector")
            p.op("vector", lambda e, lt=lt: e.scalar_tensor_tensor(
                out=AF32[:, 264:272], in0=AF32[:, 256:264], scalar=-FMAX,
                in1=lt, op0=ALU.mult, op1=ALU.add), inc="vv")
            p.drain("vector")
            p.op("vector", lambda e: e.tensor_reduce(
                out=colf[:, 41:42], in_=AF32[:, 264:272], axis=AX.X, op=ALU.max), inc="vv")
            p.drain("vector")
            p.op("vector", lambda e, lt=lt: e.tensor_scalar(
                out=AF32[:, 264:272], in0=lt, scalar1=colf[:, 41:42], scalar2=None,
                op0=ALU.is_equal), inc="vv")
            p.drain("vector")
            p.op("vector", lambda e: e.tensor_tensor(
                out=colf[:, 42:43], in0=colf[:, 40:41], in1=colf[:, 41:42],
                op=ALU.subtract), inc="vv")
            vv = p.val("vv")
            p.wait("scalar", "vv", vv)
            sv = p.op("scalar", lambda e: e.activation(out=colf[:, 42:43], in_=colf[:, 42:43],
                                                       func=AF.Sigmoid), inc="sv")
            p.wait("vector", "sv", sv)
            p.op("vector", lambda e: e.tensor_scalar(
                out=colf[:, 43:44], in0=colf[:, 42:43], scalar1=-1.0, scalar2=1.0,
                op0=ALU.mult, op1=ALU.add), inc="vv")
            p.drain("vector")
            p.op("vector", lambda e: e.tensor_scalar(
                out=AF32[:, 256:264], in0=AF32[:, 256:264], scalar1=colf[:, 42:43],
                scalar2=None, op0=ALU.mult), inc="vv")
            p.op("vector", lambda e: e.tensor_scalar(
                out=AF32[:, 264:272], in0=AF32[:, 264:272], scalar1=colf[:, 43:44],
                scalar2=None, op0=ALU.mult), inc="vv")
            p.drain("vector")
            p.op("vector", lambda e, cf=cf: e.tensor_tensor(
                out=cf, in0=AF32[:, 256:264], in1=AF32[:, 264:272], op=ALU.add), inc="vv")
            p.drain("vector")
        for tt in range(16):
            bk = tt % 2
            p.wait("tensor", "vv", p.val("vv") if tt == 0 else BG.get(("l3fc", bk), 0))
            tvv = p.op("tensor", lambda e, tt=tt, bk=bk: e.matmul(
                psum[bk][0:8, 0:128], AF32[:, 128 + tt * 8: 128 + (tt + 1) * 8],
                id32[:, 0:128], start=True, stop=True), inc="tv")
            p.wait("vector", "tv", tvv)
            vv = p.op("vector", lambda e, tt=tt, bk=bk: e.tensor_copy(
                out=rows[0:8, tt * 128:(tt + 1) * 128], in_=psum[bk][0:8, 0:128]), inc="vv")
            BG[("l3fc", bk)] = vv
        vv = p.val("vv")
        p.wait("sync", "vv", vv)
        cw = p.dma("sync", SCR["rows8"], rows[0:8, 0:2048])
        p.wait("sync", cw)
        for ee in range(8):
            p.wait("gpsimd", "tv", p.val("tv"))
            wv = None
            for ic in range(8):
                wv = p.dma("gpsimd", W0[:, ic * 256:(ic + 1) * 256],
                           I["l3f_wu"][ic * 128:(ic + 1) * 128, ee * 256:(ee + 1) * 256],
                           cont=(ic > 0))
                wv = p.dma("gpsimd", W0[:, 2048 + ic * 256: 2048 + (ic + 1) * 256],
                           I["l3f_wv"][ic * 128:(ic + 1) * 128, ee * 256:(ee + 1) * 256],
                           cont=True)
            for ic2 in range(2):
                wv = p.dma("gpsimd", W0[:, 4096 + ic2 * 1024: 4096 + (ic2 + 1) * 1024],
                           I["l3f_wo"][ee][ic2 * 128:(ic2 + 1) * 128, :], cont=True)
            p.wait("sync", "vv", p.val("vv"))
            cr = p.dma("sync", Dbf[:, 0:2048],
                       SCR["rows8"][ee:ee + 1, :].broadcast_to([128, S]))
            p.wait("tensor", wv)
            p.wait("vector", cr)
            for oc2 in range(2):
                for tc in range(4):
                    for ic in range(8):
                        p.op("tensor", lambda e, oc2=oc2, tc=tc, ic=ic: e.matmul(
                            psum[0][:, 0:512],
                            W0[:, ic * 256 + oc2 * 128: ic * 256 + oc2 * 128 + 128],
                            xb(ic)[:, tc * 512: tc * 512 + 512],
                            start=(ic == 0), stop=(ic == 7)), inc="tv")
                        p.op("tensor", lambda e, oc2=oc2, tc=tc, ic=ic: e.matmul(
                            psum[1][:, 0:512],
                            W0[:, 2048 + ic * 256 + oc2 * 128: 2048 + ic * 256 + oc2 * 128 + 128],
                            xb(ic)[:, tc * 512: tc * 512 + 512],
                            start=(ic == 0), stop=(ic == 7)), inc="tv")
                    tvv = p.val("tv")
                    p.wait("scalar", "tv", tvv)
                    sv = p.op("scalar", lambda e: e.activation(
                        out=AF32[:, 512:1024], in_=psum[0][:, 0:512],
                        func=AF.Sigmoid), inc="sv")
                    p.wait("vector", "sv", sv)
                    p.op("vector", lambda e: e.tensor_tensor(
                        out=AF32[:, 1024:1536], in0=psum[0][:, 0:512],
                        in1=AF32[:, 512:1024], op=ALU.mult), inc="vv")
                    p.drain("vector")
                    p.op("vector", lambda e: e.tensor_tensor(
                        out=AF32[:, 1024:1536], in0=AF32[:, 1024:1536],
                        in1=psum[1][:, 0:512], op=ALU.mult), inc="vv")
                    p.drain("vector")
                    vv = p.op("vector", lambda e, oc2=oc2, tc=tc: e.tensor_tensor(
                        out=BFB[:, oc2 * S + tc * 512: oc2 * S + tc * 512 + 512],
                        in0=AF32[:, 1024:1536], in1=Dbf[:, tc * 512: tc * 512 + 512],
                        op=ALU.mult), inc="vv")
                    p.drain("vector")
                    p.wait("tensor", "vv", vv)
                    p.wait("scalar", "vv", vv)
            for oc in range(8):
                for tc in range(4):
                    bk = 2 + (oc * 4 + tc) % 2
                    bank_wait(bk)
                    for ic2 in range(2):
                        p.op("tensor", lambda e, oc=oc, tc=tc, ic2=ic2, bk=bk: e.matmul(
                            psum[bk][:, 0:512],
                            W0[:, 4096 + ic2 * 1024 + oc * 128: 4096 + ic2 * 1024 + oc * 128 + 128],
                            BFB[:, ic2 * S + tc * 512: ic2 * S + tc * 512 + 512],
                            start=(ic2 == 0), stop=(ic2 == 1)), inc="tv")
                    tvv = p.val("tv")
                    p.wait("vector", "tv", tvv)
                    bank_set(bk, p.op("vector", lambda e, oc=oc, tc=tc, bk=bk: e.tensor_tensor(
                        out=xr(oc)[:, tc * 512: tc * 512 + 512], in0=psum[bk][:, 0:512],
                        in1=xr(oc)[:, tc * 512: tc * 512 + 512], op=ALU.add), inc="vv"))
            p.join()
        ln_epilogue()

    def store_out():
        p.join()
        for tc in range(16):
            for dc in range(8):
                bk = dc % 4
                bank_wait(bk)
                tv = p.op("tensor", lambda e, tc=tc, dc=dc, bk=bk: e.transpose(
                    psum[bk][:, 0:128], xr(dc)[:, tc * 128: tc * 128 + 128], id32[:]), inc="tv")
                p.wait("vector", "tv", tv)
                bank_set(bk, p.op("vector", lambda e, tc=tc, dc=dc, bk=bk: e.tensor_copy(
                    out=AF32[:, dc * 128: dc * 128 + 128], in_=psum[bk][:, 0:128]), inc="vv"))
            vv = p.val("vv")
            p.wait("sync", "vv", vv)
            sv = p.dma("sync", out_x[tc * 128:(tc + 1) * 128, :], AF32[:, 0:1024])
            p.wait("vector", sv)
        p.dma_barrier("sync", "sync")
        p.dma_barrier("gpsimd", "gpsimd")

    blocks = [l0_attn, l0_ffn, l1_attn, l1_ffn, l2_attn, l2_ffn, l3_attn_all, l3_ffn]
    for i, blk in enumerate(blocks[:NBLK]):
        p.scope(f"B{i}_{blk.__name__}")
        blk()
    p.scope("store")
    store_out()
    p.run()
    ctx.close()
    return nc


def kernel(**inputs):
    consts = host_constants()
    L = host_layout(inputs)
    nc = build(consts)
    x = np.ascontiguousarray(inputs["L0_attn_x"], dtype=np.float32)
    in_maps = []
    for b in range(8):
        m = {"x_in": np.ascontiguousarray(x[b])}
        for nm in WSHAPES:
            m[nm] = L[nm]
        for nm, arr in consts.items():
            m["c_" + nm] = arr
        in_maps.append(m)
    res = run_bass_kernel_spmd(nc, in_maps, core_ids=list(range(8)))
    out = np.stack([np.asarray(res.results[b]["out"]) for b in range(8)], 0)
    return out.astype(np.float32)


# version 06 (final)
# speedup vs baseline: 1.0232x; 1.0232x over previous
import os
import numpy as np
from contextlib import ExitStack
import concourse.bass as bass
import concourse.mybir as mybir
from concourse.bass_utils import run_bass_kernel_spmd
from concourse.bass_isa import ReduceOp
import concourse.library_config as _libcfg

F32 = mybir.dt.float32
BF16 = mybir.dt.bfloat16
I32 = mybir.dt.int32
AF = mybir.ActivationFunctionType
ALU = mybir.AluOpType
AX = mybir.AxisListType

S = 2048
D = 1024
NBLK = int(os.environ.get("K_TAP", "8"))
KDUMP = os.environ.get("K_DUMP", "")


def host_constants():
    c = {}
    c["ident"] = np.eye(128, dtype=np.float32)
    c["ones"] = np.ones((128, 128), np.float32)
    c["onescol"] = np.ones((128, 1), np.float32)
    ho = np.zeros((8, 128, 16), np.float32)
    for cc in range(8):
        ho[cc, 0:64, 2 * cc] = 1.0
        ho[cc, 64:128, 2 * cc + 1] = 1.0
    c["hones16"] = ho
    dim_head, window, n = 64, 512, S
    half = np.arange(0, dim_head, 2, dtype=np.float64)
    inv_freq = 1.0 / 10000.0 ** (half / dim_head)
    t = np.arange(n, dtype=np.float64)
    fr = np.concatenate([t[:, None] * inv_freq] * 2, -1)
    base = (half + 0.4 * dim_head) / (1.4 * dim_head)
    power = (t - n // 2) / float(window // 2)
    sc = np.concatenate([base[None, :] ** power[:, None]] * 2, -1)
    sgn = np.concatenate([-np.ones(32), np.ones(32)])
    for nm, tab in [("xpos_qc", (np.cos(fr) * sc).T), ("xpos_qs", (np.sin(fr) * sc).T * sgn[:, None]),
                    ("xpos_kc", (np.cos(fr) / sc).T), ("xpos_ks", (np.sin(fr) / sc).T * sgn[:, None])]:
        c[nm] = np.tile(tab, (2, 1)).astype(np.float32)
    p64 = np.zeros((64, 64), np.float32)
    for dd in range(64):
        p64[(dd + 32) % 64, dd] = 1.0
    z6 = np.zeros((64, 64), np.float32)
    c["perm_half"] = np.block([[p64, z6], [z6, p64]]).astype(np.float32)
    pp = np.zeros((128, 128), np.float32)
    for dd in range(128):
        pp[dd ^ 1, dd] = 1.0
    c["perm_pair"] = pp
    kj = np.arange(1024)[:, None]
    qi = np.arange(512)[None, :] + 512
    c["lmask"] = ((qi >= kj) & (qi <= kj + 512)).astype(np.float32)
    top = np.float32(2595.0) * np.log10(np.float32(1.0 + 4000.0 / 200.0))
    lin = (np.linspace(0.0, top, 32, dtype=np.float32) / np.float32(2595.0)).astype(np.float32)
    mel = np.power(np.float32(10.0), lin) - np.float32(1.0)
    c["l3_fb"] = ((np.float32(200.0) * mel / np.float32(1000.0)).astype(np.float32).reshape(32, 1))
    twopi = 2.0 * np.pi
    cw1 = np.float32(np.ldexp(np.floor(np.ldexp(twopi, 9)), -9))
    cw2 = np.float32(np.ldexp(np.floor(np.ldexp(twopi - np.float64(cw1), 21)), -21))
    cw3 = np.float32(twopi - np.float64(cw1) - np.float64(cw2))
    c["cw"] = np.array([[cw1], [cw2], [cw3], [np.float32(1.0 / twopi)]], np.float32)
    yst = np.zeros((8, 128, 16), np.float32)
    for j in range(8):
        for e in range(64):
            yst[j, 2 * e, 2 * j] = 1.0
            yst[j, 2 * e + 1, 2 * j + 1] = 1.0
    c["yred_st"] = yst
    yr2 = np.zeros((32, 128, 128), np.float32)
    for dh in range(32):
        for dp in range(2):
            yr2[dh, 2 * np.arange(64) + dp, 2 * dh + dp] = 1.0
    c["yred2"] = yr2
    c["l3t"] = np.tile(np.arange(2048, dtype=np.float32)[None, :], (32, 1))
    dp = np.zeros((2, 128), np.float32)
    dp[0, 0::2] = 1.0
    dp[1, 1::2] = 1.0
    c["dppat"] = dp
    return c


def host_layout(inp):
    L = {}
    f32 = lambda a: np.ascontiguousarray(a, dtype=np.float32)
    L["l0_wq"], L["l0_wk"], L["l0_wv"], L["l0_wo"] = (f32(inp["L0_attn_Wq"]), f32(inp["L0_attn_Wk"]),
                                                      f32(inp["L0_attn_Wv"]), f32(inp["L0_attn_Wo"]))
    for nm in "qkvo":
        L["l0_b" + nm] = f32(inp["L0_attn_b" + nm]).reshape(8, 128, 1)
    L["l0_bv_row"] = f32(inp["L0_attn_bv"]).reshape(1, D)
    L["l0_reltab"] = f32(inp["L0_attn_rel_table"].T)
    L["l0f_wg"] = f32(inp["L0_ffn_w_gate"].T)
    L["l0f_wu"] = f32(inp["L0_ffn_w_up"].T)
    L["l0f_wd"] = f32(inp["L0_ffn_w_down"].T)
    L["l1_wq"], L["l1_wk"], L["l1_wv"] = f32(inp["L1_attn_Wq"]), f32(inp["L1_attn_Wk"]), f32(inp["L1_attn_Wv"])
    for nm in "qkv":
        L["l1_b" + nm] = f32(inp["L1_attn_b" + nm]).reshape(8, 128, 1)
    L["l1_bv_row"] = f32(inp["L1_attn_bv"]).reshape(1, D)
    L["l1_wg"] = f32(inp["L1_attn_Wg"])
    L["l1_bg"] = f32(inp["L1_attn_bg"]).reshape(1, 1)
    L["l1f_wqkv"] = f32(inp["L1_ffn_wqkv"])
    L["l1f_wo"] = f32(inp["L1_ffn_wo"])
    HID, PAD = 2730, 2816
    w1, b1, w2 = inp["L1_ffn_w1"], inp["L1_ffn_b1"], inp["L1_ffn_w2"]
    w1p = np.zeros((2, D, 2 * PAD), np.float32)
    w1p[:, :, :HID] = w1[:, :, :HID]
    w1p[:, :, PAD:PAD + HID] = w1[:, :, HID:]
    b1p = np.zeros((2, 2 * PAD), np.float32)
    b1p[:, :HID] = b1[:, :HID]
    b1p[:, PAD:PAD + HID] = b1[:, HID:]
    w2p = np.zeros((2, PAD, D), np.float32)
    w2p[:, :HID, :] = w2
    L["l1f_w1"] = w1p
    L["l1f_b1a"] = f32(b1p[:, :PAD].reshape(2, 22, 128).transpose(0, 2, 1))
    L["l1f_b1g"] = f32(b1p[:, PAD:].reshape(2, 22, 128).transpose(0, 2, 1))
    L["l1f_w2"] = w2p
    L["l1f_b2"] = f32(inp["L1_ffn_b2"].reshape(2, 8, 128).transpose(0, 2, 1))
    L["l1f_qs"] = f32(np.tile(inp["L1_ffn_q_scale"][:, None, :], (1, 2, 1)).reshape(2, 128, 1))
    L["l1f_ks"] = f32(np.tile(inp["L1_ffn_k_scale"][:, None, :], (1, 2, 1)).reshape(2, 128, 1))
    L["l1f_qsr"] = f32(inp["L1_ffn_q_scale"].reshape(2, 1, 64))
    L["l1f_ksr"] = f32(inp["L1_ffn_k_scale"].reshape(2, 1, 64))
    L["l1f_lng"] = f32(inp["L1_ffn_ln_g"].reshape(2, 8, 128).transpose(0, 2, 1))
    L["l1f_lnb"] = f32(inp["L1_ffn_ln_b"].reshape(2, 8, 128).transpose(0, 2, 1))
    L["l2_wqkv"] = f32(inp["L2_attn_W_qkv"].T)
    L["l2_wo"] = f32(inp["L2_attn_W_o"].T)
    L["l2_wvgs"] = f32(inp["L2_attn_W_vgs"].T)
    wl = np.asarray(inp["L2_attn_W_LTM"], np.float32)
    L["l2_wlsm"] = np.ascontiguousarray(
        wl.reshape(16, 32, 2, 64).transpose(0, 3, 2, 1).reshape(16, 128, 32), np.float32)
    L["l2_wlT"] = f32(wl.transpose(0, 2, 1))
    for nin, nm in [("V_gs_base", "vgsb"), ("V_T0", "vt0"), ("beta_tau", "btau"),
                    ("beta_gm", "bgm"), ("C_ch", "cch"), ("gamma", "gam")]:
        L["l2_" + nm] = f32(inp["L2_attn_" + nin]).reshape(16, 1)
    L["l2_alpha"] = f32(inp["L2_attn_alpha_ppd"]).reshape(1, 1)
    L["l2_icth"] = f32(inp["L2_attn_IC_threshold"]).reshape(1, 1)
    L["l2f_wr"], L["l2f_wk"], L["l2f_wv"] = (f32(inp["L2_ffn_W_r"].T), f32(inp["L2_ffn_W_k"].T),
                                             f32(inp["L2_ffn_W_v"].T))
    L["l2f_mur"] = f32(inp["L2_ffn_mu_r"].reshape(8, 128).T)
    L["l2f_muk"] = f32(inp["L2_ffn_mu_k"].reshape(8, 128).T)
    L["l3_lnw"] = f32(inp["L3_attn_ln_w"].reshape(8, 128).T)
    L["l3_lnb"] = f32(inp["L3_attn_ln_b"].reshape(8, 128).T)
    L["l3_wq"], L["l3_wkv"], L["l3_wo"] = f32(inp["L3_attn_w_q"]), f32(inp["L3_attn_w_kv"]), f32(inp["L3_attn_w_o"])
    L["l3_theta"] = f32(inp["L3_attn_theta"]).reshape(1, 1)
    L["l3f_gw"] = f32(inp["L3_ffn_gate_w"])
    L["l3f_wu"] = f32(inp["L3_ffn_Wu"].transpose(1, 0, 2).reshape(D, 2048))
    L["l3f_wv"] = f32(inp["L3_ffn_Wv"].transpose(1, 0, 2).reshape(D, 2048))
    L["l3f_wo"] = f32(inp["L3_ffn_Wo"])
    return L


WSHAPES = {
    "l0_wq": (D, D), "l0_wk": (D, D), "l0_wv": (D, D), "l0_wo": (D, D),
    "l0_bq": (8, 128, 1), "l0_bk": (8, 128, 1), "l0_bv": (8, 128, 1), "l0_bo": (8, 128, 1),
    "l0_bv_row": (1, D), "l0_reltab": (16, 1023),
    "l0f_wg": (D, 4096), "l0f_wu": (D, 4096), "l0f_wd": (4096, D),
    "l1_wq": (D, D), "l1_wk": (D, D), "l1_wv": (D, D),
    "l1_bq": (8, 128, 1), "l1_bk": (8, 128, 1), "l1_bv": (8, 128, 1),
    "l1_bv_row": (1, D), "l1_wg": (D, 1), "l1_bg": (1, 1),
    "l1f_wqkv": (2, D, 1536), "l1f_wo": (2, 512, D),
    "l1f_w1": (2, D, 5632), "l1f_b1a": (2, 128, 22), "l1f_b1g": (2, 128, 22),
    "l1f_w2": (2, 2816, D), "l1f_b2": (2, 128, 8),
    "l1f_qs": (2, 128, 1), "l1f_ks": (2, 128, 1), "l1f_qsr": (2, 1, 64), "l1f_ksr": (2, 1, 64),
    "l1f_lng": (2, 128, 8), "l1f_lnb": (2, 128, 8),
    "l2_wqkv": (D, 3072), "l2_wo": (D, D), "l2_wvgs": (D, 16),
    "l2_wlsm": (16, 128, 32), "l2_wlT": (16, 64, 64),
    "l2_vgsb": (16, 1), "l2_vt0": (16, 1), "l2_btau": (16, 1), "l2_bgm": (16, 1),
    "l2_cch": (16, 1), "l2_gam": (16, 1), "l2_alpha": (1, 1), "l2_icth": (1, 1),
    "l2f_wr": (D, D), "l2f_wk": (D, D), "l2f_wv": (D, D), "l2f_mur": (128, 8), "l2f_muk": (128, 8),
    "l3_lnw": (128, 8), "l3_lnb": (128, 8),
    "l3_wq": (D, D), "l3_wkv": (D, 2 * D), "l3_wo": (D, D), "l3_theta": (1, 1),
    "l3f_gw": (D, 8), "l3f_wu": (D, 2048), "l3f_wv": (D, 2048), "l3f_wo": (8, 256, D),
}


class Prog:
    ENGS = ("sync", "vector", "scalar", "tensor", "gpsimd")

    def __init__(self, nc, ctx):
        self.nc = nc
        self.plan = {e: [] for e in self.ENGS}
        self.s = {}
        for name in ("tv", "vv", "sv", "gv"):
            self.s[name] = [ctx.enter_context(nc.semaphore(name)), 0]
        self.NPOOL = {"sync": 30, "gpsimd": 16}
        self.cur = {"sync": 0, "gpsimd": 0}
        for q, n in self.NPOOL.items():
            for i in range(n):
                nm = f"d{q[0]}{i}"
                self.s[nm] = [ctx.enter_context(nc.semaphore(nm)), 0]

    def val(self, name):
        return self.s[name][1]

    def wait(self, eng, name, v=None):
        if isinstance(name, tuple):
            name, v = name
        if v is None or v <= 0:
            return
        h = self.s[name][0]
        self.plan[eng].append(lambda e: e.wait_ge(h, v))

    def op(self, eng, fn, inc=None, k=1):
        if inc is None:
            self.plan[eng].append(fn)
            return None
        sem = self.s[inc]
        sem[1] += k
        h, v = sem[0], sem[1]
        self.plan[eng].append(lambda e: fn(e).then_inc(h, k))
        return v

    def dma(self, queue, dst, src, cont=False):
        if cont:
            i = (self.cur[queue] - 1) % self.NPOOL[queue]
            nm = f"d{queue[0]}{i}"
        else:
            i = self.cur[queue]
            self.cur[queue] = (i + 1) % self.NPOOL[queue]
            nm = f"d{queue[0]}{i}"
            sem = self.s[nm]
            if sem[1] > 0:
                self.wait(queue, nm, sem[1])
        v = self.op(queue, lambda e, d=dst, s=src: e.dma_start(out=d, in_=s), inc=nm, k=16)
        return (nm, v)

    def dma_barrier(self, eng, queue):
        for i in range(self.NPOOL[queue]):
            nm = f"d{queue[0]}{i}"
            self.wait(eng, nm, self.s[nm][1])

    def drain(self, eng):
        self.plan[eng].append(lambda e: e.drain())

    def uselib(self, name):
        if getattr(self, "_gplib", None) == name:
            return
        self._gplib = name
        lib = getattr(_libcfg, name)
        self.plan["gpsimd"].append(lambda e: e.load_library(lib))

    def join(self):
        snap = {n: self.s[n][1] for n in self.s}
        owner = {"tv": "tensor", "vv": "vector", "sv": "scalar", "gv": "gpsimd"}
        for eng in self.ENGS:
            for n, v in snap.items():
                if owner.get(n) != eng:
                    self.wait(eng, n, v)
        for eng in ("vector", "scalar", "gpsimd"):
            self.drain(eng)

    def scope(self, name):
        for eng in self.ENGS:
            self.plan[eng].append(("scope", name))

    def run(self):
        nc = self.nc
        with self.nc.Block() as blk:
            for eng in self.ENGS:
                def mk(eng_name):
                    def body(e):
                        cur = [None]
                        for fn in self.plan[eng_name]:
                            if isinstance(fn, tuple) and fn[0] == "scope":
                                if cur[0] is not None:
                                    nc.leave_named_scope(cur[0][0], cur[0][1], False)
                                sid, _ = nc.enter_named_scope(fn[1], False)
                                cur[0] = (fn[1], sid)
                            else:
                                fn(e)
                        if cur[0] is not None:
                            nc.leave_named_scope(cur[0][0], cur[0][1], False)
                    return body
                getattr(blk, eng)(mk(eng))


def build(consts):
    nc = bass.Bass("TRN2", target_bir_lowering=False)
    ctx = ExitStack()
    I = {}
    I["x_in"] = nc.dram_tensor("x_in", [S, D], F32, kind="ExternalInput").ap()
    out_x = nc.dram_tensor("out", [S, D], F32, kind="ExternalOutput").ap()
    dbg = nc.dram_tensor("dbg", [1024, S], F32, kind="ExternalOutput").ap()
    def dump_bf(srcfn, nchunks=8):
        p.join()
        for dc in range(nchunks):
            p.op("vector", lambda e, dc=dc: e.tensor_copy(out=AF32[:, 0:2048], in_=srcfn(dc)), inc="vv")
            vv = p.val("vv")
            p.wait("sync", "vv", vv)
            sv = p.dma("sync", dbg[dc * 128:(dc + 1) * 128, :], AF32[:, 0:2048])
            p.wait("vector", sv)
        p.join()
    for nm, sh in WSHAPES.items():
        I[nm] = nc.dram_tensor(nm, list(sh), F32, kind="ExternalInput").ap()
    for nm, arr in consts.items():
        I["c_" + nm] = nc.dram_tensor("c_" + nm, list(arr.shape), F32, kind="ExternalInput").ap()

    SCR = {}
    def scr(name, shape, dt=F32):
        SCR[name] = nc.dram_tensor(name, list(shape), dt, kind="Internal").ap()
        return SCR[name]
    scr("rows8", (8, S))
    scr("band", (16, 4096))
    scr("band2", (16, 128 * 4225 + 16))
    scr("vtok", (16, 128, 1024), BF16)
    scr("penf", (16, 128, S), BF16)
    scr("h0", (32, 128, S))
    scr("wqg", (8, 128, 4096), BF16)
    scr("wqu", (8, 128, 4096), BF16)
    scr("wqd", (32, 128, 1024), BF16)
    scr("qscr", (16, 64, S))
    scr("kscr", (16, 64, S))
    scr("vscr", (16, 64, S))
    scr("ascr", (16, S))
    scr("yscr", (16, 16, 16, 512))
    scr("oT", (8, 128, S), BF16)
    scr("kraw", (8, 128, S))
    scr("vrc", (16, 16, 8192))
    scr("y2", (16, 64, S))
    scr("c16", (16, 3))
    scr("c16b", (16, 1))
    scr("rows16", (16, S))
    scr("l3cs", (32, S))
    scr("l3sn", (2, 32, S))
    scr("w1b", (2, 1024, 5632), BF16)
    scr("w2b", (2, 2816, 1024), BF16)

    def sbuf(name, shape, dt):
        return ctx.enter_context(nc.sbuf_tensor(name, list(shape), dt))

    x_res = sbuf("x_res", (128, 8 * S), F32)
    BFA = sbuf("BFA", (128, 32768), BF16)
    BFB = sbuf("BFB", (128, 16384), BF16)
    A = sbuf("A", (128, 2048), F32)
    W0 = sbuf("W0", (128, 8192), BF16)
    W1 = sbuf("W1", (128, 2048), BF16)
    Db = sbuf("Db", (128, 4096), BF16)
    rows = sbuf("rows", (128, 2048), F32)
    colf = sbuf("colf", (128, 64), F32)
    id32 = sbuf("id32", (128, 128), F32)
    onesb = sbuf("onesb", (128, 128), BF16)
    ones32 = sbuf("ones32", (128, 4), F32)
    honesb = sbuf("honesb", (128, 128), BF16)
    sstat = sbuf("sstat", (128, 256), F32)
    hones32 = sbuf("hones32", (128, 64), F32)
    psum = [ctx.enter_context(nc.psum_tensor(f"ps{i}", [128, 512], F32)) for i in range(4)]
    psZ = ctx.enter_context(nc.psum_tensor("psZ", [128, 2048], F32))

    p = Prog(nc, ctx)
    AF32 = A[:]
    A2 = lambda: rows[:, 0:2048]
    BG = {}
    def bank_wait(key):
        p.wait("tensor", "vv", BG.get(key, 0))
    def bank_set(key, val):
        BG[key] = val
    Dbf = Db[:].bitcast(F32)
    Aint = A[:].bitcast(I32)
    BFAf = BFA[:].bitcast(F32)
    BFBf = BFB[:].bitcast(F32)
    BFBi = BFB[:].bitcast(I32)

    xr = lambda dc: x_res[:, dc * S:(dc + 1) * S]

    p.op("gpsimd", lambda e: e.memset(colf[:, 62:63], 1e-5), inc="gv")
    p.dma("sync", id32[:], I["c_ident"])
    p.dma("gpsimd", onesb[:], I["c_ones"])
    p.dma("sync", ones32[:, 0:1], I["c_onescol"])
    for cc in range(8):
        p.dma("gpsimd", honesb[:, cc * 16:cc * 16 + 16], I["c_hones16"][cc])
    p.join()

    for tc in range(16):
        sl = tc % 4
        v = p.dma("sync", BFAf[:, sl * 1024:sl * 1024 + 1024], I["x_in"][tc * 128:(tc + 1) * 128, :])
        p.wait("tensor", v)
        for dc in range(8):
            bk = dc % 4
            bank_wait(bk)
            tv = p.op("tensor", lambda e, sl=sl, dc=dc, bk=bk: e.transpose(
                psum[bk][:, 0:128], BFAf[:, sl * 1024 + dc * 128: sl * 1024 + dc * 128 + 128],
                id32[:]), inc="tv")
            p.wait("vector", "tv", tv)
            bank_set(bk, p.op("vector", lambda e, tc=tc, dc=dc, bk=bk: e.tensor_copy(
                out=x_res[:, dc * S + tc * 128: dc * S + tc * 128 + 128],
                in_=psum[bk][:, 0:128]), inc="vv"))
        if tc >= 12:
            pass
        p.wait("sync", "tv", p.val("tv"))
    p.join()

    def cast_xb():
        for dc in range(8):
            p.op("vector", lambda e, dc=dc: e.tensor_copy(out=BFA[:, dc * S:(dc + 1) * S],
                                                          in_=xr(dc)), inc="vv")
        p.join()

    xb = lambda dc: BFA[:, dc * S:(dc + 1) * S]
    qTb = lambda oc: BFA[:, 16384 + oc * S: 16384 + oc * S + S]
    kTb = lambda oc: BFB[:, oc * S: oc * S + S]

    def load_w_bf16(dst, name, ncols, nchunks, roff=0):
        v = None
        for ic in range(nchunks):
            v = p.dma("gpsimd", dst[:, ic * ncols:(ic + 1) * ncols],
                      I[name][roff + ic * 128: roff + (ic + 1) * 128, :], cont=(ic > 0))
        return v

    def load_bias_cols(name, col0, n=8):
        t = None
        for dc in range(n):
            t = p.dma("sync", colf[:, col0 + dc:col0 + dc + 1], I[name][dc], cont=(dc > 0))
        p.wait("vector", t)
        p.wait("scalar", t)

    def proj_feat(wbuf, bias_col0, dst, n_oc=8, n_ic=8, wcols=D, movsrc=None, psb=0,
                  out_bf16=True, bias=True):
        movsrc = movsrc or xb
        for oc in range(n_oc):
            for tc in range(4):
                bk = psb + ((oc * 4 + tc) % 2)
                bank_wait(bk)
                for ic in range(n_ic):
                    p.op("tensor", lambda e, oc=oc, tc=tc, ic=ic, bk=bk: e.matmul(
                        psum[bk][:, 0:512],
                        wbuf[:, ic * wcols + oc * 128: ic * wcols + oc * 128 + 128],
                        movsrc(ic)[:, tc * 512: tc * 512 + 512],
                        start=(ic == 0), stop=(ic == n_ic - 1)), inc="tv")
                tvv = p.val("tv")
                p.wait("vector", "tv", tvv)
                if bias:
                    bank_set(bk, p.op("vector", lambda e, oc=oc, tc=tc, bk=bk: e.tensor_scalar(
                        out=dst(oc)[:, tc * 512: tc * 512 + 512], in0=psum[bk][:, 0:512],
                        scalar1=colf[:, bias_col0 + oc: bias_col0 + oc + 1], scalar2=None,
                        op0=ALU.add), inc="vv"))
                else:
                    bank_set(bk, p.op("vector", lambda e, oc=oc, tc=tc, bk=bk: e.tensor_copy(
                        out=dst(oc)[:, tc * 512: tc * 512 + 512], in_=psum[bk][:, 0:512]), inc="vv"))

    def ln_epilogue():
        p.join()
        for tc in range(4):
            for dc in range(8):
                sl = dc % 2
                sq = p.op("scalar", lambda e, dc=dc, tc=tc, sl=sl: e.activation(
                    out=AF32[:, sl * 512: sl * 512 + 512],
                    in_=xr(dc)[:, tc * 512: tc * 512 + 512], func=AF.Square), inc="sv")
                p.wait("tensor", "sv", sq)
                p.op("tensor", lambda e, dc=dc, tc=tc: e.matmul(
                    psum[0][0:1, 0:512], ones32[:, 0:1],
                    xr(dc)[:, tc * 512: tc * 512 + 512],
                    start=(dc == 0), stop=(dc == 7)), inc="tv")
                tvq = p.op("tensor", lambda e, dc=dc, tc=tc, sl=sl: e.matmul(
                    psum[1][0:1, 0:512], ones32[:, 0:1],
                    AF32[:, sl * 512: sl * 512 + 512],
                    start=(dc == 0), stop=(dc == 7)), inc="tv")
                p.wait("scalar", "tv", tvq)
            tvv = p.val("tv")
            p.wait("vector", "tv", tvv)
            p.op("vector", lambda e, tc=tc: e.tensor_scalar(
                out=BFBf[0:1, tc * 512:tc * 512 + 512], in0=psum[0][0:1, 0:512],
                scalar1=1.0 / 1024.0, scalar2=None, op0=ALU.mult), inc="vv")
            p.op("vector", lambda e, tc=tc: e.tensor_scalar(
                out=BFBf[0:1, 2048 + tc * 512: 2048 + tc * 512 + 512], in0=psum[1][0:1, 0:512],
                scalar1=1.0 / 1024.0, scalar2=None, op0=ALU.mult), inc="vv")
            vv = p.val("vv")
            p.wait("tensor", "vv", vv)
        p.drain("vector")
        MEAN = BFBf[0:1, 0:S]; M2 = BFBf[0:1, 2048:2048 + S]
        AV_ = BFBf[0:1, 4096:4096 + S]; BV_ = BFBf[0:1, 6144:6144 + S]
        p.op("vector", lambda e: e.scalar_tensor_tensor(
            out=AV_, in0=MEAN, scalar=-1.0, in1=MEAN, op0=ALU.mult, op1=ALU.mult))
        p.drain("vector")
        p.op("vector", lambda e: e.tensor_tensor(out=AV_, in0=AV_, in1=M2, op=ALU.add), inc="vv")
        p.drain("vector")
        vv = p.val("vv")
        p.wait("scalar", "vv", vv)
        p.op("scalar", lambda e: e.activation(out=AV_, in_=AV_, func=AF.Sqrt,
                                              bias=colf[0:1, 62:63]), inc="sv")
        p.wait("vector", "sv", p.val("sv"))
        p.op("vector", lambda e: e.reciprocal(out=AV_, in_=AV_))
        p.drain("vector")
        p.op("vector", lambda e: e.scalar_tensor_tensor(
            out=BV_, in0=MEAN, scalar=-1.0, in1=AV_, op0=ALU.mult, op1=ALU.mult), inc="vv")
        vv = p.val("vv")
        p.wait("sync", "vv", vv)
        wv = p.dma("sync", SCR["rows8"][0:1, :], AV_)
        wv = p.dma("sync", SCR["rows8"][1:2, :], BV_, cont=True)
        p.wait("sync", wv)
        p.dma("sync", BFBf[:, 0:2048], SCR["rows8"][0:1, :].broadcast_to([128, S]))
        vb = p.dma("sync", BFBf[:, 2048:4096], SCR["rows8"][1:2, :].broadcast_to([128, S]),
                   cont=True)
        p.wait("vector", vb)
        for dc in range(8):
            p.op("vector", lambda e, dc=dc: e.tensor_tensor(
                out=xr(dc), in0=xr(dc), in1=BFBf[:, 0:2048], op=ALU.mult))
            p.drain("vector")
            p.op("vector", lambda e, dc=dc: e.tensor_tensor(
                out=xr(dc), in0=xr(dc), in1=BFBf[:, 2048:4096], op=ALU.add), inc="vv")
            p.drain("vector")
        p.join()

    def band_src(h, k0, q0):
        st = SCR["band2"][h]
        return bass.AP(st.tensor, st.offset + (2047 - (k0 - q0)), [[4224, 128], [1, 512]])

    def headnorms(srcfn, dst_r0):
        for tc in range(4):
            for dc in range(8):
                sl = dc % 2
                p.wait("scalar", "tv", p.val("tv"))
                sq = p.op("scalar", lambda e, dc=dc, tc=tc, sl=sl: e.activation(
                    out=Db[:, 2048 + sl * 512: 2048 + sl * 512 + 512],
                    in_=srcfn(dc)[:, tc * 512: tc * 512 + 512], func=AF.Square), inc="sv")
                p.wait("tensor", "sv", sq)
                p.op("tensor", lambda e, dc=dc, tc=tc, sl=sl: e.matmul(
                    psum[3][0:16, 0:512], honesb[:, dc * 16: dc * 16 + 16],
                    Db[:, 2048 + sl * 512: 2048 + sl * 512 + 512],
                    start=(dc == 0), stop=(dc == 7)), inc="tv")
            tvv = p.val("tv")
            p.wait("scalar", "tv", tvv)
            p.op("scalar", lambda e, tc=tc, dst_r0=dst_r0: e.activation(
                out=rows[dst_r0:dst_r0 + 16, tc * 512: tc * 512 + 512],
                in_=psum[3][0:16, 0:512], func=AF.Sqrt), inc="sv")
            p.wait("tensor", "sv", p.val("sv"))

    def l0_attn():
        cast_xb()
        load_bias_cols("l0_bq", 0)
        load_bias_cols("l0_bk", 8)
        load_bias_cols("l0_bo", 24)
        bv1 = p.dma("sync", rows[0:16, 0:1023], I["l0_reltab"])
        p.wait("vector", bv1)
        p.op("vector", lambda e: e.tensor_scalar(
            out=rows[0:16, 0:1023], in0=rows[0:16, 0:1023],
            scalar1=rows[0:16, 1022:1023], scalar2=None, op0=ALU.subtract), inc="vv")
        p.drain("vector")
        p.wait("gpsimd", "vv", p.val("vv"))
        p.op("gpsimd", lambda e: e.memset(BFBf[0:16, 0:4096], 1.0), inc="gv")
        p.wait("scalar", "gv", p.val("gv"))
        p.wait("scalar", "vv", p.val("vv"))
        p.op("scalar", lambda e: e.activation(
            out=BFBf[0:16, 0:1023], in_=rows[0:16, 0:1023], func=AF.Exp), inc="sv")
        p.wait("sync", "sv", p.val("sv"))
        bw = p.dma("sync", SCR["band"], BFBf[0:16, 0:4096])
        p.wait("sync", bw)
        for h in range(16):
            st = SCR["band2"][h]
            p.dma("sync", bass.AP(st.tensor, st.offset, [[4225, 128], [1, 4096]]),
                  SCR["band"][h:h + 1, 0:4096].broadcast_to([128, 4096]))
        p.join()
        vw = load_w_bf16(W0, "l0_wv", D, 8)
        p.wait("tensor", vw)
        p.join()
        wv = p.dma("sync", SCR["rows8"][2:3, 0:D], I["l0_bv_row"])
        p.wait("sync", wv)
        vb = p.dma("sync", AF32[:, 1024:2048], SCR["rows8"][2:3, 0:D].broadcast_to([128, D]))
        p.wait("vector", vb)
        for tc in range(16):
            for og in range(2):
                bk = og
                bank_wait(bk)
                for ic in range(8):
                    p.op("tensor", lambda e, tc=tc, og=og, ic=ic, bk=bk: e.matmul(
                        psum[bk][:, 0:512],
                        xb(ic)[:, tc * 128: tc * 128 + 128],
                        W0[:, ic * D + og * 512: ic * D + og * 512 + 512],
                        start=(ic == 0), stop=(ic == 7)), inc="tv")
                tvv = p.val("tv")
                p.wait("vector", "tv", tvv)
                bank_set(bk, p.op("vector", lambda e, tc=tc, og=og, bk=bk: e.tensor_tensor(
                    out=BFB[:, tc * 1024 + og * 512: tc * 1024 + og * 512 + 512],
                    in0=psum[bk][:, 0:512], in1=AF32[:, 1024 + og * 512: 1024 + og * 512 + 512],
                    op=ALU.add), inc="vv"))
        vvV = p.val("vv")
        p.wait("sync", "vv", vvV)
        for tc in range(16):
            p.dma("sync", SCR["vtok"][tc], BFB[:, tc * 1024:(tc + 1) * 1024])
        p.join()
        qw = load_w_bf16(W0, "l0_wq", D, 8)
        p.wait("tensor", qw)
        proj_feat(W0, 0, qTb)
        p.join()
        kw = load_w_bf16(W0, "l0_wk", D, 8)
        p.wait("tensor", kw)
        p.join()
        proj_feat(W0, 8, kTb, psb=2)
        p.join()
        if KDUMP == "q":
            dump_bf(qTb)
        if KDUMP == "k":
            dump_bf(kTb)
        headnorms(kTb, 0)
        p.join()
        p.op("vector", lambda e: e.tensor_reduce(out=colf[0:16, 33:34], in_=rows[0:16, 0:S],
                                                 axis=AX.X, op=ALU.max), inc="vv")
        p.join()
        headnorms(qTb, 0)
        p.join()
        p.op("vector", lambda e: e.tensor_scalar(
            out=rows[0:16, 0:S], in0=rows[0:16, 0:S],
            scalar1=colf[0:16, 33:34], scalar2=-1.0, op0=ALU.mult, op1=ALU.mult), inc="vv")
        p.join()

        W0f = W0[:].bitcast(F32)
        p.op("gpsimd", lambda e: e.memset(Db[64:65, 2048:4096], 1.0), inc="gv")
        p.wait("sync", "gv", p.val("gv"))
        probsT = lambda kc: W1[:, kc * 512:(kc + 1) * 512]
        PB = {}
        for h in range(16):
            cc, hh = h // 2, h % 2
            p.wait("sync", "tv", p.val("tv"))
            p.wait("gpsimd", "tv", p.val("tv"))
            c1 = p.dma("sync", Db[0:64, 0:S], qTb(cc)[hh * 64: hh * 64 + 64, :])
            c3 = p.dma("sync", Db[0:64, 2048:2048 + S], kTb(cc)[hh * 64: hh * 64 + 64, :],
                       cont=True)
            c2 = p.dma("gpsimd", Db[64:65, 0:S], rows[h:h + 1, 0:S])
            vl = None
            for kc in range(16):
                vl = p.dma("gpsimd", W0[:, 6144 + kc * 64: 6144 + kc * 64 + 64],
                           SCR["vtok"][kc][:, h * 64:(h + 1) * 64], cont=(kc > 0))
            p.wait("tensor", c3)
            p.wait("tensor", c2)
            p.wait("tensor", vl)
            btoks = {}
            def band_load(qc, kc):
                sl = kc % 4
                p.wait("gpsimd", "vv", PB.get(sl, 0))
                btoks[(qc, kc)] = p.dma("gpsimd", W0[:, sl * 512: sl * 512 + 512],
                                        band_src(h, kc * 128, qc * 512))
            for qc in range(4):
                elig = [kc for kc in range(16) if kc * 128 + 127 - qc * 512 >= 1025]
                for kc in elig[:4]:
                    band_load(qc, kc)
                for half in range(4):
                  for k8 in range(4):
                    kc = half * 4 + k8
                    bk = kc % 2
                    tvv = p.op("tensor", lambda e, qc=qc, kc=kc, bk=bk: e.matmul(
                        psum[bk][:, 0:512],
                        Db[0:65, 2048 + kc * 128: 2048 + kc * 128 + 128],
                        Db[0:65, qc * 512: qc * 512 + 512], start=True, stop=True), inc="tv")
                    p.wait("scalar", "tv", tvv)
                    sv = p.op("scalar", lambda e, k8=k8, bk=bk: e.activation(
                        out=probsT(k8), in_=psum[bk][:, 0:512], func=AF.Exp, scale=0.125),
                        inc="sv")
                    p.wait("tensor", "sv", sv)
                    k0, q0 = kc * 128, qc * 512
                    if k0 + 127 - q0 >= 1025:
                        sl = kc % 4
                        p.wait("vector", btoks.pop((qc, kc)))
                        p.wait("vector", "sv", sv)
                        vvb = p.op("vector", lambda e, k8=k8, sl=sl: e.tensor_tensor(
                            out=probsT(k8), in0=probsT(k8),
                            in1=W0[:, sl * 512: sl * 512 + 512],
                            op=ALU.mult), inc="vv")
                        PB[sl] = vvb
                        ei = elig.index(kc)
                        if ei + 4 < len(elig):
                            band_load(qc, elig[ei + 4])
                  p.wait("tensor", "sv", p.val("sv"))
                  p.wait("tensor", "vv", p.val("vv"))
                  for k8 in range(4):
                    kc = half * 4 + k8
                    p.op("tensor", lambda e, k8=k8, kc=kc: e.matmul(
                        psum[2][0:64, 0:512], W0[:, 6144 + kc * 64: 6144 + kc * 64 + 64],
                        probsT(k8), start=(kc == 0), stop=(kc == 15)), inc="tv")
                    p.op("tensor", lambda e, k8=k8, kc=kc: e.matmul(
                        psum[3][0:1, 0:512], onesb[:, 0:1],
                        probsT(k8), start=(kc == 0), stop=(kc == 15)), inc="tv")
                  p.wait("tensor", "tv", p.val("tv"))
                tvv = p.val("tv")
                p.wait("vector", "tv", tvv)
                p.op("vector", lambda e, hh=hh, qc=qc: e.tensor_copy(
                    out=AF32[hh * 64: hh * 64 + 64, qc * 512: qc * 512 + 512],
                    in_=psum[2][0:64, 0:512]), inc="vv")
                vv = p.op("vector", lambda e, hh=hh, qc=qc: e.reciprocal(
                    out=rows[32 + hh * 32: 33 + hh * 32, qc * 512: qc * 512 + 512],
                    in_=psum[3][0:1, 0:512]), inc="vv")
                p.wait("tensor", "vv", vv)
            if hh == 1:
                p.drain("vector")
                vv = p.val("vv")
                p.wait("sync", "vv", vv)
                p.dma("sync", SCR["rows8"][3:4, :], rows[32:33, 0:S])
                wv2 = p.dma("sync", SCR["rows8"][4:5, :], rows[64:65, 0:S], cont=True)
                p.wait("sync", wv2)
                p.wait("sync", "vv", p.val("vv"))
                v1 = p.dma("sync", W0f[0:64, 1024:3072],
                           SCR["rows8"][3:4, :].broadcast_to([64, S]))
                v2 = p.dma("sync", W0f[64:128, 1024:3072],
                           SCR["rows8"][4:5, :].broadcast_to([64, S]), cont=True)
                p.wait("vector", v2)
                p.op("vector", lambda e, cc=cc: e.tensor_tensor(
                    out=BFA[:, cc * S:(cc + 1) * S], in0=AF32[:, 0:2048],
                    in1=W0f[:, 1024:3072], op=ALU.mult), inc="vv")
                p.drain("vector")
        p.join()
        if KDUMP == "av":
            dump_bf(lambda dc: BFA[:, dc * S:(dc + 1) * S])
        ow = load_w_bf16(W0, "l0_wo", D, 8)
        p.wait("tensor", ow)
        avb = lambda ic: BFA[:, ic * S:(ic + 1) * S]
        for oc in range(8):
            for tc in range(4):
                bk = (oc * 4 + tc) % 2
                for ic in range(8):
                    p.op("tensor", lambda e, oc=oc, tc=tc, ic=ic, bk=bk: e.matmul(
                        psum[bk][:, 0:512],
                        W0[:, ic * D + oc * 128: ic * D + oc * 128 + 128],
                        avb(ic)[:, tc * 512: tc * 512 + 512],
                        start=(ic == 0), stop=(ic == 7)), inc="tv")
                tvv = p.val("tv")
                p.wait("vector", "tv", tvv)
                vv = p.op("vector", lambda e, oc=oc, tc=tc, bk=bk: e.scalar_tensor_tensor(
                    out=xr(oc)[:, tc * 512: tc * 512 + 512], in0=psum[bk][:, 0:512],
                    scalar=colf[:, 24 + oc: 24 + oc + 1],
                    in1=xr(oc)[:, tc * 512: tc * 512 + 512], op0=ALU.add, op1=ALU.add), inc="vv")
                p.wait("tensor", "vv", vv)
        ln_epilogue()

    def par_max_to_rep(accap, dstap, scr_row):
        p.join()
        for tc in range(16):
            tv = p.op("tensor", lambda e, tc=tc: e.transpose(
                psum[3][:, 0:128], accap[:, tc * 128: tc * 128 + 128], id32[:]), inc="tv")
            p.wait("vector", "tv", tv)
            vv = p.op("vector", lambda e, tc=tc: e.tensor_reduce(
                out=rows[:, 1024 + tc: 1024 + tc + 1], in_=psum[3][:, 0:128],
                axis=AX.X, op=ALU.max), inc="vv")
            p.wait("tensor", "vv", vv)
        p.join()
        for tc in range(16):
            p.dma("sync", SCR["rows8"][scr_row:scr_row + 1, tc * 128:(tc + 1) * 128],
                  rows[:, 1024 + tc: 1024 + tc + 1])
        p.dma_barrier("sync", "sync")
        bv = p.dma("sync", dstap, SCR["rows8"][scr_row:scr_row + 1, :].broadcast_to([128, S]))
        p.wait("vector", bv)
        p.join()


    def quantize_weight(name, nchunks, ncols, scr_name, mcol):
        p.join()
        nel = float(nchunks * 128 * ncols)
        for ic in range(nchunks):
            v = p.dma("sync", BFAf[:, 0:ncols] if ic % 2 == 0 else BFAf[:, ncols:2 * ncols],
                      I[name][ic * 128:(ic + 1) * 128, :])
            p.wait("vector", v)
            p.op("vector", lambda e, ic=ic: e.tensor_reduce(
                out=colf[:, 40:41], in_=BFAf[:, 0:ncols] if ic % 2 == 0 else BFAf[:, ncols:2 * ncols],
                axis=AX.X, op=ALU.add, apply_absolute_value=True), inc="vv")
            p.drain("vector")
            if ic == 0:
                p.op("vector", lambda e: e.tensor_copy(out=colf[:, 41:42], in_=colf[:, 40:41]), inc="vv")
            else:
                p.op("vector", lambda e: e.tensor_tensor(out=colf[:, 41:42], in0=colf[:, 41:42],
                                                         in1=colf[:, 40:41], op=ALU.add), inc="vv")
            p.drain("vector")
            p.wait("sync", "vv", p.val("vv"))
        p.join()
        tv = p.op("tensor", lambda e: e.matmul(psum[3][0:1, 0:1], colf[:, 41:42],
                                               ones32[:, 0:1], start=True, stop=True), inc="tv")
        p.wait("vector", "tv", tv)
        vv = p.op("vector", lambda e: e.tensor_copy(out=rows[0:1, 0:1], in_=psum[3][0:1, 0:1]),
                  inc="vv")
        p.wait("sync", "vv", vv)
        wv = p.dma("sync", SCR["rows8"][7:8, 0:1], rows[0:1, 0:1])
        p.wait("sync", wv)
        bv = p.dma("sync", colf[:, 42:43], SCR["rows8"][7:8, 0:1].broadcast_to([128, 1]))
        p.wait("vector", bv)
        p.op("vector", lambda e: e.tensor_scalar(out=colf[:, mcol:mcol + 1], in0=colf[:, 42:43],
                                                 scalar1=1.0 / nel, scalar2=1e-5,
                                                 op0=ALU.mult, op1=ALU.max), inc="vv")
        p.drain("vector")
        p.op("vector", lambda e: e.reciprocal(out=colf[:, 43:44], in_=colf[:, mcol:mcol + 1]), inc="vv")
        p.drain("vector")
        Bi32 = BFB[:].bitcast(I32)
        for ic in range(nchunks):
            p.wait("sync", "vv", p.val("vv"))
            v = p.dma("sync", BFAf[:, 0:ncols], I[name][ic * 128:(ic + 1) * 128, :])
            p.wait("vector", v)
            p.op("vector", lambda e: e.tensor_scalar(out=BFBf[:, 0:ncols], in0=BFAf[:, 0:ncols],
                                                     scalar1=colf[:, 43:44], scalar2=None,
                                                     op0=ALU.mult), inc="vv")
            p.drain("vector")
            p.op("vector", lambda e: e.tensor_copy(out=Bi32[:, 4096:4096 + ncols],
                                                   in_=BFBf[:, 0:ncols]), inc="vv")
            p.drain("vector")
            p.op("vector", lambda e: e.tensor_copy(out=BFBf[:, 0:ncols],
                                                   in_=Bi32[:, 4096:4096 + ncols]), inc="vv")
            p.drain("vector")
            p.op("vector", lambda e: e.tensor_scalar(out=Db[:, 0:ncols], in0=BFBf[:, 0:ncols],
                                                     scalar1=-1.0, scalar2=1.0,
                                                     op0=ALU.max, op1=ALU.min), inc="vv")
            p.drain("vector")
            vv = p.val("vv")
            p.wait("sync", "vv", vv)
            sv = p.dma("sync", SCR[scr_name][ic], Db[:, 0:ncols])
            p.wait("vector", sv)
        p.join()

    def l0_ffn():
        quantize_weight("l0f_wg", 8, 4096, "wqg", 48)
        quantize_weight("l0f_wu", 8, 4096, "wqu", 49)
        quantize_weight("l0f_wd", 32, 1024, "wqd", 50)
        p.op("vector", lambda e: e.scalar_tensor_tensor(
            out=AF32[:, 0:2048], in0=xr(0), scalar=-1.0, in1=xr(0),
            op0=ALU.mult, op1=ALU.max), inc="vv")
        p.drain("vector")
        for dc in range(1, 8):
            p.op("vector", lambda e, dc=dc: e.scalar_tensor_tensor(
                out=BFBf[:, 6144:8192], in0=xr(dc), scalar=-1.0, in1=xr(dc),
                op0=ALU.mult, op1=ALU.max), inc="vv")
            p.drain("vector")
            p.op("vector", lambda e: e.tensor_tensor(out=AF32[:, 0:2048], in0=AF32[:, 0:2048],
                                                     in1=BFBf[:, 6144:8192], op=ALU.max), inc="vv")
            p.drain("vector")
        par_max_to_rep(AF32[:, 0:2048], AF32[:, 0:2048], 6)
        p.op("vector", lambda e: e.tensor_scalar(out=AF32[:, 0:2048], in0=AF32[:, 0:2048],
                                                 scalar1=1e-5, scalar2=None, op0=ALU.max), inc="vv")
        p.drain("vector")
        p.op("vector", lambda e: e.reciprocal(out=BFBf[:, 0:2048], in_=AF32[:, 0:2048]), inc="vv")
        p.drain("vector")
        Bi32 = BFB[:].bitcast(I32)
        for dc in range(8):
            p.op("vector", lambda e, dc=dc: e.scalar_tensor_tensor(
                out=BFBf[:, 2048:4096], in0=xr(dc), scalar=127.0, in1=BFBf[:, 0:2048],
                op0=ALU.mult, op1=ALU.mult), inc="vv")
            p.drain("vector")
            p.op("vector", lambda e: e.tensor_copy(out=Bi32[:, 4096:6144], in_=BFBf[:, 2048:4096]), inc="vv")
            p.drain("vector")
            p.op("vector", lambda e: e.tensor_copy(out=BFBf[:, 2048:4096], in_=Bi32[:, 4096:6144]), inc="vv")
            p.drain("vector")
            p.op("vector", lambda e, dc=dc: e.tensor_scalar(
                out=BFA[:, dc * S:(dc + 1) * S], in0=BFBf[:, 2048:4096],
                scalar1=-128.0, scalar2=127.0, op0=ALU.max, op1=ALU.min), inc="vv")
            p.drain("vector")
        if KDUMP == "xhat":
            dump_bf(lambda dc: BFA[:, dc * S:(dc + 1) * S])
        p.op("vector", lambda e: e.tensor_scalar(out=BFBf[:, 0:2048], in0=AF32[:, 0:2048],
                                                 scalar1=colf[:, 48:49], scalar2=1.0 / 127.0,
                                                 op0=ALU.mult, op1=ALU.mult), inc="vv")
        p.op("vector", lambda e: e.tensor_scalar(out=BFBf[:, 2048:4096], in0=AF32[:, 0:2048],
                                                 scalar1=colf[:, 49:50], scalar2=1.0 / 127.0,
                                                 op0=ALU.mult, op1=ALU.mult), inc="vv")
        p.wait("gpsimd", "vv", p.val("vv"))
        p.op("gpsimd", lambda e: e.memset(BFBf[:, 4096:6144], 0.0), inc="gv")
        p.join()
        xhat = lambda ic: BFA[:, ic * S:(ic + 1) * S]
        for oc in range(32):
            p.wait("sync", "tv", p.val("tv"))
            for ic in range(8):
                p.dma("sync", Db[:, ic * 128: ic * 128 + 128],
                      SCR["wqg"][ic][:, oc * 128:(oc + 1) * 128], cont=(ic > 0))
                wv = p.dma("sync", Db[:, 1024 + ic * 128: 1024 + ic * 128 + 128],
                           SCR["wqu"][ic][:, oc * 128:(oc + 1) * 128], cont=True)
            p.wait("tensor", wv)
            for tc in range(4):
                bkg, bku = tc % 2, 2 + tc % 2
                bank_wait(bkg)
                bank_wait(bku)
                for ic in range(8):
                    p.op("tensor", lambda e, ic=ic, tc=tc, bkg=bkg: e.matmul(
                        psum[bkg][:, 0:512], Db[:, ic * 128: ic * 128 + 128],
                        xhat(ic)[:, tc * 512: tc * 512 + 512],
                        start=(ic == 0), stop=(ic == 7)), inc="tv")
                    p.op("tensor", lambda e, ic=ic, tc=tc, bku=bku: e.matmul(
                        psum[bku][:, 0:512], Db[:, 1024 + ic * 128: 1024 + ic * 128 + 128],
                        xhat(ic)[:, tc * 512: tc * 512 + 512],
                        start=(ic == 0), stop=(ic == 7)), inc="tv")
                tvv = p.val("tv")
                p.wait("vector", "tv", tvv)
                p.wait("vector", "sv", p.val("sv"))
                p.op("vector", lambda e, tc=tc, bkg=bkg: e.tensor_tensor(
                    out=AF32[:, 0:512], in0=psum[bkg][:, 0:512],
                    in1=BFBf[:, tc * 512: tc * 512 + 512], op=ALU.mult), inc="vv")
                vvg = p.val("vv")
                p.wait("scalar", "vv", vvg)
                p.op("scalar", lambda e: e.activation(out=AF32[:, 512:1024], in_=AF32[:, 0:512],
                                                      func=AF.Sigmoid), inc="sv")
                p.wait("vector", "sv", p.val("sv"))
                p.op("vector", lambda e: e.tensor_tensor(
                    out=AF32[:, 512:1024], in0=AF32[:, 512:1024], in1=AF32[:, 0:512],
                    op=ALU.mult), inc="vv")
                p.drain("vector")
                p.op("vector", lambda e, tc=tc, bku=bku: e.tensor_tensor(
                    out=AF32[:, 1024:1536], in0=psum[bku][:, 0:512],
                    in1=BFBf[:, 2048 + tc * 512: 2048 + tc * 512 + 512], op=ALU.mult), inc="vv")
                p.wait("vector", "sv", p.val("sv"))
                p.drain("vector")
                bank_set(bkg, p.val("vv"))
                bank_set(bku, p.val("vv"))
                hvv = p.op("vector", lambda e: e.tensor_tensor(out=AF32[:, 1536:2048],
                                                               in0=AF32[:, 512:1024],
                                                               in1=AF32[:, 1024:1536],
                                                               op=ALU.mult), inc="vv")
                p.drain("vector")
                p.op("vector", lambda e: e.scalar_tensor_tensor(
                    out=AF32[:, 0:512], in0=AF32[:, 1536:2048], scalar=-1.0,
                    in1=AF32[:, 1536:2048], op0=ALU.mult, op1=ALU.max), inc="vv")
                p.drain("vector")
                p.op("vector", lambda e, tc=tc: e.tensor_tensor(
                    out=BFBf[:, 4096 + tc * 512: 4096 + tc * 512 + 512],
                    in0=BFBf[:, 4096 + tc * 512: 4096 + tc * 512 + 512],
                    in1=AF32[:, 0:512], op=ALU.max), inc="vv")
                vv = p.val("vv")
                p.wait("sync", "vv", vv)
                hv = p.dma("sync", SCR["h0"][oc][:, tc * 512:(tc + 1) * 512], AF32[:, 1536:2048])
                p.wait("vector", hv)
        p.join()
        if KDUMP == "hmax":
            dump_bf(lambda dc: BFBf[:, 4096:6144] if dc == 0 else BFBf[:, 0:2048], 2)
        par_max_to_rep(BFBf[:, 4096:6144], BFBf[:, 4096:6144], 6)
        p.op("vector", lambda e: e.tensor_scalar(out=BFBf[:, 4096:6144], in0=BFBf[:, 4096:6144],
                                                 scalar1=1e-5, scalar2=None, op0=ALU.max), inc="vv")
        p.drain("vector")
        p.op("vector", lambda e: e.tensor_scalar(out=BFBf[:, 6144:8192], in0=BFBf[:, 4096:6144],
                                                 scalar1=colf[:, 50:51], scalar2=1.0 / 127.0,
                                                 op0=ALU.mult, op1=ALU.mult), inc="vv")
        p.drain("vector")
        p.op("vector", lambda e: e.reciprocal(out=BFBf[:, 4096:6144], in_=BFBf[:, 4096:6144]), inc="vv")
        p.join()
        if KDUMP == "dq2":
            dump_bf(lambda dc: BFBf[:, 6144:8192] if dc == 0 else BFBf[:, 4096:6144], 2)
        Bi32 = BFB[:].bitcast(I32)
        xh2 = lambda ic: BFA[:, 16384 + ic * 512: 16384 + ic * 512 + 512]
        for tc in range(4):
            for ic in range(32):
                sl = ic % 2
                p.wait("sync", "vv", p.val("vv"))
                hv = p.dma("sync", BFAf[:, 4096 + sl * 512: 4096 + sl * 512 + 512],
                           SCR["h0"][ic][:, tc * 512:(tc + 1) * 512])
                p.wait("vector", hv)
                p.op("vector", lambda e, sl=sl, tc=tc: e.scalar_tensor_tensor(
                    out=BFAf[:, 5120:5632], in0=BFAf[:, 4096 + sl * 512: 4096 + sl * 512 + 512],
                    scalar=127.0, in1=BFBf[:, 4096 + tc * 512: 4096 + tc * 512 + 512],
                    op0=ALU.mult, op1=ALU.mult), inc="vv")
                p.drain("vector")
                p.op("vector", lambda e: e.tensor_copy(out=Bi32[:, 0:512],
                                                       in_=BFAf[:, 5120:5632]), inc="vv")
                p.drain("vector")
                p.op("vector", lambda e: e.tensor_copy(out=BFAf[:, 5120:5632],
                                                       in_=Bi32[:, 0:512]), inc="vv")
                p.drain("vector")
                p.op("vector", lambda e, ic=ic: e.tensor_scalar(
                    out=xh2(ic), in0=BFAf[:, 5120:5632],
                    scalar1=-128.0, scalar2=127.0, op0=ALU.max, op1=ALU.min), inc="vv")
                p.drain("vector")
            vvq = p.val("vv")
            p.wait("tensor", "vv", vvq)
            p.wait("sync", "vv", vvq)
            for og in range(8):
                bk = og % 2
                for ic in range(32):
                    sl2 = ic % 2
                    p.wait("sync", "tv", p.val("tv"))
                    wv = p.dma("sync", Db[:, sl2 * 128: sl2 * 128 + 128],
                               SCR["wqd"][ic][:, og * 128:(og + 1) * 128])
                    p.wait("tensor", wv)
                    p.op("tensor", lambda e, og=og, ic=ic, bk=bk, sl2=sl2: e.matmul(
                        psum[bk][:, 0:512], Db[:, sl2 * 128: sl2 * 128 + 128],
                        xh2(ic), start=(ic == 0), stop=(ic == 31)), inc="tv")
                tvv = p.val("tv")
                p.wait("vector", "tv", tvv)
                p.op("vector", lambda e, og=og, tc=tc, bk=bk: e.tensor_tensor(
                    out=AF32[:, 0:512], in0=psum[bk][:, 0:512],
                    in1=BFBf[:, 6144 + tc * 512: 6144 + tc * 512 + 512], op=ALU.mult), inc="vv")
                p.drain("vector")
                if KDUMP == "dout" and tc == 0:
                    vvd = p.val("vv")
                    p.wait("sync", "vv", vvd)
                    dv = p.dma("sync", dbg[og * 128:(og + 1) * 128, 0:512], AF32[:, 0:512])
                    p.wait("vector", dv)
                vre = p.op("vector", lambda e, og=og, tc=tc: e.tensor_tensor(
                    out=xr(og)[:, tc * 512: tc * 512 + 512], in0=AF32[:, 0:512],
                    in1=xr(og)[:, tc * 512: tc * 512 + 512], op=ALU.add), inc="vv")
                p.drain("vector")
                p.wait("tensor", "vv", vre)
        if KDUMP == "y0":
            dump_bf(xr)
        if KDUMP == "wg":
            p.join()
            for ic in range(4):
                v = p.dma("sync", Db[:, 0:4096], SCR["wqg"][ic])
                p.wait("vector", v)
                p.op("vector", lambda e: e.tensor_copy(out=AF32[:, 0:2048], in_=Db[:, 0:2048]), inc="vv")
                vv = p.val("vv")
                p.wait("sync", "vv", vv)
                sv = p.dma("sync", dbg[ic * 128:(ic + 1) * 128, :], AF32[:, 0:2048])
                p.wait("vector", sv)
                p.wait("sync", sv)
        ln_epilogue()

    def l1_attn():
        cast_xb()
        load_bias_cols("l1_bq", 0)
        load_bias_cols("l1_bk", 8)
        gv1 = None
        for ic in range(8):
            gv1 = p.dma("gpsimd", Db[:, ic:ic + 1], I["l1_wg"][ic * 128:(ic + 1) * 128, :],
                        cont=(ic > 0))
        bgv = p.dma("sync", colf[0:1, 35:36], I["l1_bg"])
        p.op("gpsimd", lambda e: e.memset(colf[0:1, 36:37], float(np.log(2046.0))), inc="gv")
        p.dma_barrier("tensor", "gpsimd")
        p.wait("tensor", "vv", p.val("vv"))
        for tc in range(4):
            for ic in range(8):
                p.op("tensor", lambda e, tc=tc, ic=ic: e.matmul(
                    psum[0][0:1, 0:512], Db[:, ic:ic + 1],
                    xb(ic)[:, tc * 512: tc * 512 + 512],
                    start=(ic == 0), stop=(ic == 7)), inc="tv")
            tvv = p.val("tv")
            p.wait("scalar", "tv", tvv)
            p.wait("scalar", bgv)
            p.wait("scalar", "gv", p.val("gv"))
            sv = p.op("scalar", lambda e, tc=tc: e.activation(
                out=BFBf[0:1, tc * 512: tc * 512 + 512], in_=psum[0][0:1, 0:512],
                func=AF.Sigmoid, bias=colf[0:1, 35:36]), inc="sv")
            p.wait("tensor", "sv", sv)
        p.drain("scalar")
        p.op("scalar", lambda e: e.activation(
            out=BFBf[0:1, 2048:2048 + S], in_=BFBf[0:1, 0:S], func=AF.Exp,
            scale=-float(np.log(2046.0)), bias=colf[0:1, 36:37]), inc="sv")
        sv = p.val("sv")
        p.wait("sync", "sv", sv)
        wv = p.dma("sync", SCR["rows8"][0:1, :], BFBf[0:1, 0:S])
        wv = p.dma("sync", SCR["rows8"][1:2, :], BFBf[0:1, 2048:2048 + S], cont=True)
        p.wait("sync", wv)
        if KDUMP == "g":
            p.dma("sync", dbg[0:2, :], SCR["rows8"][0:2, :])
        p.dma("sync", BFBf[:, 2048:4096], SCR["rows8"][0:1, :].broadcast_to([128, S]))
        bv = p.dma("sync", BFBf[:, 4096:6144], SCR["rows8"][1:2, :].broadcast_to([128, S]),
                   cont=True)
        p.wait("gpsimd", wv)
        p.op("gpsimd", lambda e: e.iota(out=BFBf[:, 0:2048], pattern=[[1, 2048]], base=0,
                                        channel_multiplier=0,
                                        allow_small_or_imprecise_dtypes=True), inc="gv")
        p.op("gpsimd", lambda e: e.iota(out=colf[:, 37:38], pattern=[[1, 1]], base=0,
                                        channel_multiplier=1,
                                        allow_small_or_imprecise_dtypes=True), inc="gv")
        gvv = p.val("gv")
        p.wait("vector", "gv", gvv)
        p.wait("vector", bv)
        for kc in range(16):
            c1 = p.dma("sync", colf[:, 38:39],
                       SCR["rows8"][0:1, kc * 128:(kc + 1) * 128].transpose([1, 0]))
            p.wait("vector", c1)
            p.op("vector", lambda e, kc=kc: e.tensor_scalar(
                out=colf[:, 39:40], in0=colf[:, 37:38], scalar1=float(kc * 128),
                scalar2=None, op0=ALU.add), inc="vv")
            p.drain("vector")
            p.op("vector", lambda e: e.tensor_scalar(
                out=BFBf[:, 6144:8192], in0=BFBf[:, 0:2048], scalar1=-1.0,
                scalar2=colf[:, 39:40], op0=ALU.mult, op1=ALU.add), inc="vv")
            p.drain("vector")
            p.op("vector", lambda e: e.scalar_tensor_tensor(
                out=BFBf[:, 6144:8192], in0=BFBf[:, 6144:8192], scalar=-1.0,
                in1=BFBf[:, 6144:8192], op0=ALU.mult, op1=ALU.max), inc="vv")
            p.drain("vector")
            p.op("vector", lambda e: e.tensor_scalar(
                out=AF32[:, 0:2048], in0=BFBf[:, 6144:8192], scalar1=-1.0, scalar2=2.0,
                op0=ALU.mult, op1=ALU.add), inc="vv")
            p.drain("vector")
            p.op("vector", lambda e: e.tensor_tensor(
                out=AF32[:, 0:2048], in0=AF32[:, 0:2048], in1=BFBf[:, 4096:6144],
                op=ALU.add), inc="vv")
            p.drain("vector")
            p.op("vector", lambda e: e.tensor_scalar(
                out=AF32[:, 0:2048], in0=AF32[:, 0:2048], scalar1=1e-32, scalar2=1.0,
                op0=ALU.max, op1=ALU.min), inc="vv")
            p.drain("vector")
            p.op("vector", lambda e: e.tensor_scalar(
                out=colf[:, 40:41], in0=colf[:, 38:39], scalar1=-1.0, scalar2=1.0,
                op0=ALU.mult, op1=ALU.add), inc="vv")
            p.op("vector", lambda e: e.tensor_scalar(
                out=colf[:, 41:42], in0=colf[:, 38:39], scalar1=1.0, scalar2=None,
                op0=ALU.add), inc="vv")
            p.drain("vector")
            p.op("vector", lambda e: e.tensor_scalar(
                out=BFBf[:, 6144:8192], in0=BFBf[:, 2048:4096], scalar1=-1.0,
                scalar2=colf[:, 40:41], op0=ALU.mult, op1=ALU.add), inc="vv")
            p.drain("vector")
            p.op("vector", lambda e: e.tensor_scalar(
                out=BFBf[:, 6144:8192], in0=BFBf[:, 6144:8192], scalar1=0.0, scalar2=None,
                op0=ALU.max), inc="vv")
            p.drain("vector")
            p.op("vector", lambda e: e.tensor_scalar(
                out=A2(), in0=BFBf[:, 2048:4096], scalar1=-1.0, scalar2=1.0,
                op0=ALU.mult, op1=ALU.add), inc="vv")
            p.drain("vector")
            p.op("vector", lambda e: e.tensor_tensor(
                out=BFBf[:, 6144:8192], in0=BFBf[:, 6144:8192], in1=A2(), op=ALU.mult), inc="vv")
            p.drain("vector")
            p.op("vector", lambda e: e.tensor_scalar(
                out=A2(), in0=BFBf[:, 2048:4096], scalar1=-1.0,
                scalar2=colf[:, 41:42], op0=ALU.mult, op1=ALU.add), inc="vv")
            p.drain("vector")
            p.op("vector", lambda e: e.tensor_scalar(
                out=A2(), in0=A2(), scalar1=1.0, scalar2=None, op0=ALU.min), inc="vv")
            p.drain("vector")
            p.op("vector", lambda e: e.tensor_tensor(
                out=A2(), in0=A2(), in1=BFBf[:, 2048:4096], op=ALU.mult), inc="vv")
            p.drain("vector")
            p.op("vector", lambda e: e.tensor_tensor(
                out=BFBf[:, 6144:8192], in0=BFBf[:, 6144:8192], in1=A2(), op=ALU.add), inc="vv")
            p.drain("vector")
            p.op("vector", lambda e: e.tensor_tensor(
                out=BFBf[:, 6144:8192], in0=BFBf[:, 6144:8192], in1=AF32[:, 0:2048],
                op=ALU.mult), inc="vv")
            p.drain("vector")
            p.op("vector", lambda e: e.tensor_scalar(
                out=Db[:, 0:2048], in0=BFBf[:, 6144:8192], scalar1=1e-32, scalar2=None,
                op0=ALU.max), inc="vv")
            vvp = p.val("vv")
            p.wait("sync", "vv", vvp)
            sv2 = p.dma("sync", SCR["penf"][kc], Db[:, 0:2048])
            p.wait("vector", sv2)
        p.join()
        if KDUMP == "pen":
            for kc in range(8):
                v = p.dma("sync", Db[:, 0:2048], SCR["penf"][kc])
                p.wait("vector", v)
                p.op("vector", lambda e: e.tensor_copy(out=AF32[:, 0:2048], in_=Db[:, 0:2048]), inc="vv")
                vv = p.val("vv")
                p.wait("sync", "vv", vv)
                sv = p.dma("sync", dbg[kc * 128:(kc + 1) * 128, :], AF32[:, 0:2048])
                p.wait("vector", sv)
                p.wait("sync", sv)
        vw = load_w_bf16(W0, "l1_wv", D, 8)
        p.wait("tensor", vw)
        p.join()
        wv = p.dma("sync", SCR["rows8"][2:3, 0:D], I["l1_bv_row"])
        p.wait("sync", wv)
        vb = p.dma("sync", AF32[:, 1024:2048], SCR["rows8"][2:3, 0:D].broadcast_to([128, D]))
        p.wait("vector", vb)
        for tc in range(16):
            for og in range(2):
                bk = og
                bank_wait(bk)
                for ic in range(8):
                    p.op("tensor", lambda e, tc=tc, og=og, ic=ic, bk=bk: e.matmul(
                        psum[bk][:, 0:512],
                        xb(ic)[:, tc * 128: tc * 128 + 128],
                        W0[:, ic * D + og * 512: ic * D + og * 512 + 512],
                        start=(ic == 0), stop=(ic == 7)), inc="tv")
                tvv = p.val("tv")
                p.wait("vector", "tv", tvv)
                bank_set(bk, p.op("vector", lambda e, tc=tc, og=og, bk=bk: e.tensor_tensor(
                    out=BFB[:, tc * 1024 + og * 512: tc * 1024 + og * 512 + 512],
                    in0=psum[bk][:, 0:512], in1=AF32[:, 1024 + og * 512: 1024 + og * 512 + 512],
                    op=ALU.add), inc="vv"))
        vvV = p.val("vv")
        p.wait("sync", "vv", vvV)
        for tc in range(16):
            p.dma("sync", SCR["vtok"][tc], BFB[:, tc * 1024:(tc + 1) * 1024])
        p.join()
        qw = load_w_bf16(W0, "l1_wq", D, 8)
        p.wait("tensor", qw)
        proj_feat(W0, 0, qTb)
        p.join()
        kw = load_w_bf16(W0, "l1_wk", D, 8)
        p.wait("tensor", kw)
        proj_feat(W0, 8, kTb, psb=2)
        p.join()
        headnorms(kTb, 0)
        p.join()
        p.op("vector", lambda e: e.tensor_reduce(out=colf[0:16, 33:34], in_=rows[0:16, 0:S],
                                                 axis=AX.X, op=ALU.max), inc="vv")
        p.join()
        headnorms(qTb, 0)
        p.join()
        p.op("vector", lambda e: e.tensor_scalar(
            out=rows[0:16, 0:S], in0=rows[0:16, 0:S],
            scalar1=colf[0:16, 33:34], scalar2=-1.0, op0=ALU.mult, op1=ALU.mult), inc="vv")
        p.join()
        W0f = W0[:].bitcast(F32)
        p.op("gpsimd", lambda e: e.memset(Db[64:65, 2048:4096], 1.0), inc="gv")
        p.wait("sync", "gv", p.val("gv"))
        probsT = lambda kc: W1[:, kc * 512:(kc + 1) * 512]
        PG = {}
        for h in range(16):
            cc, hh = h // 2, h % 2
            p.wait("sync", "tv", p.val("tv"))
            p.wait("gpsimd", "tv", p.val("tv"))
            c1 = p.dma("sync", Db[0:64, 0:S], qTb(cc)[hh * 64: hh * 64 + 64, :])
            c3 = p.dma("sync", Db[0:64, 2048:2048 + S], kTb(cc)[hh * 64: hh * 64 + 64, :],
                       cont=True)
            c2 = p.dma("gpsimd", Db[64:65, 0:S], rows[h:h + 1, 0:S])
            vl = None
            for kc in range(16):
                vl = p.dma("gpsimd", W0[:, 6144 + kc * 64: 6144 + kc * 64 + 64],
                           SCR["vtok"][kc][:, h * 64:(h + 1) * 64], cont=(kc > 0))
            p.wait("tensor", c3)
            p.wait("tensor", c2)
            p.wait("tensor", vl)
            ptok = {}
            def pen_load(qc, kc):
                sl = kc % 4
                q = "sync" if kc % 2 else "gpsimd"
                p.wait(q, "vv", PG.get(sl, 0))
                ptok[kc] = p.dma(q, W0[:, sl * 512: sl * 512 + 512],
                                 SCR["penf"][kc][:, qc * 512:(qc + 1) * 512])
            for qc in range(4):
                for kc in range(4):
                    pen_load(qc, kc)
                for half in range(4):
                  for k8 in range(4):
                    kc = half * 4 + k8
                    bk = kc % 2
                    sl = kc % 4
                    tvv = p.op("tensor", lambda e, qc=qc, kc=kc, bk=bk: e.matmul(
                        psum[bk][:, 0:512],
                        Db[0:65, 2048 + kc * 128: 2048 + kc * 128 + 128],
                        Db[0:65, qc * 512: qc * 512 + 512], start=True, stop=True), inc="tv")
                    p.wait("scalar", "tv", tvv)
                    sv = p.op("scalar", lambda e, k8=k8, bk=bk: e.activation(
                        out=probsT(k8), in_=psum[bk][:, 0:512], func=AF.Exp, scale=0.125),
                        inc="sv")
                    p.wait("tensor", "sv", sv)
                    p.wait("vector", ptok.pop(kc))
                    p.wait("vector", "sv", sv)
                    vvp = p.op("vector", lambda e, k8=k8, sl=sl: e.tensor_tensor(
                        out=probsT(k8), in0=probsT(k8),
                        in1=W0[:, sl * 512: sl * 512 + 512], op=ALU.mult), inc="vv")
                    PG[sl] = vvp
                    if kc + 4 < 16:
                        pen_load(qc, kc + 4)
                  p.wait("tensor", "sv", p.val("sv"))
                  p.wait("tensor", "vv", p.val("vv"))
                  for k8 in range(4):
                    kc = half * 4 + k8
                    p.op("tensor", lambda e, k8=k8, kc=kc: e.matmul(
                        psum[2][0:64, 0:512], W0[:, 6144 + kc * 64: 6144 + kc * 64 + 64],
                        probsT(k8), start=(kc == 0), stop=(kc == 15)), inc="tv")
                    p.op("tensor", lambda e, k8=k8, kc=kc: e.matmul(
                        psum[3][0:1, 0:512], onesb[:, 0:1],
                        probsT(k8), start=(kc == 0), stop=(kc == 15)), inc="tv")
                  p.wait("tensor", "tv", p.val("tv"))
                tvv = p.val("tv")
                p.wait("vector", "tv", tvv)
                p.op("vector", lambda e, hh=hh, qc=qc: e.tensor_copy(
                    out=AF32[hh * 64: hh * 64 + 64, qc * 512: qc * 512 + 512],
                    in_=psum[2][0:64, 0:512]), inc="vv")
                vv = p.op("vector", lambda e, hh=hh, qc=qc: e.reciprocal(
                    out=rows[32 + hh * 32: 33 + hh * 32, qc * 512: qc * 512 + 512],
                    in_=psum[3][0:1, 0:512]), inc="vv")
                p.wait("tensor", "vv", vv)
            if hh == 1:
                p.drain("vector")
                vv = p.val("vv")
                p.wait("sync", "vv", vv)
                p.dma("sync", SCR["rows8"][3:4, :], rows[32:33, 0:S])
                wv2 = p.dma("sync", SCR["rows8"][4:5, :], rows[64:65, 0:S], cont=True)
                p.wait("sync", wv2)
                p.wait("sync", "vv", p.val("vv"))
                v1 = p.dma("sync", W0f[0:64, 1024:3072],
                           SCR["rows8"][3:4, :].broadcast_to([64, S]))
                v2 = p.dma("sync", W0f[64:128, 1024:3072],
                           SCR["rows8"][4:5, :].broadcast_to([64, S]), cont=True)
                p.wait("vector", v2)
                p.op("vector", lambda e: e.tensor_tensor(
                    out=psZ[:, 0:2048], in0=AF32[:, 0:2048],
                    in1=W0f[:, 1024:3072], op=ALU.mult), inc="vv")
                p.drain("vector")
                p.op("vector", lambda e, cc=cc: e.tensor_tensor(
                    out=xr(cc), in0=xr(cc), in1=psZ[:, 0:2048], op=ALU.add), inc="vv")
                p.drain("vector")
        ln_epilogue()

    def l1_ffn_layer(l):
        pcast_tok = None
        for ic in range(8):
            p.dma("gpsimd", SCR["w1b"][l][ic * 128:(ic + 1) * 128, :],
                  I["l1f_w1"][l][ic * 128:(ic + 1) * 128, :], cont=(ic > 0))
        for jc in range(22):
            pcast_tok = p.dma("gpsimd", SCR["w2b"][l][jc * 128:(jc + 1) * 128, :],
                              I["l1f_w2"][l][jc * 128:(jc + 1) * 128, :], cont=True)
        p.join()
        for dc in range(4):
            p.op("vector", lambda e, dc=dc: e.tensor_copy(out=BFA[:, dc * S:(dc + 1) * S],
                                                          in_=xr(dc)), inc="vv")
        for dc in range(4, 8):
            p.op("gpsimd", lambda e, dc=dc: e.memset(BFA[:, dc * S: dc * S + 1], 0.0), inc="gv")
            p.op("vector", lambda e, dc=dc: e.tensor_copy(
                out=BFA[:, dc * S + 1:(dc + 1) * S], in_=xr(dc)[:, 0:S - 1]), inc="vv")
        p.join()
        hb = lambda ic: BFA[:, ic * S:(ic + 1) * S]
        vw = None
        for ic in range(8):
            vw = p.dma("gpsimd", W0[:, ic * 512:(ic + 1) * 512],
                       I["l1f_wqkv"][l][ic * 128:(ic + 1) * 128, 1024:1536], cont=(ic > 0))
        p.wait("tensor", vw)
        for tc in range(16):
            bank_wait(0)
            for ic in range(8):
                p.op("tensor", lambda e, tc=tc, ic=ic: e.matmul(
                    psum[0][:, 0:512],
                    hb(ic)[:, tc * 128: tc * 128 + 128],
                    W0[:, ic * 512:(ic + 1) * 512],
                    start=(ic == 0), stop=(ic == 7)), inc="tv")
            tvv = p.val("tv")
            p.wait("vector", "tv", tvv)
            bank_set(0, p.op("vector", lambda e, tc=tc: e.tensor_copy(
                out=BFB[:, tc * 512:(tc + 1) * 512], in_=psum[0][:, 0:512]), inc="vv"))
        vvV = p.val("vv")
        p.wait("sync", "vv", vvV)
        for tc in range(16):
            p.dma("sync", SCR["vtok"][tc][:, 0:512], BFB[:, tc * 512:(tc + 1) * 512])
        p.join()
        qw = None
        for ic in range(8):
            qw = p.dma("gpsimd", W0[:, ic * 1024:(ic + 1) * 1024],
                       I["l1f_wqkv"][l][ic * 128:(ic + 1) * 128, 0:1024], cont=(ic > 0))
        p.wait("tensor", qw)
        qf = lambda c: BFAf[:, 8192 + c * S: 8192 + (c + 1) * S]
        kf = lambda c: BFBf[:, c * S:(c + 1) * S]
        for half, dstf in ((0, qf), (1, kf)):
            for oc in range(4):
                for tc in range(4):
                    bk = (oc * 4 + tc) % 2
                    bank_wait(bk)
                    for ic in range(8):
                        p.op("tensor", lambda e, half=half, oc=oc, tc=tc, ic=ic, bk=bk: e.matmul(
                            psum[bk][:, 0:512],
                            W0[:, ic * 1024 + half * 512 + oc * 128:
                                  ic * 1024 + half * 512 + oc * 128 + 128],
                            hb(ic)[:, tc * 512: tc * 512 + 512],
                            start=(ic == 0), stop=(ic == 7)), inc="tv")
                    tvv = p.val("tv")
                    p.wait("vector", "tv", tvv)
                    bank_set(bk, p.op("vector", lambda e, dstf=dstf, oc=oc, tc=tc, bk=bk:
                        e.tensor_copy(out=dstf(oc)[:, tc * 512: tc * 512 + 512],
                                      in_=psum[bk][:, 0:512]), inc="vv"))
        p.join()
        def l2norm_scale(srcf, scol_name, row8):
            for tc in range(4):
                for c in range(4):
                    sl = c % 2
                    p.wait("scalar", "tv", p.val("tv"))
                    sq = p.op("scalar", lambda e, c=c, tc=tc, sl=sl: e.activation(
                        out=AF32[:, sl * 512: sl * 512 + 512],
                        in_=srcf(c)[:, tc * 512: tc * 512 + 512], func=AF.Square), inc="sv")
                    p.wait("tensor", "sv", sq)
                    p.op("tensor", lambda e, c=c, tc=tc, sl=sl: e.matmul(
                        psum[3][0:8, 0:512], hones32[:, c * 16: c * 16 + 8],
                        AF32[:, sl * 512: sl * 512 + 512],
                        start=(c == 0), stop=(c == 3)), inc="tv")
                tvv = p.val("tv")
                p.wait("scalar", "tv", tvv)
                sv = p.op("scalar", lambda e, tc=tc: e.activation(
                    out=rows[0:8, tc * 512: tc * 512 + 512],
                    in_=psum[3][0:8, 0:512], func=AF.Sqrt), inc="sv")
                p.wait("tensor", "sv", sv)
                p.wait("vector", "sv", sv)
            p.op("vector", lambda e: e.tensor_scalar(out=rows[0:8, 0:S], in0=rows[0:8, 0:S],
                                                     scalar1=1e-12, scalar2=None,
                                                     op0=ALU.max), inc="vv")
            p.drain("vector")
            p.op("vector", lambda e: e.reciprocal(out=rows[0:8, 0:S], in_=rows[0:8, 0:S]), inc="vv")
            vv = p.val("vv")
            p.wait("sync", "vv", vv)
            wv = p.dma("sync", SCR["rows8"][row8:row8 + 8, :], rows[0:8, 0:S])
            p.wait("sync", wv)
            p.wait("scalar", wv)
            p.wait("vector", wv)
        l2norm_scale(qf, "qs", 0)
        def apply_scale(srcf, row8, scol):
            for c in range(4):
                p.wait("sync", "vv", p.val("vv"))
                rep = p.dma("sync", Dbf[:, 0:2048],
                            SCR["rows8"][row8 + 2 * c: row8 + 2 * c + 2, :]
                            .unsqueeze(1).broadcast_to([2, 64, S]))
                p.wait("vector", rep)
                p.op("vector", lambda e, srcf=srcf, c=c, scol=scol: e.scalar_tensor_tensor(
                    out=srcf(c), in0=srcf(c), scalar=scol, in1=Dbf[:, 0:2048],
                    op0=ALU.mult, op1=ALU.mult), inc="vv")
                p.drain("vector")
                vv = p.val("vv")
                p.wait("sync", "vv", vv)
        qscol = colf[:, 44:45]
        kscol = colf[:, 45:46]
        p.dma("sync", qscol, I["l1f_qs"][l])
        kv1 = p.dma("sync", kscol, I["l1f_ks"][l], cont=True)
        p.wait("vector", kv1)
        apply_scale(qf, 0, qscol)
        l2norm_scale(kf, "ks", 0)
        apply_scale(kf, 0, kscol)
        p.join()
        qb = lambda c: BFA[:, c * S:(c + 1) * S]
        kb = lambda c: BFA[:, 8192 + c * S: 8192 + (c + 1) * S]
        pw = p.dma("sync", sstat[:, 0:128], I["c_perm_half"])
        for (srcf, dstf, tcos, tsin) in ((qf, qb, "c_xpos_qc", "c_xpos_qs"),
                                         (kf, kb, "c_xpos_kc", "c_xpos_ks")):
            p.wait("sync", "vv", p.val("vv"))
            t1 = p.dma("sync", AF32[:, 0:2048], I[tcos])
            t2 = p.dma("sync", rows[:, 0:2048], I[tsin], cont=True)
            p.wait("tensor", t2)
            p.wait("tensor", pw)
            p.wait("vector", t2)
            for c in range(4):
                for tc in range(4):
                    bk = 2 + (tc % 2)
                    bank_wait(bk)
                    p.op("tensor", lambda e, srcf=srcf, c=c, tc=tc, bk=bk: e.matmul(
                        psum[bk][:, 0:512], sstat[:, 0:128],
                        srcf(c)[:, tc * 512: tc * 512 + 512],
                        start=True, stop=True), inc="tv")
                    tvv = p.val("tv")
                    p.wait("vector", "tv", tvv)
                    p.op("vector", lambda e, srcf=srcf, c=c, tc=tc: e.tensor_tensor(
                        out=Dbf[:, 0:512],
                        in0=srcf(c)[:, tc * 512: tc * 512 + 512],
                        in1=AF32[:, tc * 512: tc * 512 + 512], op=ALU.mult), inc="vv")
                    p.drain("vector")
                    p.op("vector", lambda e, c=c, tc=tc, bk=bk: e.tensor_tensor(
                        out=psum[bk][:, 0:512], in0=psum[bk][:, 0:512],
                        in1=rows[:, tc * 512: tc * 512 + 512], op=ALU.mult), inc="vv")
                    p.drain("vector")
                    vv = p.op("vector", lambda e, dstf=dstf, c=c, tc=tc, bk=bk: e.tensor_tensor(
                        out=dstf(c)[:, tc * 512: tc * 512 + 512],
                        in0=Dbf[:, 0:512],
                        in1=psum[bk][:, 0:512], op=ALU.add), inc="vv")
                    p.drain("vector")
                    bank_set(bk, vv)
        p.join()
        return hb, qb, kb, pcast_tok

    def l1_ffn():
        p.join()
        for dc in range(8):
            p.dma("sync", SCR["h0"][dc], xr(dc), cont=(dc > 0))
        for c in range(4):
            p.dma("sync", hones32[:, c * 16:(c + 1) * 16], I["c_hones16"][c])
        p.join()
        for l in range(2):
            hbf, qb, kb, pcast_tok = l1_ffn_layer(l)
            if KDUMP == "l1fq" and l == 0:
                dump_bf(qb, 4)
            if KDUMP == "l1fk" and l == 0:
                dump_bf(kb, 4)
            q1 = p.dma("sync", rows[0:1, 0:64], I["l1f_qsr"][l])
            k1 = p.dma("sync", rows[0:1, 64:128], I["l1f_ksr"][l], cont=True)
            p.wait("vector", k1)
            p.op("vector", lambda e: e.tensor_reduce(out=rows[0:1, 128:129], in_=rows[0:1, 0:64],
                                                     axis=AX.X, op=ALU.max,
                                                     apply_absolute_value=True), inc="vv")
            p.op("vector", lambda e: e.tensor_reduce(out=rows[0:1, 129:130], in_=rows[0:1, 64:128],
                                                     axis=AX.X, op=ALU.max,
                                                     apply_absolute_value=True), inc="vv")
            p.drain("vector")
            p.op("vector", lambda e: e.tensor_tensor(out=rows[0:1, 130:131], in0=rows[0:1, 128:129],
                                                     in1=rows[0:1, 129:130], op=ALU.mult), inc="vv")
            p.drain("vector")
            p.op("vector", lambda e: e.tensor_scalar(out=rows[0:1, 130:131], in0=rows[0:1, 130:131],
                                                     scalar1=-8.0, scalar2=None, op0=ALU.mult),
                 inc="vv")
            vv = p.val("vv")
            p.wait("sync", "vv", vv)
            mv = p.dma("sync", SCR["rows8"][7:8, 0:1], rows[0:1, 130:131])
            p.wait("sync", mv)
            mv2 = p.dma("sync", colf[:, 46:47], SCR["rows8"][7:8, 0:1].broadcast_to([128, 1]))
            p.wait("scalar", mv2)
            oT = lambda c: BFBf[:, c * S:(c + 1) * S]
            lmaskb = W0
            lv = None
            for kc in range(8):
                lv = p.dma("gpsimd", lmaskb[:, kc * 512:(kc + 1) * 512],
                           I["c_lmask"][kc * 128:(kc + 1) * 128, :], cont=(kc > 0))
            p.wait("vector", lv)
            p.wait("tensor", lv)
            for h in range(8):
                cc, hh = h // 2, h % 2
                for w in range(4):
                    kc_lo = 4 if w == 0 else 0
                    for half in range(2):
                        kcs = [kc for kc in range(kc_lo, 8) if kc % 2 == half] if False                             else list(range(kc_lo + half * ((8 - kc_lo) // 2), 8))
                        pass
                    kcl = list(range(kc_lo, 8))
                    for gi in range(0, len(kcl), 4):
                        grp = kcl[gi:gi + 4]
                        for j, kc in enumerate(grp):
                            bk = kc % 2
                            kcol = (w - 1) * 512 + kc * 128
                            tvv = p.op("tensor", lambda e, cc=cc, hh=hh, w=w, bk=bk,
                                       kcol=kcol: e.matmul(
                                psum[bk][:, 0:512],
                                kb(cc)[hh * 64: hh * 64 + 64, kcol: kcol + 128],
                                qb(cc)[hh * 64: hh * 64 + 64, w * 512:(w + 1) * 512],
                                start=True, stop=True), inc="tv")
                            p.wait("scalar", "tv", tvv)
                            sv = p.op("scalar", lambda e, j=j, bk=bk: e.activation(
                                out=W1[:, j * 512:(j + 1) * 512], in_=psum[bk][:, 0:512],
                                func=AF.Exp, scale=8.0, bias=colf[:, 46:47]), inc="sv")
                            p.wait("vector", "sv", sv)
                            p.wait("tensor", "sv", sv)
                            p.op("vector", lambda e, j=j, kc=kc: e.tensor_tensor(
                                out=W1[:, j * 512:(j + 1) * 512],
                                in0=W1[:, j * 512:(j + 1) * 512],
                                in1=lmaskb[:, kc * 512:(kc + 1) * 512], op=ALU.mult), inc="vv")
                        p.wait("tensor", "vv", p.val("vv"))
                        for j, kc in enumerate(grp):
                            kcol = (w - 1) * 512 + kc * 128
                            tci = (w - 1) * 4 + kc
                            sl = kc % 2
                            p.wait("sync", "tv", p.val("tv"))
                            vl = p.dma("sync", W0[:, 4096 + sl * 128: 4096 + sl * 128 + 64],
                                       SCR["vtok"][tci][:, h * 64:(h + 1) * 64])
                            p.wait("tensor", vl)
                            first = (kc == kc_lo)
                            last = (kc == 7)
                            p.op("tensor", lambda e, j=j, sl=sl, first=first, last=last: e.matmul(
                                psum[2][0:64, 0:512], W0[:, 4096 + sl * 128: 4096 + sl * 128 + 64],
                                W1[:, j * 512:(j + 1) * 512], start=first, stop=last), inc="tv")
                            p.op("tensor", lambda e, j=j, first=first, last=last: e.matmul(
                                psum[3][0:1, 0:512], onesb[:, 0:1],
                                W1[:, j * 512:(j + 1) * 512], start=first, stop=last), inc="tv")
                        p.wait("tensor", "tv", p.val("tv"))
                    tvv = p.val("tv")
                    p.wait("vector", "tv", tvv)
                    p.op("vector", lambda e, hh=hh, h=h, w=w, cc=cc: e.tensor_copy(
                        out=oT(cc)[hh * 64: hh * 64 + 64, w * 512:(w + 1) * 512],
                        in_=psum[2][0:64, 0:512]), inc="vv")
                    vvr = p.op("vector", lambda e, w=w: e.reciprocal(
                        out=rows[64:65, w * 512: w * 512 + 512],
                        in_=psum[3][0:1, 0:512]), inc="vv")
                    p.wait("tensor", "vv", vvr)
                p.drain("vector")
                vv = p.val("vv")
                p.wait("sync", "vv", vv)
                dn = p.dma("sync", SCR["rows8"][h:h + 1, :], rows[64:65, 0:S])
                p.wait("sync", dn)
            for c in range(4):
                p.wait("sync", "vv", p.val("vv"))
                rep = p.dma("sync", Dbf[:, 0:2048],
                            SCR["rows8"][2 * c: 2 * c + 2, :].unsqueeze(1)
                            .broadcast_to([2, 64, S]))
                p.wait("vector", rep)
                p.op("vector", lambda e, c=c: e.tensor_tensor(
                    out=BFA[:, c * S:(c + 1) * S], in0=oT(c), in1=Dbf[:, 0:2048],
                    op=ALU.mult), inc="vv")
                p.drain("vector")
            if KDUMP == "l1fo" and l == 0:
                dump_bf(lambda c: BFA[:, c * S:(c + 1) * S], 4)
            p.join()
            ow = None
            for ic in range(4):
                ow = p.dma("gpsimd", W0[:, ic * 1024:(ic + 1) * 1024],
                           I["l1f_wo"][l][ic * 128:(ic + 1) * 128, :], cont=(ic > 0))
            p.wait("tensor", ow)
            for oc in range(8):
                for tc in range(4):
                    bk = (oc * 4 + tc) % 2
                    for ic in range(4):
                        p.op("tensor", lambda e, oc=oc, tc=tc, ic=ic, bk=bk: e.matmul(
                            psum[bk][:, 0:512],
                            W0[:, ic * 1024 + oc * 128: ic * 1024 + oc * 128 + 128],
                            BFA[:, ic * S + tc * 512: ic * S + tc * 512 + 512],
                            start=(ic == 0), stop=(ic == 3)), inc="tv")
                    tvv = p.val("tv")
                    p.wait("vector", "tv", tvv)
                    vv = p.op("vector", lambda e, oc=oc, tc=tc, bk=bk: e.tensor_tensor(
                        out=xr(oc)[:, tc * 512: tc * 512 + 512], in0=psum[bk][:, 0:512],
                        in1=xr(oc)[:, tc * 512: tc * 512 + 512], op=ALU.add), inc="vv")
                    p.wait("tensor", "vv", vv)
            if KDUMP == "l1fx" and l == 0:
                dump_bf(xr)
            p.join()
            lgv = p.dma("sync", rows[:, 1024:1032], I["l1f_lng"][l])
            lbv = p.dma("sync", rows[:, 1032:1040], I["l1f_lnb"][l], cont=True)
            b1av = p.dma("sync", rows[:, 1040:1062], I["l1f_b1a"][l], cont=True)
            b1gv = p.dma("sync", rows[:, 1062:1084], I["l1f_b1g"][l], cont=True)
            b2v = p.dma("sync", rows[:, 1084:1092], I["l1f_b2"][l], cont=True)
            p.wait("vector", b2v)
            p.wait("scalar", b2v)
            def build_hsh(dc):
                p.wait("vector", "sv", p.val("sv"))
                p.wait("vector", "tv", p.val("tv"))
                if dc < 4:
                    p.op("vector", lambda e, dc=dc: e.tensor_copy(out=AF32[:, 0:2048],
                                                                  in_=xr(dc)), inc="vv")
                else:
                    p.drain("gpsimd")
                    p.wait("gpsimd", "vv", p.val("vv"))
                    p.wait("gpsimd", "sv", p.val("sv"))
                    p.wait("gpsimd", "tv", p.val("tv"))
                    p.op("gpsimd", lambda e: e.memset(AF32[:, 0:1], 0.0), inc="gv")
                    p.wait("vector", "gv", p.val("gv"))
                    p.op("vector", lambda e, dc=dc: e.tensor_copy(
                        out=AF32[:, 1:2048], in_=xr(dc)[:, 0:S - 1]), inc="vv")
                p.drain("vector")
            for tc in range(4):
                for dc in range(8):
                    build_hsh(dc)
                    vvh = p.val("vv")
                    p.wait("scalar", "vv", vvh)
                    sq = p.op("scalar", lambda e, tc=tc: e.activation(
                        out=BFAf[:, 0:512],
                        in_=AF32[:, tc * 512: tc * 512 + 512], func=AF.Square), inc="sv")
                    p.wait("tensor", "sv", sq)
                    p.wait("tensor", "vv", vvh)
                    p.op("tensor", lambda e, dc=dc, tc=tc: e.matmul(
                        psum[0][0:1, 0:512], ones32[:, 0:1],
                        AF32[:, tc * 512: tc * 512 + 512],
                        start=(dc == 0), stop=(dc == 7)), inc="tv")
                    tvq = p.op("tensor", lambda e, dc=dc, tc=tc: e.matmul(
                        psum[1][0:1, 0:512], ones32[:, 0:1],
                        BFAf[:, 0:512],
                        start=(dc == 0), stop=(dc == 7)), inc="tv")
                    p.wait("scalar", "tv", tvq)
                    p.wait("vector", "tv", tvq)
                tvv = p.val("tv")
                p.wait("vector", "tv", tvv)
                p.op("vector", lambda e, tc=tc: e.tensor_scalar(
                    out=BFBf[0:1, tc * 512:tc * 512 + 512], in0=psum[0][0:1, 0:512],
                    scalar1=1.0 / 1024.0, scalar2=None, op0=ALU.mult), inc="vv")
                p.op("vector", lambda e, tc=tc: e.tensor_scalar(
                    out=BFBf[0:1, 2048 + tc * 512: 2048 + tc * 512 + 512], in0=psum[1][0:1, 0:512],
                    scalar1=1.0 / 1024.0, scalar2=None, op0=ALU.mult), inc="vv")
                vv = p.val("vv")
                p.wait("tensor", "vv", vv)
            p.drain("vector")
            MEAN = BFBf[0:1, 0:S]; M2 = BFBf[0:1, 2048:2048 + S]
            AV_ = BFBf[0:1, 4096:4096 + S]; BV_ = BFBf[0:1, 6144:6144 + S]
            p.op("vector", lambda e: e.scalar_tensor_tensor(
                out=AV_, in0=MEAN, scalar=-1.0, in1=MEAN, op0=ALU.mult, op1=ALU.mult))
            p.drain("vector")
            p.op("vector", lambda e: e.tensor_tensor(out=AV_, in0=AV_, in1=M2, op=ALU.add), inc="vv")
            p.drain("vector")
            p.wait("scalar", "vv", p.val("vv"))
            p.op("scalar", lambda e: e.activation(out=AV_, in_=AV_, func=AF.Sqrt,
                                                  bias=colf[0:1, 62:63]), inc="sv")
            p.wait("vector", "sv", p.val("sv"))
            p.op("vector", lambda e: e.reciprocal(out=AV_, in_=AV_))
            p.drain("vector")
            p.op("vector", lambda e: e.scalar_tensor_tensor(
                out=BV_, in0=MEAN, scalar=-1.0, in1=AV_, op0=ALU.mult, op1=ALU.mult), inc="vv")
            vv = p.val("vv")
            p.wait("sync", "vv", vv)
            p.dma("sync", SCR["rows8"][0:1, :], AV_)
            wv = p.dma("sync", SCR["rows8"][1:2, :], BV_, cont=True)
            p.wait("sync", wv)
            p.dma("sync", BFBf[:, 0:2048], SCR["rows8"][0:1, :].broadcast_to([128, S]))
            vb = p.dma("sync", BFBf[:, 2048:4096], SCR["rows8"][1:2, :].broadcast_to([128, S]),
                       cont=True)
            p.wait("vector", vb)
            hb2 = lambda dc: BFA[:, 16384 + dc * S: 16384 + (dc + 1) * S]
            for dc in range(8):
                build_hsh(dc)
                p.op("vector", lambda e, dc=dc: e.scalar_tensor_tensor(
                    out=AF32[:, 0:2048], in0=AF32[:, 0:2048], scalar=rows[:, 1024 + dc: 1024 + dc + 1],
                    in1=BFBf[:, 0:2048], op0=ALU.mult, op1=ALU.mult), inc="vv")
                p.drain("vector")
                p.op("vector", lambda e, dc=dc: e.scalar_tensor_tensor(
                    out=BFBf[:, 4096:6144], in0=BFBf[:, 2048:4096],
                    scalar=rows[:, 1024 + dc: 1024 + dc + 1],
                    in1=AF32[:, 0:2048], op0=ALU.mult, op1=ALU.add), inc="vv")
                p.drain("vector")
                p.op("vector", lambda e, dc=dc: e.tensor_scalar(
                    out=hb2(dc), in0=BFBf[:, 4096:6144],
                    scalar1=rows[:, 1032 + dc: 1032 + dc + 1], scalar2=None, op0=ALU.add), inc="vv")
                p.drain("vector")
            if KDUMP == "l1fh" and l == 0:
                dump_bf(hb2)
            p.join()
            prod = lambda j: BFA[:, j * 512:(j + 1) * 512]
            w1base = SCR["w1b"][l]
            w2base = SCR["w2b"][l]
            WG = {}
            wtok = {}
            W2G = {}
            w2tok = {}
            p.wait("sync", "tv", p.val("tv"))
            p.wait("gpsimd", "tv", p.val("tv"))
            p.wait("sync", pcast_tok)
            for tc in range(4):
                def w1_load(j, tc=None):
                    sl = j % 4
                    q = "sync" if j % 2 else "gpsimd"
                    p.wait(q, "tv", WG.get(sl, 0))
                    t1 = p.dma(q, W0[:, sl * 2048: sl * 2048 + 1024],
                               bass.AP(w1base.tensor, w1base.offset + j * 128,
                                       [[5632, 128], [128 * 5632, 8], [1, 128]]))
                    t2 = p.dma(q, W0[:, sl * 2048 + 1024: sl * 2048 + 2048],
                               bass.AP(w1base.tensor, w1base.offset + 2816 + j * 128,
                                       [[5632, 128], [128 * 5632, 8], [1, 128]]), cont=True)
                    wtok[j] = (t1, t2)
                for j in range(4):
                    w1_load(j)
                for j in range(22):
                    sl = j % 4
                    t1, t2 = wtok.pop(j)
                    p.wait("tensor", t2)
                    for ic in range(8):
                        p.op("tensor", lambda e, ic=ic, tc=tc, sl=sl: e.matmul(
                            psum[0][:, 0:512], W0[:, sl * 2048 + ic * 128: sl * 2048 + ic * 128 + 128],
                            hb2(ic)[:, tc * 512: tc * 512 + 512],
                            start=(ic == 0), stop=(ic == 7)), inc="tv")
                        p.op("tensor", lambda e, ic=ic, tc=tc, sl=sl: e.matmul(
                            psum[1][:, 0:512], W0[:, sl * 2048 + 1024 + ic * 128:
                                                    sl * 2048 + 1024 + ic * 128 + 128],
                            hb2(ic)[:, tc * 512: tc * 512 + 512],
                            start=(ic == 0), stop=(ic == 7)), inc="tv")
                    tvv = p.val("tv")
                    WG[sl] = tvv
                    if j + 4 < 22:
                        w1_load(j + 4)
                    p.wait("scalar", "tv", tvv)
                    sv = p.op("scalar", lambda e, j=j: e.activation(
                        out=AF32[:, 0:512], in_=psum[1][:, 0:512], func=AF.Gelu,
                        bias=rows[:, 1062 + j: 1062 + j + 1]), inc="sv")
                    p.wait("vector", "sv", sv)
                    vv = p.op("vector", lambda e, j=j: e.scalar_tensor_tensor(
                        out=prod(j), in0=psum[0][:, 0:512], scalar=rows[:, 1040 + j: 1040 + j + 1],
                        in1=AF32[:, 0:512], op0=ALU.add, op1=ALU.mult), inc="vv")
                    p.drain("vector")
                    p.wait("tensor", "vv", vv)
                    p.wait("scalar", "vv", vv)
                def w2_load(op2, j):
                    sl = j % 8
                    q = "sync" if j % 2 else "gpsimd"
                    p.wait(q, "tv", W2G.get(sl, 0))
                    w2tok[j] = p.dma(q, W1[:, sl * 256:(sl + 1) * 256],
                                     w2base[j * 128:(j + 1) * 128, op2 * 256:(op2 + 1) * 256])
                for op2 in range(4):
                    for j in range(8):
                        w2_load(op2, j)
                    for j in range(22):
                        sl = j % 8
                        p.wait("tensor", w2tok.pop(j))
                        p.op("tensor", lambda e, j=j, sl=sl: e.matmul(
                            psum[2][:, 0:512], W1[:, sl * 256: sl * 256 + 128],
                            prod(j), start=(j == 0), stop=(j == 21)), inc="tv")
                        p.op("tensor", lambda e, j=j, sl=sl: e.matmul(
                            psum[3][:, 0:512], W1[:, sl * 256 + 128: sl * 256 + 256],
                            prod(j), start=(j == 0), stop=(j == 21)), inc="tv")
                        W2G[sl] = p.val("tv")
                        if j + 8 < 22:
                            w2_load(op2, j + 8)
                    tvv = p.val("tv")
                    p.wait("vector", "tv", tvv)
                    for pi in range(2):
                        oc = op2 * 2 + pi
                        vv = p.op("vector", lambda e, oc=oc, tc=tc, pi=pi: e.scalar_tensor_tensor(
                            out=xr(oc)[:, tc * 512: tc * 512 + 512], in0=psum[2 + pi][:, 0:512],
                            scalar=rows[:, 1084 + oc: 1084 + oc + 1],
                            in1=xr(oc)[:, tc * 512: tc * 512 + 512], op0=ALU.add, op1=ALU.add),
                            inc="vv")
                    p.drain("vector")
                    p.wait("tensor", "vv", p.val("vv"))
            p.join()
            if KDUMP == "l1fz" and l == 0:
                dump_bf(xr)
            if KDUMP == "l1fz1" and l == 1:
                dump_bf(xr)
        for dc in range(8):
            lv = p.dma("sync", AF32[:, 0:2048], SCR["h0"][dc])
            p.wait("vector", lv)
            vv = p.op("vector", lambda e, dc=dc: e.tensor_tensor(
                out=xr(dc), in0=xr(dc), in1=AF32[:, 0:2048], op=ALU.add), inc="vv")
            p.drain("vector")
            p.wait("sync", "vv", vv)
        ln_epilogue()

    def l2_attn():
        p.join()
        cast_xb()
        Ab = A[:].bitcast(BF16)
        W1f = W1[:].bitcast(F32)
        def sb3(apobj, off, dims):
            return bass.AP(apobj.tensor, apobj.offset + off,
                           [[apobj.ap[0][0], apobj.ap[0][1]]] + dims)
        p.dma("sync", colf[0:16, 48:49], I["l2_vgsb"])
        p.dma("sync", colf[0:16, 49:50], I["l2_vt0"], cont=True)
        p.dma("sync", colf[0:16, 51:52], I["l2_btau"], cont=True)
        p.dma("sync", colf[0:16, 52:53], I["l2_cch"], cont=True)
        p.dma("sync", colf[0:16, 53:54], I["l2_bgm"], cont=True)
        p.dma("sync", colf[0:16, 54:55], I["l2_gam"], cont=True)
        p.dma("sync", colf[0:16, 55:56], I["l2_alpha"].broadcast_to([16, 1]), cont=True)
        cv = p.dma("sync", colf[0:16, 56:57], I["l2_icth"].broadcast_to([16, 1]), cont=True)
        p.wait("vector", cv)
        def colop(fn):
            p.op("vector", fn, inc="vv")
            p.drain("vector")
        colop(lambda e: e.tensor_tensor(out=colf[0:16, 50:51], in0=colf[0:16, 48:49],
                                        in1=colf[0:16, 49:50], op=ALU.subtract))
        colop(lambda e: e.reciprocal(out=colf[0:16, 57:58], in_=colf[0:16, 52:53]))
        colop(lambda e: e.tensor_tensor(out=colf[0:16, 57:58], in0=colf[0:16, 57:58],
                                        in1=colf[0:16, 51:52], op=ALU.mult))
        colop(lambda e: e.tensor_tensor(out=colf[0:16, 58:59], in0=colf[0:16, 51:52],
                                        in1=colf[0:16, 55:56], op=ALU.mult))
        colop(lambda e: e.tensor_scalar(out=colf[0:16, 58:59], in0=colf[0:16, 58:59],
                                        scalar1=-1.0, scalar2=None, op0=ALU.mult))
        colop(lambda e: e.tensor_tensor(out=colf[0:16, 59:60], in0=colf[0:16, 55:56],
                                        in1=colf[0:16, 56:57], op=ALU.mult))
        colop(lambda e: e.tensor_scalar(out=colf[0:16, 59:60], in0=colf[0:16, 59:60],
                                        scalar1=-1.0, scalar2=None, op0=ALU.mult))
        colop(lambda e: e.tensor_tensor(out=colf[0:16, 60:61], in0=colf[0:16, 53:54],
                                        in1=colf[0:16, 54:55], op=ALU.mult))
        vvc = p.val("vv")
        p.wait("sync", "vv", vvc)
        p.dma("sync", SCR["c16"], colf[0:16, 57:60])
        g2 = p.dma("sync", SCR["c16b"], colf[0:16, 60:61], cont=True)
        p.wait("sync", g2)
        wv = None
        for ic in range(8):
            wv = p.dma("gpsimd", W1[:, ic * 16:(ic + 1) * 16],
                       I["l2_wvgs"][ic * 128:(ic + 1) * 128, :], cont=(ic > 0))
        p.wait("tensor", wv)
        for tc in range(4):
            for ic in range(8):
                p.op("tensor", lambda e, tc=tc, ic=ic: e.matmul(
                    psum[2][0:16, 0:512], W1[:, ic * 16:(ic + 1) * 16],
                    xb(ic)[:, tc * 512: tc * 512 + 512],
                    start=(ic == 0), stop=(ic == 7)), inc="tv")
            tvv = p.val("tv")
            p.wait("vector", "tv", tvv)
            vv = p.op("vector", lambda e, tc=tc: e.tensor_scalar(
                out=rows[0:16, tc * 512: tc * 512 + 512], in0=psum[2][0:16, 0:512],
                scalar1=colf[0:16, 50:51], scalar2=None, op0=ALU.add), inc="vv")
            p.wait("tensor", "vv", vv)
        vv = p.val("vv")
        p.wait("sync", "vv", vv)
        at = p.dma("sync", SCR["ascr"], rows[0:16, 0:2048])
        p.wait("vector", at)
        for oc24 in range(24):
            sl = oc24 % 2
            p.wait("gpsimd", "tv", p.val("tv"))
            wv = None
            for ic in range(8):
                wv = p.dma("gpsimd", W0[:, sl * 1024 + ic * 128: sl * 1024 + ic * 128 + 128],
                           I["l2_wqkv"][ic * 128:(ic + 1) * 128, oc24 * 128:(oc24 + 1) * 128],
                           cont=(ic > 0))
            p.wait("tensor", wv)
            for tc in range(4):
                bk = tc % 2
                bank_wait(bk)
                for ic in range(8):
                    p.op("tensor", lambda e, sl=sl, tc=tc, ic=ic, bk=bk: e.matmul(
                        psum[bk][:, 0:512], W0[:, sl * 1024 + ic * 128: sl * 1024 + ic * 128 + 128],
                        xb(ic)[:, tc * 512: tc * 512 + 512],
                        start=(ic == 0), stop=(ic == 7)), inc="tv")
                tvv = p.val("tv")
                p.wait("vector", "tv", tvv)
                bank_set(bk, p.op("vector", lambda e, tc=tc, bk=bk: e.tensor_copy(
                    out=AF32[:, tc * 512: tc * 512 + 512], in_=psum[bk][:, 0:512]), inc="vv"))
            vv = p.val("vv")
            p.wait("sync", "vv", vv)
            if oc24 < 8:
                p.dma("sync", SCR["qscr"][2 * oc24], AF32[0:64, 0:2048])
                t = p.dma("sync", SCR["qscr"][2 * oc24 + 1], AF32[64:128, 0:2048], cont=True)
            elif oc24 < 16:
                t = p.dma("sync", SCR["kraw"][oc24 - 8], AF32[:, 0:2048])
            else:
                c0 = oc24 - 16
                t = None
                for hp in range(2):
                    h = 2 * c0 + hp
                    for cc in range(16):
                        dst = bass.AP(SCR["vrc"].tensor, SCR["vrc"].offset + (h * 16 + cc) * 8192,
                                      [[128, 32], [4096, 2], [1, 128]])
                        t = p.dma("sync", dst,
                                  AF32[hp * 64:(hp + 1) * 64, cc * 128:(cc + 1) * 128],
                                  cont=not (hp == 0 and cc == 0))
            p.wait("vector", t)
        p.join()
        for c in range(8):
            hs = p.dma("sync", sstat[:, 0:16], I["c_hones16"][c])
            kv = p.dma("sync", AF32[:, 0:2048], SCR["kraw"][c], cont=True)
            p.wait("tensor", kv)
            p.wait("scalar", kv)
            p.wait("vector", kv)
            for tc in range(4):
                sq = p.op("scalar", lambda e, tc=tc: e.activation(
                    out=BFBf[:, 0:512], in_=AF32[:, tc * 512: tc * 512 + 512],
                    func=AF.Square), inc="sv")
                p.wait("tensor", "sv", sq)
                p.op("tensor", lambda e, tc=tc: e.matmul(
                    psum[2][0:16, 0:512], sstat[:, 0:16], AF32[:, tc * 512: tc * 512 + 512],
                    start=True, stop=True), inc="tv")
                tvq = p.op("tensor", lambda e: e.matmul(
                    psum[3][0:16, 0:512], sstat[:, 0:16], BFBf[:, 0:512],
                    start=True, stop=True), inc="tv")
                p.wait("vector", "tv", tvq)
                p.wait("scalar", "tv", tvq)
                p.op("vector", lambda e, tc=tc: e.tensor_scalar(
                    out=rows[0:16, tc * 512: tc * 512 + 512], in0=psum[2][0:16, 0:512],
                    scalar1=1.0 / 64.0, scalar2=None, op0=ALU.mult), inc="vv")
                vv = p.op("vector", lambda e, tc=tc: e.tensor_scalar(
                    out=BFBf[0:16, 2048 + tc * 512: 2048 + tc * 512 + 512],
                    in0=psum[3][0:16, 0:512],
                    scalar1=1.0 / 64.0, scalar2=None, op0=ALU.mult), inc="vv")
                p.wait("tensor", "vv", vv)
            p.drain("vector")
            MU = rows[0:16, 0:2048]
            MS = BFBf[0:16, 2048:4096]
            NG = BFBf[0:16, 4096:6144]
            p.op("vector", lambda e: e.scalar_tensor_tensor(
                out=NG, in0=MU, scalar=-1.0,
                in1=MU, op0=ALU.mult, op1=ALU.mult), inc="vv")
            p.drain("vector")
            p.op("vector", lambda e: e.tensor_tensor(
                out=MS, in0=MS, in1=NG, op=ALU.add), inc="vv")
            vv = p.val("vv")
            p.wait("scalar", "vv", vv)
            p.op("scalar", lambda e: e.activation(out=MS, in_=MS, func=AF.Sqrt,
                                                  bias=colf[0:16, 62:63]), inc="sv")
            p.wait("vector", "sv", p.val("sv"))
            p.op("vector", lambda e: e.reciprocal(out=MS, in_=MS))
            p.drain("vector")
            p.op("vector", lambda e: e.tensor_scalar(
                out=MS, in0=MS,
                scalar1=colf[0:16, 60:61], scalar2=None, op0=ALU.mult), inc="vv")
            p.drain("vector")
            p.op("vector", lambda e: e.scalar_tensor_tensor(
                out=NG, in0=MU, scalar=-1.0,
                in1=MS, op0=ALU.mult, op1=ALU.mult), inc="vv")
            vv = p.val("vv")
            p.wait("sync", "vv", vv)
            p.dma("sync", SCR["rows8"][0:2, :], BFBf[2 * c: 2 * c + 2, 2048:4096])
            wb = p.dma("sync", SCR["rows8"][2:4, :], BFBf[2 * c: 2 * c + 2, 4096:6144],
                       cont=True)
            p.wait("sync", wb)
            p.dma("sync", Dbf[:, 0:2048],
                  SCR["rows8"][0:2, :].unsqueeze(1).broadcast_to([2, 64, S]))
            rb = p.dma("sync", BFBf[:, 2048:4096],
                       SCR["rows8"][2:4, :].unsqueeze(1).broadcast_to([2, 64, S]), cont=True)
            p.wait("vector", rb)
            p.op("vector", lambda e: e.tensor_tensor(out=AF32[:, 0:2048], in0=AF32[:, 0:2048],
                                                     in1=Dbf[:, 0:2048], op=ALU.mult), inc="vv")
            p.drain("vector")
            p.op("vector", lambda e: e.tensor_tensor(out=AF32[:, 0:2048], in0=AF32[:, 0:2048],
                                                     in1=BFBf[:, 2048:4096], op=ALU.add), inc="vv")
            vv = p.val("vv")
            p.wait("sync", "vv", vv)
            p.dma("sync", SCR["kscr"][2 * c], AF32[0:64, 0:2048])
            kw = p.dma("sync", SCR["kscr"][2 * c + 1], AF32[64:128, 0:2048], cont=True)
            p.wait("vector", kw)
        p.join()
        if KDUMP == "l2k":
            dv = p.dma("sync", dbg, bass.AP(SCR["kscr"].tensor, SCR["kscr"].offset,
                                            [[2048, 1024], [1, 2048]]))
            p.wait("sync", dv)
        if KDUMP == "l2a":
            dv = p.dma("sync", dbg[0:16, :], SCR["ascr"])
            p.wait("sync", dv)
        yv = None
        for dh in range(32):
            yv = p.dma("gpsimd", W0[:, dh * 128:(dh + 1) * 128], I["c_yred2"][dh], cont=(dh > 0))
        p.wait("tensor", yv)
        dpv = p.dma("sync", rows[0:2, 0:128], I["c_dppat"])
        p.wait("tensor", dpv)
        PZ_G = 0
        A1o, A2o, A3o, A4o = 0, 4096, 8192, 12288
        A1 = BFAf[:, 0:4096]; A2 = BFAf[:, 4096:8192]
        A3 = BFAf[:, 8192:12288]; A4 = BFAf[:, 12288:16384]
        d3 = [[128, 32], [1, 128]]
        A1_3 = sb3(BFAf, A1o, d3); A2_3 = sb3(BFAf, A2o, d3)
        Db_3 = sb3(Db[:], 0, d3)
        Wsm3 = None
        koff = lambda sl: 64 + sl * 384
        qoff = lambda sl: 192 + sl * 384
        aoff = lambda sl: 320 + sl * 384
        vt = lambda sl: BFBf[:, sl * 4096:(sl + 1) * 4096]
        VT_G = [0, 0]
        YF_G = 0
        y2tok = None
        gq_tok = None
        for h in range(16):
            p.wait("sync", "sv", p.val("sv"))
            ccols = p.dma("sync", colf[:, 57:60], SCR["c16"][h:h + 1, 0:3].broadcast_to([128, 3]))
            p.wait("scalar", ccols)
            p.wait("sync", "vv", p.val("vv"))
            wsm = p.dma("sync", W1f[:, 0:32], I["l2_wlsm"][h])
            p.wait("vector", wsm)
            p.drain("gpsimd")
            p.wait("gpsimd", "vv", p.val("vv"))
            p.wait("gpsimd", "tv", p.val("tv"))
            p.op("gpsimd", lambda e: e.memset(W1f[:, 32:64], 0.0), inc="gv")
            gq = p.dma("gpsimd", Ab[64:128, 0:2048], SCR["qscr"][h])
            gq_tok = p.dma("gpsimd", Ab[64:128, 2048:2112], I["l2_wlT"][h], cont=True)
            p.wait("vector", "gv", p.val("gv"))
            if y2tok is not None:
                p.wait("vector", y2tok)
            Wsm3 = W1f[:, 0:32].unsqueeze(2).broadcast_to([128, 32, 128])
            for ch in range(16):
                sl = ch % 2
                p.wait("sync", "vv", VT_G[sl])
                a_t = p.dma("sync", W1f[:, aoff(sl):aoff(sl) + 128],
                            SCR["ascr"][h:h + 1, ch * 128:(ch + 1) * 128].broadcast_to([128, 128]))
                p.dma("sync", W1f[:, koff(sl):koff(sl) + 128],
                      bass.AP(SCR["kscr"].tensor, SCR["kscr"].offset + h * 131072 + ch * 128,
                              [[2048, 64], [0, 2], [1, 128]]), cont=True)
                p.dma("sync", W1f[:, qoff(sl):qoff(sl) + 128],
                      bass.AP(SCR["qscr"].tensor, SCR["qscr"].offset + h * 131072 + ch * 128,
                              [[2048, 64], [0, 2], [1, 128]]), cont=True)
                v_t = p.dma("sync", BFBf[0:2, sl * 4096:(sl + 1) * 4096],
                            bass.AP(SCR["vrc"].tensor, SCR["vrc"].offset + (h * 16 + ch) * 8192,
                                    [[4096, 2], [1, 4096]]), cont=True)
                p.wait("vector", v_t)
                p.wait("tensor", v_t)
                d3h = [[128, 16], [1, 128]]
                zvs = []
                for ph in range(2):
                    po = ph * 2048
                    wsm3h = W1f[:, 16 * ph: 16 * ph + 16].unsqueeze(2).broadcast_to([128, 16, 128])
                    a3 = W1f[:, aoff(sl):aoff(sl) + 128].unsqueeze(1).broadcast_to([128, 16, 128])
                    zv = p.op("vector", lambda e, wsm3h=wsm3h, a3=a3, po=po: e.tensor_tensor(
                        out=sb3(BFAf, A1o + po, d3h), in0=wsm3h, in1=a3, op=ALU.add), inc="vv")
                    zvs.append(zv)
                p.drain("vector")
                sths = []
                for ph in range(2):
                    po = ph * 2048
                    A1h = BFAf[:, A1o + po: A1o + po + 2048]
                    A2h = BFAf[:, A2o + po: A2o + po + 2048]
                    A3h = BFAf[:, A3o + po: A3o + po + 2048]
                    A4h = BFAf[:, A4o + po: A4o + po + 2048]
                    p.wait("scalar", "vv", zvs[ph])
                    p.op("scalar", lambda e, A1h=A1h, A2h=A2h: e.activation(
                        out=A2h, in_=A1h, func=AF.Sigmoid, scale=-1.0), inc="sv")
                    p.drain("scalar")
                    p.op("scalar", lambda e, A1h=A1h, A2h=A2h: e.activation(
                        out=A1h, in_=A2h, func=AF.Ln), inc="sv")
                    p.drain("scalar")
                    p.op("scalar", lambda e, A1h=A1h, A3h=A3h: e.activation(
                        out=A3h, in_=A1h, func=AF.Exp, scale=colf[:, 57:58]), inc="sv")
                    p.drain("scalar")
                    sth = p.op("scalar", lambda e, A1h=A1h, A4h=A4h: e.activation(
                        out=A4h, in_=A1h, func=AF.Tanh, scale=colf[:, 58:59],
                        bias=colf[:, 59:60]), inc="sv")
                    p.drain("scalar")
                    sths.append(sth)
                ybvs = []
                for ph in range(2):
                    po = ph * 2048
                    A1h = BFAf[:, A1o + po: A1o + po + 2048]
                    A2h = BFAf[:, A2o + po: A2o + po + 2048]
                    A3h = BFAf[:, A3o + po: A3o + po + 2048]
                    A4h = BFAf[:, A4o + po: A4o + po + 2048]
                    k3 = W1f[:, koff(sl):koff(sl) + 128].unsqueeze(1).broadcast_to([128, 16, 128])
                    q3 = W1f[:, qoff(sl):qoff(sl) + 128].unsqueeze(1).broadcast_to([128, 16, 128])
                    p.wait("vector", "sv", sths[ph])
                    p.op("vector", lambda e, A1h=A1h, A2h=A2h: e.scalar_tensor_tensor(
                        out=A2h, in0=A2h, scalar=1.0, in1=A1h,
                        op0=ALU.subtract, op1=ALU.mult), inc="vv")
                    p.drain("vector")
                    p.op("vector", lambda e, A2h=A2h, A4h=A4h: e.tensor_tensor(
                        out=A2h, in0=A2h, in1=A4h, op=ALU.mult), inc="vv")
                    p.drain("vector")
                    p.op("vector", lambda e, k3=k3, po=po: e.tensor_tensor(
                        out=sb3(BFAf, A2o + po, d3h), in0=sb3(BFAf, A2o + po, d3h),
                        in1=k3, op=ALU.mult), inc="vv")
                    p.drain("vector")
                    p.wait("tensor", "vv", PZ_G)
                    for s in range(4):
                        p.op("tensor", lambda e, s=s, sl=sl, po=po: e.matmul(
                            psZ[:, s * 512:(s + 1) * 512], rows[0:2, 0:128],
                            BFBf[0:2, sl * 4096 + po + s * 512: sl * 4096 + po + s * 512 + 512],
                            start=True, stop=True), inc="tv")
                    tvz = p.val("tv")
                    p.wait("vector", "tv", tvz)
                    bv = p.op("vector", lambda e, A2h=A2h: e.tensor_tensor(
                        out=A2h, in0=A2h, in1=psZ[:, 0:2048], op=ALU.mult), inc="vv")
                    p.drain("vector")
                    PZ_G = bv
                    for jj in range(16):
                        j = ph * 16 + jj
                        p.op("vector", lambda e, j=j, po=po, jj=jj: e.tensor_tensor_scan(
                            out=A1[:, j * 128:(j + 1) * 128], data0=A3[:, j * 128:(j + 1) * 128],
                            data1=A2[:, j * 128:(j + 1) * 128], initial=W1f[:, 32 + j: 33 + j],
                            op0=ALU.mult, op1=ALU.add), inc="vv")
                    p.drain("vector")
                    p.op("vector", lambda e, ph=ph, po=po: e.tensor_copy(
                        out=W1f[:, 32 + 16 * ph: 48 + 16 * ph],
                        in_=sb3(BFAf, A1o + po + 127, [[128, 16]])), inc="vv")
                    p.drain("vector")
                    p.wait("vector", "tv", YF_G)
                    ybv = p.op("vector", lambda e, q3=q3, po=po: e.tensor_tensor(
                        out=sb3(Db[:], po, d3h), in0=sb3(BFAf, A1o + po, d3h),
                        in1=q3, op=ALU.mult), inc="vv")
                    p.drain("vector")
                    ybvs.append(ybv)
                    p.wait("tensor", "vv", ybv)
                    for jj in range(16):
                        dh = ph * 16 + jj
                        p.op("tensor", lambda e, dh=dh: e.matmul(
                            psum[2][:, 0:128], W0[:, dh * 128:(dh + 1) * 128],
                            Db[:, dh * 128:(dh + 1) * 128],
                            start=(dh == 0), stop=(dh == 31)), inc="tv")
                VT_G[sl] = ybvs[1]
                YF_G = p.val("tv")
                p.wait("vector", "tv", YF_G)
                p.op("vector", lambda e, ch=ch: e.tensor_copy(
                    out=AF32[0:64, ch * 128:(ch + 1) * 128], in_=psum[2][0:64, 0:128]), inc="vv")
            p.wait("tensor", gq_tok)
            for tc in range(4):
                tvv = p.op("tensor", lambda e, tc=tc: e.matmul(
                    psum[3][0:64, 0:512], Ab[64:128, 2048:2112],
                    Ab[64:128, tc * 512:(tc + 1) * 512], start=True, stop=True), inc="tv")
                p.wait("vector", "tv", tvv)
                vv = p.op("vector", lambda e, tc=tc: e.tensor_tensor(
                    out=rows[64:128, tc * 512:(tc + 1) * 512], in0=psum[3][0:64, 0:512],
                    in1=AF32[0:64, tc * 512:(tc + 1) * 512], op=ALU.add), inc="vv")
                p.wait("tensor", "vv", vv)
            vv = p.val("vv")
            p.wait("sync", "vv", vv)
            y2tok = p.dma("sync", SCR["y2"][h], rows[64:128, 0:2048])
        p.join()
        if KDUMP == "l2y":
            for c in range(8):
                yv = p.dma("gpsimd", BFA[:, c * 2048:(c + 1) * 2048], SCR["y2"][2 * c:2 * c + 2],
                           cont=(c > 0))
            p.wait("vector", yv)
            dump_bf(lambda dc: BFA[:, dc * 2048:(dc + 1) * 2048])
        yv = None
        for c in range(8):
            yv = p.dma("gpsimd", BFA[:, c * 2048:(c + 1) * 2048], SCR["y2"][2 * c:2 * c + 2],
                       cont=(c > 0))
        wv = load_w_bf16(W0, "l2_wo", 1024, 8)
        p.wait("tensor", yv)
        p.wait("tensor", wv)
        for oc in range(8):
            for tc in range(4):
                bk = (oc * 4 + tc) % 2
                bank_wait(bk)
                for ic in range(8):
                    p.op("tensor", lambda e, oc=oc, tc=tc, ic=ic, bk=bk: e.matmul(
                        psum[bk][:, 0:512],
                        W0[:, ic * 1024 + oc * 128: ic * 1024 + oc * 128 + 128],
                        BFA[:, ic * 2048 + tc * 512: ic * 2048 + tc * 512 + 512],
                        start=(ic == 0), stop=(ic == 7)), inc="tv")
                tvv = p.val("tv")
                p.wait("vector", "tv", tvv)
                bank_set(bk, p.op("vector", lambda e, oc=oc, tc=tc, bk=bk: e.tensor_tensor(
                    out=xr(oc)[:, tc * 512: tc * 512 + 512], in0=psum[bk][:, 0:512],
                    in1=xr(oc)[:, tc * 512: tc * 512 + 512], op=ALU.add), inc="vv"))
        ln_epilogue()

    def l2_ffn():
        p.join()
        mv = p.dma("sync", colf[:, 44:52], I["l2f_mur"])
        mv = p.dma("sync", colf[:, 52:60], I["l2f_muk"], cont=True)
        p.wait("vector", mv)
        xrb = lambda ic: BFA[:, ic * S:(ic + 1) * S]
        xkb = lambda ic: BFA[:, 16384 + ic * S: 16384 + (ic + 1) * S]
        for dc in range(8):
            p.drain("gpsimd")
            p.wait("gpsimd", "vv", p.val("vv"))
            p.op("gpsimd", lambda e: e.memset(AF32[:, 0:1], 0.0), inc="gv")
            p.wait("vector", "gv", p.val("gv"))
            p.op("vector", lambda e, dc=dc: e.tensor_copy(out=AF32[:, 1:2048],
                                                          in_=xr(dc)[:, 0:S - 1]), inc="vv")
            p.drain("vector")
            p.op("vector", lambda e, dc=dc: e.tensor_tensor(
                out=Dbf[:, 0:2048], in0=xr(dc), in1=AF32[:, 0:2048], op=ALU.subtract), inc="vv")
            p.drain("vector")
            p.op("vector", lambda e, dc=dc: e.scalar_tensor_tensor(
                out=xrb(dc), in0=Dbf[:, 0:2048], scalar=colf[:, 44 + dc: 44 + dc + 1],
                in1=AF32[:, 0:2048], op0=ALU.mult, op1=ALU.add), inc="vv")
            p.op("vector", lambda e, dc=dc: e.scalar_tensor_tensor(
                out=xkb(dc), in0=Dbf[:, 0:2048], scalar=colf[:, 52 + dc: 52 + dc + 1],
                in1=AF32[:, 0:2048], op0=ALU.mult, op1=ALU.add), inc="vv")
            p.drain("vector")
        p.join()
        SG = {}
        wv = load_w_bf16(W0, "l2f_wk", 1024, 8)
        p.wait("tensor", wv)
        for oc in range(8):
            for tc in range(4):
                bk = (oc * 4 + tc) % 2
                bank_wait(bk)
                for ic in range(8):
                    p.op("tensor", lambda e, oc=oc, tc=tc, ic=ic, bk=bk: e.matmul(
                        psum[bk][:, 0:512],
                        W0[:, ic * 1024 + oc * 128: ic * 1024 + oc * 128 + 128],
                        xkb(ic)[:, tc * 512: tc * 512 + 512],
                        start=(ic == 0), stop=(ic == 7)), inc="tv")
                tvv = p.val("tv")
                p.wait("scalar", "tv", tvv)
                sl2 = (oc * 4 + tc) % 4
                p.wait("scalar", "vv", SG.get(sl2, 0))
                sv = p.op("scalar", lambda e, bk=bk, sl2=sl2: e.activation(
                    out=AF32[:, sl2 * 512: sl2 * 512 + 512], in_=psum[bk][:, 0:512],
                    func=AF.Relu), inc="sv")
                p.wait("vector", "sv", sv)
                vv = p.op("vector", lambda e, oc=oc, tc=tc, sl2=sl2: e.tensor_tensor(
                    out=BFB[:, oc * S + tc * 512: oc * S + tc * 512 + 512],
                    in0=AF32[:, sl2 * 512: sl2 * 512 + 512],
                    in1=AF32[:, sl2 * 512: sl2 * 512 + 512], op=ALU.mult), inc="vv")
                SG[sl2] = vv
                p.wait("tensor", "sv", p.val("sv"))
        p.join()
        RG = {}
        wv = load_w_bf16(W0, "l2f_wr", 1024, 8)
        p.wait("tensor", wv)
        for oc in range(8):
            slw = oc % 2
            p.wait("gpsimd", "tv", p.val("tv"))
            vv2 = None
            for ic in range(8):
                vv2 = p.dma("gpsimd", W1[:, slw * 1024 + ic * 128: slw * 1024 + ic * 128 + 128],
                            I["l2f_wv"][ic * 128:(ic + 1) * 128, oc * 128:(oc + 1) * 128],
                            cont=(ic > 0))
            for tc in range(4):
                bk = tc % 2
                bank_wait(bk)
                p.wait("tensor", "sv", RG.get(bk, 0))
                for ic in range(8):
                    p.op("tensor", lambda e, oc=oc, tc=tc, ic=ic, bk=bk: e.matmul(
                        psum[bk][:, 0:512],
                        W0[:, ic * 1024 + oc * 128: ic * 1024 + oc * 128 + 128],
                        xrb(ic)[:, tc * 512: tc * 512 + 512],
                        start=(ic == 0), stop=(ic == 7)), inc="tv")
                tvv = p.val("tv")
                p.wait("scalar", "tv", tvv)
                sv = p.op("scalar", lambda e, tc=tc, bk=bk: e.activation(
                    out=AF32[:, tc * 512: tc * 512 + 512], in_=psum[bk][:, 0:512],
                    func=AF.Sigmoid), inc="sv")
                p.wait("vector", "sv", sv)
                RG[bk] = sv
            p.wait("tensor", "sv", p.val("sv"))
            p.wait("tensor", vv2)
            for tc in range(4):
                bk = 2 + tc % 2
                bank_wait(bk)
                for ic in range(8):
                    p.op("tensor", lambda e, oc=oc, tc=tc, ic=ic, bk=bk, slw=slw: e.matmul(
                        psum[bk][:, 0:512],
                        W1[:, slw * 1024 + ic * 128: slw * 1024 + ic * 128 + 128],
                        BFB[:, ic * S + tc * 512: ic * S + tc * 512 + 512],
                        start=(ic == 0), stop=(ic == 7)), inc="tv")
                tvv = p.val("tv")
                p.wait("vector", "tv", tvv)
                p.op("vector", lambda e, tc=tc, bk=bk: e.tensor_tensor(
                    out=Dbf[:, 0:512], in0=psum[bk][:, 0:512],
                    in1=AF32[:, tc * 512: tc * 512 + 512], op=ALU.mult), inc="vv")
                p.drain("vector")
                vv = p.op("vector", lambda e, oc=oc, tc=tc, bk=bk: e.tensor_tensor(
                    out=xr(oc)[:, tc * 512: tc * 512 + 512], in0=Dbf[:, 0:512],
                    in1=xr(oc)[:, tc * 512: tc * 512 + 512], op=ALU.add), inc="vv")
                p.drain("vector")
                bank_set(bk, vv)
        ln_epilogue()

    def l3_attn():
        p.join()
        PI = float(np.float32(np.pi))
        cwv = p.dma("sync", colf[0:32, 40:41], I["l3_theta"].broadcast_to([32, 1]))
        for j in range(4):
            cwv = p.dma("sync", colf[0:32, 41 + j: 42 + j],
                        I["c_cw"][j:j + 1, :].broadcast_to([32, 1]), cont=True)
        fbv = p.dma("sync", colf[0:32, 45:46], I["c_l3_fb"], cont=True)
        tv_ = p.dma("sync", rows[0:32, 0:2048], I["c_l3t"], cont=True)
        p.wait("vector", tv_)
        p.op("vector", lambda e: e.tensor_scalar(out=colf[0:32, 46:47], in0=colf[0:32, 40:41],
                                                 scalar1=1.0 / 220.0, scalar2=None,
                                                 op0=ALU.mult), inc="vv")
        p.drain("vector")
        p.op("vector", lambda e: e.tensor_tensor(out=colf[0:32, 46:47], in0=colf[0:32, 46:47],
                                                 in1=colf[0:32, 45:46], op=ALU.mult), inc="vv")
        p.drain("vector")
        for j in range(3):
            p.op("vector", lambda e, j=j: e.tensor_scalar(
                out=colf[0:32, 47 + j: 48 + j], in0=colf[0:32, 41 + j: 42 + j],
                scalar1=-1.0, scalar2=None, op0=ALU.mult), inc="vv")
        p.drain("vector")
        p.op("vector", lambda e: e.tensor_scalar(out=rows[0:32, 0:2048], in0=rows[0:32, 0:2048],
                                                 scalar1=colf[0:32, 46:47], scalar2=None,
                                                 op0=ALU.mult), inc="vv")
        p.drain("vector")
        p.op("vector", lambda e: e.tensor_scalar(out=BFBf[0:32, 0:2048], in0=rows[0:32, 0:2048],
                                                 scalar1=colf[0:32, 44:45], scalar2=None,
                                                 op0=ALU.mult), inc="vv")
        p.drain("vector")
        p.op("vector", lambda e: e.tensor_copy(out=BFBi[0:32, 2048:4096],
                                               in_=BFBf[0:32, 0:2048]), inc="vv")
        p.drain("vector")
        p.op("vector", lambda e: e.tensor_copy(out=BFBf[0:32, 0:2048],
                                               in_=BFBi[0:32, 2048:4096]), inc="vv")
        p.drain("vector")
        for j in range(3):
            p.op("vector", lambda e, j=j: e.scalar_tensor_tensor(
                out=rows[0:32, 0:2048], in0=BFBf[0:32, 0:2048],
                scalar=colf[0:32, 47 + j: 48 + j], in1=rows[0:32, 0:2048],
                op0=ALU.mult, op1=ALU.add), inc="vv")
            p.drain("vector")
        p.op("vector", lambda e: e.tensor_scalar(out=rows[0:32, 0:2048], in0=rows[0:32, 0:2048],
                                                 scalar1=-PI, scalar2=PI,
                                                 op0=ALU.max, op1=ALU.min), inc="vv")
        vv = p.val("vv")
        p.wait("scalar", "vv", vv)
        sv = p.op("scalar", lambda e: e.activation(out=BFBf[0:32, 0:2048],
                                                   in_=rows[0:32, 0:2048],
                                                   func=AF.Sin), inc="sv")
        p.wait("vector", "sv", sv)
        p.op("vector", lambda e: e.scalar_tensor_tensor(
            out=rows[0:32, 0:2048], in0=rows[0:32, 0:2048], scalar=-1.0,
            in1=rows[0:32, 0:2048], op0=ALU.mult, op1=ALU.max), inc="vv")
        p.drain("vector")
        p.op("vector", lambda e: e.tensor_scalar(out=rows[0:32, 0:2048], in0=rows[0:32, 0:2048],
                                                 scalar1=-1.0, scalar2=float(np.float32(np.pi / 2)),
                                                 op0=ALU.mult, op1=ALU.add), inc="vv")
        p.op("vector", lambda e: e.tensor_scalar(out=BFBf[0:32, 2048:4096],
                                                 in0=BFBf[0:32, 0:2048],
                                                 scalar1=-1.0, scalar2=None,
                                                 op0=ALU.mult), inc="vv")
        vv = p.val("vv")
        p.wait("scalar", "vv", vv)
        sv = p.op("scalar", lambda e: e.activation(out=rows[0:32, 0:2048],
                                                   in_=rows[0:32, 0:2048],
                                                   func=AF.Sin), inc="sv")
        p.wait("sync", "sv", sv)
        p.wait("sync", "vv", vv)
        p.dma("sync", SCR["l3cs"], rows[0:32, 0:2048])
        p.dma("sync", SCR["l3sn"][0], BFBf[0:32, 2048:4096], cont=True)
        tb = p.dma("sync", SCR["l3sn"][1], BFBf[0:32, 0:2048], cont=True)
        p.wait("vector", tb)
        p.wait("scalar", tb)
        lw = p.dma("sync", colf[:, 0:8], I["l3_lnw"])
        lb = p.dma("sync", colf[:, 8:16], I["l3_lnb"], cont=True)
        p.wait("vector", lb)
        p.wait("scalar", lb)
        for tc in range(4):
            for dc in range(8):
                sl = dc % 2
                sq = p.op("scalar", lambda e, dc=dc, tc=tc, sl=sl: e.activation(
                    out=AF32[:, sl * 512: sl * 512 + 512],
                    in_=xr(dc)[:, tc * 512: tc * 512 + 512], func=AF.Square), inc="sv")
                p.wait("tensor", "sv", sq)
                p.op("tensor", lambda e, dc=dc, tc=tc: e.matmul(
                    psum[0][0:1, 0:512], ones32[:, 0:1],
                    xr(dc)[:, tc * 512: tc * 512 + 512],
                    start=(dc == 0), stop=(dc == 7)), inc="tv")
                tvq = p.op("tensor", lambda e, dc=dc, tc=tc, sl=sl: e.matmul(
                    psum[1][0:1, 0:512], ones32[:, 0:1],
                    AF32[:, sl * 512: sl * 512 + 512],
                    start=(dc == 0), stop=(dc == 7)), inc="tv")
                p.wait("scalar", "tv", tvq)
            tvv = p.val("tv")
            p.wait("vector", "tv", tvv)
            p.op("vector", lambda e, tc=tc: e.tensor_scalar(
                out=BFBf[0:1, tc * 512:tc * 512 + 512], in0=psum[0][0:1, 0:512],
                scalar1=1.0 / 1024.0, scalar2=None, op0=ALU.mult), inc="vv")
            p.op("vector", lambda e, tc=tc: e.tensor_scalar(
                out=BFBf[0:1, 2048 + tc * 512: 2048 + tc * 512 + 512], in0=psum[1][0:1, 0:512],
                scalar1=1.0 / 1024.0, scalar2=None, op0=ALU.mult), inc="vv")
            vv = p.val("vv")
            p.wait("tensor", "vv", vv)
        p.drain("vector")
        MEAN = BFBf[0:1, 0:S]; M2 = BFBf[0:1, 2048:2048 + S]
        AV_ = BFBf[0:1, 4096:4096 + S]; BV_ = BFBf[0:1, 6144:6144 + S]
        p.op("vector", lambda e: e.scalar_tensor_tensor(
            out=AV_, in0=MEAN, scalar=-1.0, in1=MEAN, op0=ALU.mult, op1=ALU.mult))
        p.drain("vector")
        p.op("vector", lambda e: e.tensor_tensor(out=AV_, in0=AV_, in1=M2, op=ALU.add), inc="vv")
        p.drain("vector")
        p.wait("scalar", "vv", p.val("vv"))
        p.op("scalar", lambda e: e.activation(out=AV_, in_=AV_, func=AF.Sqrt,
                                              bias=colf[0:1, 62:63]), inc="sv")
        p.wait("vector", "sv", p.val("sv"))
        p.op("vector", lambda e: e.reciprocal(out=AV_, in_=AV_))
        p.drain("vector")
        p.op("vector", lambda e: e.scalar_tensor_tensor(
            out=BV_, in0=MEAN, scalar=-1.0, in1=AV_, op0=ALU.mult, op1=ALU.mult), inc="vv")
        vv = p.val("vv")
        p.wait("sync", "vv", vv)
        p.dma("sync", SCR["rows8"][0:1, :], AV_)
        wv = p.dma("sync", SCR["rows8"][1:2, :], BV_, cont=True)
        p.wait("sync", wv)
        p.dma("sync", BFBf[:, 0:2048], SCR["rows8"][0:1, :].broadcast_to([128, S]))
        vb = p.dma("sync", BFBf[:, 2048:4096], SCR["rows8"][1:2, :].broadcast_to([128, S]),
                   cont=True)
        p.wait("vector", vb)
        for dc in range(8):
            p.op("vector", lambda e, dc=dc: e.scalar_tensor_tensor(
                out=AF32[:, 0:2048], in0=xr(dc), scalar=colf[:, dc: dc + 1],
                in1=BFBf[:, 0:2048], op0=ALU.mult, op1=ALU.mult), inc="vv")
            p.drain("vector")
            p.op("vector", lambda e, dc=dc: e.scalar_tensor_tensor(
                out=AF32[:, 0:2048], in0=BFBf[:, 2048:4096], scalar=colf[:, dc: dc + 1],
                in1=AF32[:, 0:2048], op0=ALU.mult, op1=ALU.add), inc="vv")
            p.drain("vector")
            p.op("vector", lambda e, dc=dc: e.tensor_scalar(
                out=BFA[:, dc * S:(dc + 1) * S], in0=AF32[:, 0:2048],
                scalar1=colf[:, 8 + dc: 8 + dc + 1], scalar2=None, op0=ALU.add), inc="vv")
            p.drain("vector")
        p.join()
        hb = lambda ic: BFA[:, ic * S:(ic + 1) * S]
        pv = p.dma("sync", sstat[:, 0:128], I["c_perm_pair"])
        p.wait("tensor", pv)
        def proj_rope(wname, coloff, dstf):
            p.wait("gpsimd", "tv", p.val("tv"))
            wv = None
            for ic in range(8):
                wv = p.dma("gpsimd", W0[:, ic * 1024:(ic + 1) * 1024],
                           I[wname][ic * 128:(ic + 1) * 128, coloff:coloff + 1024],
                           cont=(ic > 0))
            p.wait("tensor", wv)
            for oc in range(8):
                for tc in range(4):
                    bk = tc % 2
                    bank_wait(bk)
                    for ic in range(8):
                        p.op("tensor", lambda e, oc=oc, tc=tc, ic=ic, bk=bk: e.matmul(
                            psum[bk][:, 0:512],
                            W0[:, ic * 1024 + oc * 128: ic * 1024 + oc * 128 + 128],
                            hb(ic)[:, tc * 512: tc * 512 + 512],
                            start=(ic == 0), stop=(ic == 7)), inc="tv")
                    tvv = p.val("tv")
                    p.wait("vector", "tv", tvv)
                    bank_set(bk, p.op("vector", lambda e, tc=tc, bk=bk: e.tensor_copy(
                        out=AF32[:, tc * 512: tc * 512 + 512], in_=psum[bk][:, 0:512]), inc="vv"))
                vvq = p.val("vv")
                for tc in range(4):
                    p.wait("sync", "vv", p.val("vv"))
                    cs1 = p.dma("sync", Dbf[0:64, 1024:1536],
                                bass.AP(SCR["l3cs"].tensor, SCR["l3cs"].offset + tc * 512,
                                        [[2048, 32], [0, 2], [1, 512]]))
                    cs2 = p.dma("sync", Dbf[64:128, 1024:1536],
                                bass.AP(SCR["l3cs"].tensor, SCR["l3cs"].offset + tc * 512,
                                        [[2048, 32], [0, 2], [1, 512]]), cont=True)
                    sn1 = p.dma("sync", Dbf[0:64, 1536:2048],
                                bass.AP(SCR["l3sn"].tensor, SCR["l3sn"].offset + tc * 512,
                                        [[2048, 32], [65536, 2], [1, 512]]), cont=True)
                    sn2 = p.dma("sync", Dbf[64:128, 1536:2048],
                                bass.AP(SCR["l3sn"].tensor, SCR["l3sn"].offset + tc * 512,
                                        [[2048, 32], [65536, 2], [1, 512]]), cont=True)
                    p.wait("tensor", "vv", vvq)
                    p.wait("tensor", sn2)
                    tvp = p.op("tensor", lambda e, tc=tc: e.matmul(
                        psum[2][:, 0:512], sstat[:, 0:128],
                        AF32[:, tc * 512: tc * 512 + 512], start=True, stop=True), inc="tv")
                    p.wait("vector", sn2)
                    p.wait("vector", "tv", tvp)
                    p.op("vector", lambda e, tc=tc: e.tensor_tensor(
                        out=Dbf[:, 0:512], in0=AF32[:, tc * 512: tc * 512 + 512],
                        in1=Dbf[:, 1024:1536], op=ALU.mult), inc="vv")
                    p.op("vector", lambda e: e.tensor_tensor(
                        out=Dbf[:, 512:1024], in0=psum[2][:, 0:512],
                        in1=Dbf[:, 1536:2048], op=ALU.mult), inc="vv")
                    p.drain("vector")
                    vvr = p.op("vector", lambda e, oc=oc, tc=tc: e.tensor_tensor(
                        out=dstf(oc)[:, tc * 512: tc * 512 + 512], in0=Dbf[:, 0:512],
                        in1=Dbf[:, 512:1024], op=ALU.add), inc="vv")
                    p.drain("vector")
                    p.wait("tensor", "vv", vvr)
        qTb3 = lambda oc: BFA[:, 16384 + oc * S: 16384 + (oc + 1) * S]
        kTb3 = lambda oc: BFB[:, oc * S:(oc + 1) * S]
        if KDUMP == "l3h":
            dump_bf(hb)
        if KDUMP == "l3tab":
            p.join()
            dv = p.dma("sync", dbg[0:32, :], SCR["l3cs"])
            dv = p.dma("sync", dbg[32:64, :], SCR["l3sn"][1], cont=True)
            dv = p.dma("sync", dbg[64:96, :], SCR["l3sn"][0], cont=True)
            p.wait("sync", dv)
        proj_rope("l3_wq", 0, qTb3)
        if KDUMP == "l3q":
            dump_bf(qTb3)
        proj_rope("l3_wkv", 0, kTb3)
        if KDUMP == "l3k":
            dump_bf(kTb3)
        for dblk in range(2):
            p.wait("gpsimd", "tv", p.val("tv"))
            wv = None
            for ic in range(8):
                wv = p.dma("gpsimd", W0[:, ic * 512:(ic + 1) * 512],
                           I["l3_wkv"][ic * 128:(ic + 1) * 128,
                                       1024 + dblk * 512: 1024 + (dblk + 1) * 512],
                           cont=(ic > 0))
            p.wait("tensor", wv)
            for tt in range(16):
                bk = tt % 2
                bank_wait(bk)
                for ic in range(8):
                    p.op("tensor", lambda e, ic=ic, tt=tt, bk=bk: e.matmul(
                        psum[bk][:, 0:512], hb(ic)[:, tt * 128:(tt + 1) * 128],
                        W0[:, ic * 512:(ic + 1) * 512],
                        start=(ic == 0), stop=(ic == 7)), inc="tv")
                tvv = p.val("tv")
                p.wait("vector", "tv", tvv)
                vv = p.op("vector", lambda e, tt=tt, bk=bk: e.tensor_copy(
                    out=Db[:, bk * 512: bk * 512 + 512], in_=psum[bk][:, 0:512]), inc="vv")
                p.wait("sync", "vv", vv)
                dv = p.dma("sync", SCR["vtok"][tt][:, dblk * 512:(dblk + 1) * 512],
                           Db[:, bk * 512: bk * 512 + 512])
                p.wait("vector", dv)
                bank_set(bk, vv)
        p.join()
        return hb
    def l3_attn2(hb):
        qTb3 = lambda oc: BFA[:, 16384 + oc * S: 16384 + (oc + 1) * S]
        kTb3 = lambda oc: BFB[:, oc * S:(oc + 1) * S]
        p.op("gpsimd", lambda e: e.memset(colf[:, 56:57], float(np.float32(1.0 / np.sqrt(2.0)))),
             inc="gv")
        p.wait("scalar", "gv", p.val("gv"))
        for dblk in range(2):
            for tt in range(16):
                vl = p.dma("sync", Db[:, 0:512] if tt % 2 == 0 else Db[:, 512:1024],
                           SCR["vtok"][tt][:, dblk * 512:(dblk + 1) * 512])
                p.wait("tensor", vl)
                p.op("tensor", lambda e, tt=tt: e.matmul(
                    psum[3][0:1, 0:512], onesb[:, 0:1],
                    Db[:, 0:512] if tt % 2 == 0 else Db[:, 512:1024],
                    start=(tt == 0), stop=(tt == 15)), inc="tv")
                p.wait("sync", "tv", p.val("tv"))
            tvv = p.val("tv")
            p.wait("vector", "tv", tvv)
            vv = p.op("vector", lambda e, dblk=dblk: e.tensor_scalar(
                out=rows[0:1, dblk * 512: dblk * 512 + 512], in0=psum[3][0:1, 0:512],
                scalar1=0.5, scalar2=None, op0=ALU.mult), inc="vv")
            p.wait("tensor", "vv", vv)
        vv = p.val("vv")
        p.wait("sync", "vv", vv)
        vs = p.dma("sync", SCR["rows8"][4:5, 0:1024], rows[0:1, 0:1024])
        p.wait("sync", vs)
        vh = None
        for c in range(8):
            vh = p.dma("sync", colf[:, 48 + c: 48 + c + 1],
                       SCR["rows8"][4:5, c * 128:(c + 1) * 128], cont=(c > 0))
        p.wait("vector", vh)
        p.wait("scalar", vh)
        p.wait("gpsimd", "vv", p.val("vv"))
        VG = {}
        for h in range(16):
            cc, hh = h // 2, h % 2
            slv = h % 2
            p.wait("gpsimd", "tv", VG.get(slv, 0))
            vl = None
            for tt in range(16):
                vl = p.dma("gpsimd", W1[:, slv * 1024 + tt * 64: slv * 1024 + (tt + 1) * 64],
                           SCR["vtok"][tt][:, h * 64:(h + 1) * 64], cont=(tt > 0))
            for qc in range(4):
                p.wait("scalar", "tv", p.val("tv"))
                for jc in range(16):
                    bk = jc % 2
                    tvv = p.op("tensor", lambda e, cc=cc, hh=hh, qc=qc, jc=jc, bk=bk: e.matmul(
                        psum[bk][:, 0:512],
                        kTb3(cc)[hh * 64: hh * 64 + 64, jc * 128:(jc + 1) * 128],
                        qTb3(cc)[hh * 64: hh * 64 + 64, qc * 512:(qc + 1) * 512],
                        start=True, stop=True), inc="tv")
                    p.wait("scalar", "tv", tvv)
                    sv = p.op("scalar", lambda e, jc=jc, bk=bk: e.activation(
                        out=W0[:, jc * 512:(jc + 1) * 512], in_=psum[bk][:, 0:512],
                        func=AF.Square, scale=float(np.float32(0.125 / np.sqrt(2.0))),
                        bias=colf[:, 56:57]), inc="sv")
                    p.wait("tensor", "sv", sv)
                p.wait("tensor", vl)
                for jc in range(16):
                    p.op("tensor", lambda e, jc=jc: e.matmul(
                        psum[3][0:1, 0:512], onesb[:, 0:1], W0[:, jc * 512:(jc + 1) * 512],
                        start=(jc == 0), stop=(jc == 15)), inc="tv")
                    p.op("tensor", lambda e, h=h, jc=jc, hh=hh, slv=slv: e.matmul(
                        psum[2][hh * 64: hh * 64 + 64, 0:512],
                        W1[:, slv * 1024 + jc * 64: slv * 1024 + (jc + 1) * 64],
                        W0[:, jc * 512:(jc + 1) * 512],
                        start=(jc == 0), stop=(jc == 15)), inc="tv")
                tvv = p.val("tv")
                VG[slv] = tvv
                if KDUMP == "l3tp" and h == 0 and qc == 0:
                    p.join()
                    for cpart in range(4):
                        p.op("vector", lambda e, cpart=cpart: e.tensor_copy(
                            out=AF32[:, 0:2048], in_=W0[:, cpart * 2048:(cpart + 1) * 2048]),
                            inc="vv")
                        vvd = p.val("vv")
                        p.wait("sync", "vv", vvd)
                        dd = p.dma("sync", dbg[cpart * 128:(cpart + 1) * 128, :], AF32[:, 0:2048])
                        p.wait("vector", dd)
                    p.join()
                p.wait("vector", "tv", tvv)
                p.op("vector", lambda e: e.tensor_scalar(
                    out=rows[0:1, 0:512], in0=psum[3][0:1, 0:512],
                    scalar1=1024.0, scalar2=None, op0=ALU.add), inc="vv")
                p.drain("vector")
                p.op("vector", lambda e: e.reciprocal(out=rows[0:1, 0:512],
                                                      in_=rows[0:1, 0:512]), inc="vv")
                p.drain("vector")
                vvr = p.op("vector", lambda e, h=h, hh=hh, cc=cc, qc=qc: e.tensor_scalar(
                    out=qTb3(cc)[hh * 64: hh * 64 + 64, qc * 512:(qc + 1) * 512],
                    in0=psum[2][hh * 64: hh * 64 + 64, 0:512],
                    scalar1=colf[hh * 64: hh * 64 + 64, 48 + cc: 48 + cc + 1], scalar2=None,
                    op0=ALU.add), inc="vv")
                p.wait("sync", "vv", vvr)
                dn = p.dma("sync", SCR["rows16"][h:h + 1, qc * 512:(qc + 1) * 512],
                           rows[0:1, 0:512])
                p.wait("vector", dn)
                p.wait("tensor", "vv", vvr)
        p.join()
        if KDUMP == "l3d":
            dv = p.dma("sync", dbg[0:16, :], SCR["rows16"])
            p.wait("sync", dv)
        for c in range(8):
            p.wait("sync", "vv", p.val("vv"))
            rb = p.dma("sync", Dbf[:, 0:2048],
                       SCR["rows16"][2 * c: 2 * c + 2, :].unsqueeze(1).broadcast_to([2, 64, S]))
            p.wait("vector", rb)
            p.op("vector", lambda e, c=c: e.tensor_tensor(
                out=BFA[:, c * S:(c + 1) * S], in0=BFA[:, 16384 + c * S: 16384 + (c + 1) * S],
                in1=Dbf[:, 0:2048], op=ALU.mult), inc="vv")
            p.drain("vector")
        p.join()
        if KDUMP == "l3o":
            dump_bf(lambda c: BFA[:, c * S:(c + 1) * S])
        wv = load_w_bf16(W0, "l3_wo", 1024, 8)
        p.wait("tensor", wv)
        for oc in range(8):
            for tc in range(4):
                bk = (oc * 4 + tc) % 2
                bank_wait(bk)
                for ic in range(8):
                    p.op("tensor", lambda e, oc=oc, tc=tc, ic=ic, bk=bk: e.matmul(
                        psum[bk][:, 0:512],
                        W0[:, ic * 1024 + oc * 128: ic * 1024 + oc * 128 + 128],
                        BFA[:, ic * S + tc * 512: ic * S + tc * 512 + 512],
                        start=(ic == 0), stop=(ic == 7)), inc="tv")
                tvv = p.val("tv")
                p.wait("vector", "tv", tvv)
                bank_set(bk, p.op("vector", lambda e, oc=oc, tc=tc, bk=bk: e.tensor_tensor(
                    out=xr(oc)[:, tc * 512: tc * 512 + 512], in0=psum[bk][:, 0:512],
                    in1=xr(oc)[:, tc * 512: tc * 512 + 512], op=ALU.add), inc="vv"))
        ln_epilogue()

    def l3_attn_all():
        hb = l3_attn()
        l3_attn2(hb)

    def l3_ffn():
        p.join()
        cast_xb()
        gw = None
        for ic in range(8):
            gw = p.dma("sync", sstat[:, ic * 8:(ic + 1) * 8],
                       I["l3f_gw"][ic * 128:(ic + 1) * 128, :], cont=(ic > 0))
        p.wait("tensor", gw)
        for tc in range(4):
            for ic in range(8):
                p.op("tensor", lambda e, tc=tc, ic=ic: e.matmul(
                    psum[2][0:8, 0:512], sstat[:, ic * 8:(ic + 1) * 8],
                    xr(ic)[:, tc * 512: tc * 512 + 512],
                    start=(ic == 0), stop=(ic == 7)), inc="tv")
            tvv = p.val("tv")
            p.wait("vector", "tv", tvv)
            vv = p.op("vector", lambda e, tc=tc: e.tensor_copy(
                out=rows[0:8, tc * 512: tc * 512 + 512], in_=psum[2][0:8, 0:512]), inc="vv")
            p.wait("tensor", "vv", vv)
        for tt in range(16):
            bk = tt % 2
            p.wait("tensor", "vv", BG.get(("l3f", bk), 0))
            tvv = p.op("tensor", lambda e, tt=tt, bk=bk: e.matmul(
                psum[bk][:, 0:8], rows[0:8, tt * 128:(tt + 1) * 128],
                id32[0:8, 0:8], start=True, stop=True), inc="tv")
            p.wait("vector", "tv", tvv)
            vv = p.op("vector", lambda e, tt=tt, bk=bk: e.tensor_copy(
                out=AF32[:, tt * 8:(tt + 1) * 8], in_=psum[bk][:, 0:8]), inc="vv")
            BG[("l3f", bk)] = vv
        FMAX = 3.0e38
        for tt in range(16):
            lt = AF32[:, tt * 8:(tt + 1) * 8]
            cf = AF32[:, 128 + tt * 8: 128 + (tt + 1) * 8]
            p.op("vector", lambda e, lt=lt: e.tensor_reduce(
                out=colf[:, 40:41], in_=lt, axis=AX.X, op=ALU.max), inc="vv")
            p.drain("vector")
            p.op("vector", lambda e, lt=lt: e.tensor_scalar(
                out=AF32[:, 256:264], in0=lt, scalar1=colf[:, 40:41], scalar2=None,
                op0=ALU.is_equal), inc="vv")
            p.drain("vector")
            p.op("vector", lambda e, lt=lt: e.scalar_tensor_tensor(
                out=AF32[:, 264:272], in0=AF32[:, 256:264], scalar=-FMAX,
                in1=lt, op0=ALU.mult, op1=ALU.add), inc="vv")
            p.drain("vector")
            p.op("vector", lambda e: e.tensor_reduce(
                out=colf[:, 41:42], in_=AF32[:, 264:272], axis=AX.X, op=ALU.max), inc="vv")
            p.drain("vector")
            p.op("vector", lambda e, lt=lt: e.tensor_scalar(
                out=AF32[:, 264:272], in0=lt, scalar1=colf[:, 41:42], scalar2=None,
                op0=ALU.is_equal), inc="vv")
            p.drain("vector")
            p.op("vector", lambda e: e.tensor_tensor(
                out=colf[:, 42:43], in0=colf[:, 40:41], in1=colf[:, 41:42],
                op=ALU.subtract), inc="vv")
            vv = p.val("vv")
            p.wait("scalar", "vv", vv)
            sv = p.op("scalar", lambda e: e.activation(out=colf[:, 42:43], in_=colf[:, 42:43],
                                                       func=AF.Sigmoid), inc="sv")
            p.wait("vector", "sv", sv)
            p.op("vector", lambda e: e.tensor_scalar(
                out=colf[:, 43:44], in0=colf[:, 42:43], scalar1=-1.0, scalar2=1.0,
                op0=ALU.mult, op1=ALU.add), inc="vv")
            p.drain("vector")
            p.op("vector", lambda e: e.tensor_scalar(
                out=AF32[:, 256:264], in0=AF32[:, 256:264], scalar1=colf[:, 42:43],
                scalar2=None, op0=ALU.mult), inc="vv")
            p.op("vector", lambda e: e.tensor_scalar(
                out=AF32[:, 264:272], in0=AF32[:, 264:272], scalar1=colf[:, 43:44],
                scalar2=None, op0=ALU.mult), inc="vv")
            p.drain("vector")
            p.op("vector", lambda e, cf=cf: e.tensor_tensor(
                out=cf, in0=AF32[:, 256:264], in1=AF32[:, 264:272], op=ALU.add), inc="vv")
            p.drain("vector")
        for tt in range(16):
            bk = tt % 2
            p.wait("tensor", "vv", p.val("vv") if tt == 0 else BG.get(("l3fc", bk), 0))
            tvv = p.op("tensor", lambda e, tt=tt, bk=bk: e.matmul(
                psum[bk][0:8, 0:128], AF32[:, 128 + tt * 8: 128 + (tt + 1) * 8],
                id32[:, 0:128], start=True, stop=True), inc="tv")
            p.wait("vector", "tv", tvv)
            vv = p.op("vector", lambda e, tt=tt, bk=bk: e.tensor_copy(
                out=rows[0:8, tt * 128:(tt + 1) * 128], in_=psum[bk][0:8, 0:128]), inc="vv")
            BG[("l3fc", bk)] = vv
        vv = p.val("vv")
        p.wait("sync", "vv", vv)
        cw = p.dma("sync", SCR["rows8"], rows[0:8, 0:2048])
        p.wait("sync", cw)
        for ee in range(8):
            p.wait("gpsimd", "tv", p.val("tv"))
            wv = None
            for ic in range(8):
                wv = p.dma("gpsimd", W0[:, ic * 256:(ic + 1) * 256],
                           I["l3f_wu"][ic * 128:(ic + 1) * 128, ee * 256:(ee + 1) * 256],
                           cont=(ic > 0))
                wv = p.dma("gpsimd", W0[:, 2048 + ic * 256: 2048 + (ic + 1) * 256],
                           I["l3f_wv"][ic * 128:(ic + 1) * 128, ee * 256:(ee + 1) * 256],
                           cont=True)
            for ic2 in range(2):
                wv = p.dma("gpsimd", W0[:, 4096 + ic2 * 1024: 4096 + (ic2 + 1) * 1024],
                           I["l3f_wo"][ee][ic2 * 128:(ic2 + 1) * 128, :], cont=True)
            p.wait("sync", "vv", p.val("vv"))
            cr = p.dma("sync", Dbf[:, 0:2048],
                       SCR["rows8"][ee:ee + 1, :].broadcast_to([128, S]))
            p.wait("tensor", wv)
            p.wait("vector", cr)
            for oc2 in range(2):
                for tc in range(4):
                    for ic in range(8):
                        p.op("tensor", lambda e, oc2=oc2, tc=tc, ic=ic: e.matmul(
                            psum[0][:, 0:512],
                            W0[:, ic * 256 + oc2 * 128: ic * 256 + oc2 * 128 + 128],
                            xb(ic)[:, tc * 512: tc * 512 + 512],
                            start=(ic == 0), stop=(ic == 7)), inc="tv")
                        p.op("tensor", lambda e, oc2=oc2, tc=tc, ic=ic: e.matmul(
                            psum[1][:, 0:512],
                            W0[:, 2048 + ic * 256 + oc2 * 128: 2048 + ic * 256 + oc2 * 128 + 128],
                            xb(ic)[:, tc * 512: tc * 512 + 512],
                            start=(ic == 0), stop=(ic == 7)), inc="tv")
                    tvv = p.val("tv")
                    p.wait("scalar", "tv", tvv)
                    sv = p.op("scalar", lambda e: e.activation(
                        out=AF32[:, 512:1024], in_=psum[0][:, 0:512],
                        func=AF.Sigmoid), inc="sv")
                    p.wait("vector", "sv", sv)
                    p.op("vector", lambda e: e.tensor_tensor(
                        out=AF32[:, 1024:1536], in0=psum[0][:, 0:512],
                        in1=AF32[:, 512:1024], op=ALU.mult), inc="vv")
                    p.drain("vector")
                    p.op("vector", lambda e: e.tensor_tensor(
                        out=AF32[:, 1024:1536], in0=AF32[:, 1024:1536],
                        in1=psum[1][:, 0:512], op=ALU.mult), inc="vv")
                    p.drain("vector")
                    vv = p.op("vector", lambda e, oc2=oc2, tc=tc: e.tensor_tensor(
                        out=BFB[:, oc2 * S + tc * 512: oc2 * S + tc * 512 + 512],
                        in0=AF32[:, 1024:1536], in1=Dbf[:, tc * 512: tc * 512 + 512],
                        op=ALU.mult), inc="vv")
                    p.drain("vector")
                    p.wait("tensor", "vv", vv)
                    p.wait("scalar", "vv", vv)
            for oc in range(8):
                for tc in range(4):
                    bk = 2 + (oc * 4 + tc) % 2
                    bank_wait(bk)
                    for ic2 in range(2):
                        p.op("tensor", lambda e, oc=oc, tc=tc, ic2=ic2, bk=bk: e.matmul(
                            psum[bk][:, 0:512],
                            W0[:, 4096 + ic2 * 1024 + oc * 128: 4096 + ic2 * 1024 + oc * 128 + 128],
                            BFB[:, ic2 * S + tc * 512: ic2 * S + tc * 512 + 512],
                            start=(ic2 == 0), stop=(ic2 == 1)), inc="tv")
                    tvv = p.val("tv")
                    p.wait("vector", "tv", tvv)
                    bank_set(bk, p.op("vector", lambda e, oc=oc, tc=tc, bk=bk: e.tensor_tensor(
                        out=xr(oc)[:, tc * 512: tc * 512 + 512], in0=psum[bk][:, 0:512],
                        in1=xr(oc)[:, tc * 512: tc * 512 + 512], op=ALU.add), inc="vv"))
            p.join()
        ln_epilogue()

    def store_out():
        p.join()
        for tc in range(16):
            for dc in range(8):
                bk = dc % 4
                bank_wait(bk)
                tv = p.op("tensor", lambda e, tc=tc, dc=dc, bk=bk: e.transpose(
                    psum[bk][:, 0:128], xr(dc)[:, tc * 128: tc * 128 + 128], id32[:]), inc="tv")
                p.wait("vector", "tv", tv)
                bank_set(bk, p.op("vector", lambda e, tc=tc, dc=dc, bk=bk: e.tensor_copy(
                    out=AF32[:, dc * 128: dc * 128 + 128], in_=psum[bk][:, 0:128]), inc="vv"))
            vv = p.val("vv")
            p.wait("sync", "vv", vv)
            sv = p.dma("sync", out_x[tc * 128:(tc + 1) * 128, :], AF32[:, 0:1024])
            p.wait("vector", sv)
        p.dma_barrier("sync", "sync")
        p.dma_barrier("gpsimd", "gpsimd")

    blocks = [l0_attn, l0_ffn, l1_attn, l1_ffn, l2_attn, l2_ffn, l3_attn_all, l3_ffn]
    for i, blk in enumerate(blocks[:NBLK]):
        p.scope(f"B{i}_{blk.__name__}")
        blk()
    p.scope("store")
    store_out()
    p.run()
    ctx.close()
    return nc


def kernel(**inputs):
    consts = host_constants()
    L = host_layout(inputs)
    nc = build(consts)
    x = np.ascontiguousarray(inputs["L0_attn_x"], dtype=np.float32)
    in_maps = []
    for b in range(8):
        m = {"x_in": np.ascontiguousarray(x[b])}
        for nm in WSHAPES:
            m[nm] = L[nm]
        for nm, arr in consts.items():
            m["c_" + nm] = arr
        in_maps.append(m)
    res = run_bass_kernel_spmd(nc, in_maps, core_ids=list(range(8)))
    out = np.stack([np.asarray(res.results[b]["out"]) for b in range(8)], 0)
    return out.astype(np.float32)
